# version 05
# baseline (speedup 1.0000x reference)
import contextlib
import numpy as np
import ml_dtypes

import concourse.bass as bass
import concourse.mybir as mybir
from concourse.alu_op_type import AluOpType
from concourse.bass_utils import run_bass_kernel_spmd

F32 = mybir.dt.float32
BF16 = mybir.dt.bfloat16
U32 = mybir.dt.uint32
AF = mybir.ActivationFunctionType
AX = mybir.AxisListType
OP = AluOpType

BF = ml_dtypes.bfloat16

B, S, D = 4, 4096, 1024
T = 2048
NCHUNK = D // 128
RG_PAIRS = [[0, 1], [2, 3], [4, 5], [6, 7]]


ENGINES = ("pe", "act", "dve", "pool", "sp")


class Buf:
    __slots__ = ("t", "w", "readers", "name")

    def __init__(self, t, name=""):
        self.t = t
        self.w = None
        self.readers = []
        self.name = name

    def __getitem__(self, idx):
        return self.t[idx]

    @property
    def ap(self):
        return self.t[:]


class Prog:
    def __init__(self, nc, stack: contextlib.ExitStack):
        self.nc = nc
        self.stack = stack
        self.cmds = {e: [] for e in ENGINES}
        self.sems = {}
        self.sem_count = {}
        self.watermark = {e: {} for e in ENGINES}
        for e in ("pe", "act", "dve", "pool"):
            self._mk_sem(e)

    def _mk_sem(self, name):
        if name not in self.sems:
            h = self.stack.enter_context(self.nc.semaphore(name))
            self.sems[name] = h
            self.sem_count[name] = 0
        return self.sems[name]

    def dma_sem(self, name):
        return self._mk_sem(name)

    def fconst(self, val):
        if not hasattr(self, "_fc"):
            self._fc = sbuf(self, self.stack, "fcsts", [128, 16], F32)
            self._fc_map = {}
        v = float(val)
        if v not in self._fc_map:
            col = len(self._fc_map)
            assert col < 16
            self._fc_map[v] = col
            self.op("dve", lambda e, col=col, v=v: e.memset(
                self._fc[:, col:col + 1], v), writes=[self._fc])
        col = self._fc_map[v]
        return self._fc[:, col:col + 1], self._fc

    def fconstn(self, val, npart):
        ap, b = self.fconst(val)
        return self._fc[0:npart, self._fc_map[float(val)]:self._fc_map[float(val)] + 1], b

    def _engine_obj(self, e):
        return {"pe": self.nc.tensor, "act": self.nc.scalar, "dve": self.nc.vector,
                "pool": self.nc.gpsimd, "sp": self.nc.sync}[e]

    def _add_wait(self, eng, tok):
        if tok is None:
            return
        sem, val = tok
        wm = self.watermark[eng].get(sem, 0)
        if val <= wm:
            return
        self.watermark[eng][sem] = val
        h = self.sems[sem]
        self.cmds[eng].append(("w", h, val))

    def _deps(self, eng, reads, writes, acc):
        toks = []
        for b in reads:
            if b.w is not None:
                toks.append(b.w)
        for b in writes:
            if not acc and b.w is not None:
                toks.append(b.w)
            toks.extend(b.readers)
        out = []
        for sem, val in toks:
            if eng == "pe" and sem == "pe":
                continue
            out.append((sem, val))
        return out

    def op(self, eng, fn, reads=(), writes=(), acc=False, inc_sem=None, inc=1,
           deps=()):
        sem = inc_sem or eng
        self._mk_sem(sem)
        for tok in deps:
            self._add_wait(eng, tok)
        for tok in self._deps(eng, reads, writes, acc):
            self._add_wait(eng, tok)
        self.sem_count[sem] += inc
        tok = (sem, self.sem_count[sem])
        h = self.sems[sem]
        self.cmds[eng].append(("i", fn, h, inc))
        for b in writes:
            b.w = tok
            b.readers = []
        for b in reads:
            b.readers.append(tok)
        return tok

    NROT = 3
    NPOOL = 20

    def dma(self, queue, out_buf, out_ap, in_buf, in_ap, sem_name, **kw):
        if not hasattr(self, "_dma_uses"):
            self._dma_uses = {}
            self._pool_map = {}
        slot = self._pool_map.setdefault(sem_name, len(self._pool_map) % self.NPOOL)
        k = self._dma_uses.get(sem_name, 0)
        self._dma_uses[sem_name] = k + 1
        sub = f"q{slot}.{k % self.NROT}"
        self._mk_sem(sub)
        prior = self.sem_count[sub]
        if prior:
            self._add_wait(queue, (sub, prior))
        reads = [in_buf] if in_buf is not None else []
        writes = [out_buf] if out_buf is not None else []
        return self.op(queue, lambda e: e.dma_start(out=out_ap, in_=in_ap, **kw),
                       reads=reads, writes=writes, inc_sem=sub, inc=16)

    def wait(self, eng, tok):
        self._add_wait(eng, tok)

    def wait_all(self, eng, sem):
        names = [sem]
        if hasattr(self, "_pool_map") and sem in self._pool_map:
            slot = self._pool_map[sem]
            names = [f"q{slot}.{r}" for r in range(self.NROT)]
        for s in names:
            if self.sem_count.get(s):
                self._add_wait(eng, (s, self.sem_count[s]))

    def barrier(self, engines=None):
        engines = engines or ENGINES
        sems = list(self.sem_count.keys())
        for e in engines:
            for s in sems:
                if self.sem_count[s]:
                    self._add_wait(e, (s, self.sem_count[s]))

    def play(self, block):
        for e in ENGINES:
            cmds = self.cmds[e]
            if not cmds:
                continue

            def runner(eng_obj, cmds=cmds):
                for c in cmds:
                    if c[0] == "w":
                        eng_obj.wait_ge(c[1], c[2])
                    else:
                        inst = c[1](eng_obj)
                        if inst is not None:
                            inst.then_inc(c[2], c[3])
                        else:
                            eng_obj.nop().then_inc(c[2], c[3])

            getattr(block, {"pe": "tensor", "act": "scalar", "dve": "vector",
                            "pool": "gpsimd", "sp": "sync"}[e])(runner)



_UNIQ = [0]


def _uq(name):
    _UNIQ[0] += 1
    return f"{name}_u{_UNIQ[0]}"


def sbuf(p, stack, name, shape, dtype):
    t = stack.enter_context(p.nc.sbuf_tensor(_uq(name), shape, dtype))
    return Buf(t, name)


def psum(p, stack, name, shape, dtype=F32):
    t = stack.enter_context(p.nc.psum_tensor(_uq(name), shape, dtype))
    return Buf(t, name)


def dram(p, name, shape, dtype, kind=None):
    if kind:
        t = p.nc.dram_tensor(name, shape, dtype, kind=kind)
    else:
        t = p.nc.dram_tensor(name, shape, dtype)
    return Buf(t, name)


def ceil_div(a, b):
    return (a + b - 1) // b


def chunks_of(n, size):
    out = []
    i = 0
    while i < n:
        out.append((i, min(size, n - i)))
        i += size
    return out



def fm(x):
    return np.ascontiguousarray(np.asarray(x).T)


def as_np(x):
    return np.ascontiguousarray(np.asarray(x))


def _mkio(nc, pre, io):
    def DI(name, shape, dt):
        if name in io:
            return io[name]
        return Buf(nc.dram_tensor(pre + name, shape, dt, kind="ExternalInput"))

    def DO(name, shape, dt):
        if name in io:
            return io[name]
        return Buf(nc.dram_tensor(pre + name, shape, dt, kind="ExternalOutput"))

    def DN(name, shape, dt):
        return Buf(nc.dram_tensor(_uq(pre + name), shape, dt))
    return DI, DO, DN


class _Ctx:
    def __init__(self, nc, p):
        self.nc = nc
        self.p = p



def emit_ln_stream(p, stack, tag, x_dram, y_dram, n_tok, consts, eps=1e-5,
                   out_dt=BF16, col_off=None, gamma=None, beta=None):
    nc = p.nc
    if col_off is None:
        col_off = [0] * NCHUNK
    strips = chunks_of(n_tok, 512)
    ns = len(strips)
    assert ns <= 8
    with contextlib.ExitStack() as st:
        xb = [sbuf(p, st, f"{tag}_x{i}", [128, 8 * 512], F32) for i in range(2)]
        sq = [sbuf(p, st, f"{tag}_sq{i}", [128, 512], F32) for i in range(2)]
        pA = psum(p, st, f"{tag}_pA", [8, 512])
        pB = psum(p, st, f"{tag}_pB", [8, 512])
        mu = sbuf(p, st, f"{tag}_mu", [8, 512], F32)
        ex2 = sbuf(p, st, f"{tag}_ex2", [8, 512], F32)
        var = sbuf(p, st, f"{tag}_var", [8, 512], F32)
        rstd = sbuf(p, st, f"{tag}_rstd", [8, 512], F32)
        nmr = sbuf(p, st, f"{tag}_nmr", [8, 512], F32)
        pR = psum(p, st, f"{tag}_pR", [128, 512])
        pM = psum(p, st, f"{tag}_pM", [128, 512])
        tmp = sbuf(p, st, f"{tag}_tmp", [128, 512], F32)
        yb = sbuf(p, st, f"{tag}_yb", [128, 512], out_dt)
        yb2 = sbuf(p, st, f"{tag}_yb2", [128, 512], out_dt)
        ssel = consts["stats_sel"]
        bsel = consts["bsel"]
        first = True
        for s, (t0, tl) in enumerate(strips):
            x = xb[s % 2]
            for c in range(NCHUNK):
                o = col_off[c]
                p.dma("sp", x, x[:, c * 512:c * 512 + tl],
                      None, x_dram[c * 128:(c + 1) * 128, t0 + o:t0 + o + tl],
                      f"{tag}_ld", allow_slow_non_contiguous=True)
            for c in range(NCHUNK):
                xc = x[:, c * 512:c * 512 + tl]
                p.op("pe", lambda e, xc=xc, s=s, f=first, tl=tl: e.matmul(
                    pA[0:8, 0:tl], ssel[:, 8 * s:8 * s + 8], xc, start=f, stop=False),
                    reads=[ssel, x], writes=[pA], acc=not first)
                p.op("act", lambda e, xc=xc, o=sq[c % 2], tl=tl: e.activation(
                    o[:, 0:tl], xc, AF.Square), reads=[x], writes=[sq[c % 2]])
                p.op("pe", lambda e, s=s, c=c, tl=tl, f=first: e.matmul(
                    pB[0:8, 0:tl], ssel[:, 8 * s:8 * s + 8], sq[c % 2][:, 0:tl],
                    start=f, stop=False),
                    reads=[ssel, sq[c % 2]], writes=[pB], acc=not first)
                first = False
        p.op("dve", lambda e: e.tensor_scalar(mu.ap, pA.ap, 1.0 / D, None, OP.mult),
             reads=[pA], writes=[mu])
        p.op("dve", lambda e: e.tensor_scalar(ex2.ap, pB.ap, 1.0 / D, None, OP.mult),
             reads=[pB], writes=[ex2])
        p.op("act", lambda e: e.activation(var.ap, mu.ap, AF.Square),
             reads=[mu], writes=[var])
        p.op("dve", lambda e: e.tensor_sub(var.ap, ex2.ap, var.ap),
             reads=[ex2, var], writes=[var])
        epsc, epsb = p.fconstn(eps, 8)
        p.op("act", lambda e: e.activation(rstd.ap, var.ap, AF.Ln, bias=epsc),
             reads=[var, epsb], writes=[rstd])
        p.op("act", lambda e: e.activation(rstd.ap, rstd.ap, AF.Exp, scale=-0.5),
             reads=[rstd], writes=[rstd])
        p.op("dve", lambda e: e.scalar_tensor_tensor(
            nmr.ap, mu.ap, -1.0, rstd.ap, OP.mult, OP.mult),
            reads=[mu, rstd], writes=[nmr])
        for s, (t0, tl) in enumerate(strips):
            x = xb[s % 2]
            for c in range(NCHUNK):
                o = col_off[c]
                p.dma("sp", x, x[:, c * 512:c * 512 + tl],
                      None, x_dram[c * 128:(c + 1) * 128, t0 + o:t0 + o + tl],
                      f"{tag}_ld", allow_slow_non_contiguous=True)
            p.op("pe", lambda e, s=s, tl=tl: e.matmul(
                pR[:, 0:tl], bsel[:, 128 * s:128 * s + 128], rstd[:, 0:tl],
                start=True, stop=True), reads=[bsel, rstd], writes=[pR])
            p.op("pe", lambda e, s=s, tl=tl: e.matmul(
                pM[:, 0:tl], bsel[:, 128 * s:128 * s + 128], nmr[:, 0:tl],
                start=True, stop=True), reads=[bsel, nmr], writes=[pM])
            for c in range(NCHUNK):
                xc = x[:, c * 512:c * 512 + tl]
                p.op("dve", lambda e, xc=xc, tl=tl: e.tensor_mul(
                    tmp[:, 0:tl], xc, pR[:, 0:tl]), reads=[x, pR], writes=[tmp])
                p.op("dve", lambda e, tl=tl: e.tensor_add(
                    yb[:, 0:tl], tmp[:, 0:tl], pM[:, 0:tl]),
                    reads=[tmp, pM], writes=[yb])
                out = yb
                if gamma is not None:
                    p.op("dve", lambda e, tl=tl, c=c: e.tensor_scalar(
                        yb2[:, 0:tl], yb[:, 0:tl], gamma[:, c:c + 1],
                        beta[:, c:c + 1], OP.mult, OP.add),
                        reads=[yb, gamma, beta], writes=[yb2])
                    out = yb2
                p.dma("sp", None, y_dram[c * 128:(c + 1) * 128, t0:t0 + tl],
                      out, out[:, 0:tl], f"{tag}_st",
                      allow_slow_non_contiguous=True)
        p.wait_all("sp", f"{tag}_st")
        p.barrier()




def build_s2a(ctx=None, pre="", io=None):
    standalone = ctx is None
    nc = bass.Bass("TRN2", target_bir_lowering=False) if standalone else ctx.nc
    DI, DO, DN = _mkio(nc, pre, io or {})
    x_full = DI("x_full", [D, S], F32)
    wq = DI("wq", [D, 512], F32)
    wk = DI("wk", [D, 512], F32)
    wv = DI("wv", [D, 512], F32)
    bq = DI("bq", [512, 1], F32)
    bk = DI("bk", [512, 1], F32)
    bv = DI("bv", [512, 1], F32)
    c_stats = DI("c_stats", [128, 64], F32)
    c_bsel = DI("c_bsel", [8, 1024], F32)
    c_hsel = DI("c_hsel", [128, 32], F32)
    c_dmask = DI("c_dmask", [128, 4 * 512], BF16)
    o_un = DO("o_un", [512, S], BF16)
    o_zr = DO("o_zr", [8, S], F32)
    lnx = DN("lnx", [D, S], BF16)

    with contextlib.ExitStack() as stack:
        p = ctx.p if not standalone else Prog(nc, stack)
        p.fconst(0.0)
        cs = sbuf(p, stack, "cs", [128, 64], F32)
        cb = sbuf(p, stack, "cb", [8, 1024], F32)
        ch = sbuf(p, stack, "ch", [128, 32], F32)
        cm = sbuf(p, stack, "cm", [128, 4 * 512], BF16)
        p.dma("sp", cs, cs.ap, None, c_stats[:, :], "cld")
        p.dma("sp", cb, cb.ap, None, c_bsel[:, :], "cld")
        p.dma("sp", ch, ch.ap, None, c_hsel[:, :], "cld")
        p.dma("sp", cm, cm.ap, None, c_dmask[:, :], "cld")
        consts = {"stats_sel": cs, "bsel": cb}
        p.barrier()
        emit_ln_stream(p, stack, "ln", x_full, lnx, S, consts)

        K = [sbuf(p, stack, f"K{c}", [128, S], BF16) for c in range(4)]
        Q = [sbuf(p, stack, f"Q{c}", [128, S], BF16) for c in range(4)]
        V = [sbuf(p, stack, f"V{t}", [128, 512], BF16) for t in range(32)]
        qn = sbuf(p, stack, "qn", [8, 8], F32)
        kn = sbuf(p, stack, "kn", [8, 8], F32)
        Mb = sbuf(p, stack, "Mb", [128, 64], F32)
        wqs = [sbuf(p, stack, f"wqs{i}", [128, 512], BF16) for i in range(NCHUNK)]
        wks = [sbuf(p, stack, f"wks{i}", [128, 512], BF16) for i in range(NCHUNK)]
        wvs = [sbuf(p, stack, f"wvs{i}", [128, 512], BF16) for i in range(NCHUNK)]
        bqs = sbuf(p, stack, "bqs", [128, 4], F32)
        bks = sbuf(p, stack, "bks", [128, 4], F32)
        bvr = sbuf(p, stack, "bvr", [1, 512], F32)
        onesc = sbuf(p, stack, "onesc", [128, 1], BF16)
        p.op("dve", lambda e: e.memset(onesc.ap, 1.0), writes=[onesc])
        for i in range(NCHUNK):
            p.dma("pool", wqs[i], wqs[i].ap, None, wq[i * 128:(i + 1) * 128, :], "wld")
            p.dma("pool", wks[i], wks[i].ap, None, wk[i * 128:(i + 1) * 128, :], "wld")
            p.dma("pool", wvs[i], wvs[i].ap, None, wv[i * 128:(i + 1) * 128, :], "wld")
        for oc in range(4):
            p.dma("sp", bqs, bqs[:, oc:oc + 1], None, bq[oc * 128:(oc + 1) * 128, :], "cld")
            p.dma("sp", bks, bks[:, oc:oc + 1], None, bk[oc * 128:(oc + 1) * 128, :], "cld")
        p.dma("sp", bvr, bvr.ap, None, bv[:, 0:1].rearrange("a b -> b a"), "cld")
        p.barrier()

        with contextlib.ExitStack() as st:
            lx = [sbuf(p, st, f"lx{i}", [128, 8 * 512], BF16) for i in range(2)]
            pP = [psum(p, st, f"pP{i}", [128, 512]) for i in range(2)]
            pN = psum(p, st, "pN", [8, 512])
            sqb = [sbuf(p, st, f"sqb{i}", [128, 512], F32) for i in range(2)]
            nmax = sbuf(p, st, "nmax", [8, 1], F32)
            pBv = psum(p, st, "pBv", [128, 512])
            onesr = sbuf(p, st, "onesr", [1, 128], F32)
            sBv = sbuf(p, st, "sBv", [128, 512], F32)
            p.op("dve", lambda e: e.memset(onesr.ap, 1.0), writes=[onesr])
            p.op("pe", lambda e: e.matmul(pBv.ap, onesr.ap, bvr.ap, start=True,
                                          stop=True), reads=[onesr, bvr], writes=[pBv])
            p.op("dve", lambda e: e.tensor_copy(sBv.ap, pBv.ap),
                 reads=[pBv], writes=[sBv])
            for tt in range(8):
                x = lx[tt % 2]
                for c in range(NCHUNK):
                    p.dma("sp", x, x[:, c * 512:(c + 1) * 512],
                          None, lnx[c * 128:(c + 1) * 128, tt * 512:(tt + 1) * 512],
                          "lxld")
                for (W, BIAS, OUT, NTAB) in ((wks, bks, K, kn), (wqs, bqs, Q, qn)):
                    for oc in range(4):
                        pp = pP[oc % 2]
                        for ic in range(NCHUNK):
                            p.op("pe", lambda e, W=W, ic=ic, oc=oc, x=x, pp=pp:
                                 e.matmul(pp.ap,
                                          W[ic][:, oc * 128:(oc + 1) * 128],
                                          x[:, ic * 512:(ic + 1) * 512],
                                          start=(ic == 0), stop=(ic == NCHUNK - 1)),
                                 reads=[W[ic], x], writes=[pp], acc=(ic > 0))
                        dst = OUT[oc]
                        p.op("act", lambda e, dst=dst, tt=tt, oc=oc, pp=pp, BIAS=BIAS:
                             e.activation(dst[:, tt * 512:(tt + 1) * 512], pp.ap,
                                          AF.Identity, bias=BIAS[:, oc:oc + 1]),
                             reads=[pp, BIAS], writes=[dst])
                        sq = sqb[oc % 2]
                        p.op("act", lambda e, dst=dst, tt=tt, sq=sq: e.activation(
                            sq.ap, dst[:, tt * 512:(tt + 1) * 512], AF.Square),
                            reads=[dst], writes=[sq])
                        p.op("pe", lambda e, oc=oc, sq=sq: e.matmul(
                            pN.ap, ch[:, 8 * oc:8 * oc + 8], sq.ap,
                            start=(oc == 0), stop=(oc == 3)),
                            reads=[ch, sq], writes=[pN], acc=(oc > 0))
                    p.op("dve", lambda e: e.reduce_max(nmax.ap, pN.ap, axis=AX.X),
                         reads=[pN], writes=[nmax])
                    p.op("dve", lambda e, NTAB=NTAB, tt=tt: e.tensor_copy(
                        NTAB[:, tt:tt + 1], nmax.ap), reads=[nmax], writes=[NTAB])
                for sub in range(4):
                    pv = pP[sub % 2]
                    for ic in range(NCHUNK):
                        p.op("pe", lambda e, ic=ic, x=x, sub=sub, pv=pv: e.matmul(
                            pv.ap,
                            x[:, ic * 512 + sub * 128:ic * 512 + sub * 128 + 128],
                            wvs[ic].ap, start=(ic == 0), stop=(ic == NCHUNK - 1)),
                            reads=[wvs[ic], x], writes=[pv], acc=(ic > 0))
                    vt = V[tt * 4 + sub]
                    p.op("dve", lambda e, vt=vt, pv=pv: e.tensor_add(
                        vt.ap, pv.ap, sBv.ap), reads=[pv, sBv], writes=[vt])
            p.barrier()

        negM = sbuf(p, stack, "negM", [8, 8], F32)
        for g in range(1, 8):
            p.op("dve", lambda e, g=g: e.tensor_max(
                kn[:, g:g + 1], kn[:, g - 1:g], kn[:, g:g + 1]),
                reads=[kn], writes=[kn])
        p.op("dve", lambda e: e.tensor_mul(negM.ap, qn.ap, kn.ap),
             reads=[qn, kn], writes=[negM])
        tinyc, tinyb = p.fconstn(1e-30, 8)
        p.op("act", lambda e: e.activation(negM.ap, negM.ap, AF.Ln, bias=tinyc),
             reads=[negM, tinyb], writes=[negM])
        p.op("act", lambda e: e.activation(negM.ap, negM.ap, AF.Exp, scale=0.5),
             reads=[negM], writes=[negM])
        p.op("dve", lambda e: e.tensor_scalar(negM.ap, negM.ap, -0.125, 55.0,
                                              OP.mult, OP.add),
             reads=[negM], writes=[negM])
        with contextlib.ExitStack() as st:
            pMb = psum(p, st, "pMb", [128, 64])
            for h in range(8):
                p.op("pe", lambda e, h=h: e.matmul(
                    pMb[:, 8 * h:8 * h + 8], cb[:, 128 * h:128 * h + 128],
                    negM.ap, start=True, stop=True),
                    reads=[cb, negM], writes=[pMb], acc=(h > 0))
            p.op("dve", lambda e: e.tensor_copy(Mb.ap, pMb.ap),
                 reads=[pMb], writes=[Mb])
        p.barrier()

        with contextlib.ExitStack() as st:
            pS = [psum(p, st, f"pS{i}", [128, 512]) for i in range(2)]
            pZ = psum(p, st, "pZ", [1, 512])
            pO = [psum(p, st, f"pO{i}", [64, 512]) for i in range(2)]
            PH = [sbuf(p, st, f"PH{i}", [128, 512], BF16) for i in range(3)]
            rzb = sbuf(p, st, "rzb", [1, 512], F32)
            osb = [sbuf(p, st, f"osb{i}", [64, 512], BF16) for i in range(2)]
            phi = 0
            for h in range(8):
                koff = 64 * (h % 2)
                kc_chunk = h // 2
                for g in range(8):
                    po = pO[g % 2]
                    nsub = 4 * (g + 1)
                    for s_i in range(nsub):
                        kcol = s_i * 128
                        ps = pS[s_i % 2]
                        p.op("pe", lambda e, kcol=kcol, g=g, ps=ps, kc_chunk=kc_chunk,
                             koff=koff: e.matmul(
                                 ps.ap,
                                 K[kc_chunk][koff:koff + 64, kcol:kcol + 128],
                                 Q[kc_chunk][koff:koff + 64, g * 512:(g + 1) * 512],
                                 start=True, stop=True),
                             reads=[K[kc_chunk], Q[kc_chunk]], writes=[ps])
                        ph = PH[phi % 3]
                        phi += 1
                        col = h * 8 + g
                        p.op("act", lambda e, ps=ps, ph=ph, col=col: e.activation(
                            ph.ap, ps.ap, AF.Exp, scale=0.125,
                            bias=Mb[:, col:col + 1]),
                            reads=[ps, Mb], writes=[ph])
                        if s_i // 4 == g:
                            sub = s_i % 4
                            p.op("dve", lambda e, ph=ph, sub=sub: e.tensor_mul(
                                ph.ap, ph.ap, cm[:, 512 * sub:512 * (sub + 1)]),
                                reads=[cm, ph], writes=[ph])
                        p.op("pe", lambda e, ph=ph, s_i=s_i: e.matmul(
                            pZ.ap, onesc.ap, ph.ap,
                            start=(s_i == 0), stop=(s_i == nsub - 1)),
                            reads=[onesc, ph], writes=[pZ], acc=(s_i > 0))
                        p.op("pe", lambda e, ph=ph, s_i=s_i, po=po, h=h: e.matmul(
                            po.ap, V[s_i][:, 64 * h:64 * h + 64], ph.ap,
                            start=(s_i == 0), stop=(s_i == nsub - 1)),
                            reads=[V[s_i], ph], writes=[po], acc=(s_i > 0))
                    p.op("dve", lambda e: e.reciprocal(rzb.ap, pZ.ap),
                         reads=[pZ], writes=[rzb])
                    p.dma("pool", None, o_zr[h:h + 1, g * 512:(g + 1) * 512],
                          rzb, rzb.ap, "zst")
                    ob = osb[g % 2]
                    p.op("dve", lambda e, po=po, ob=ob: e.tensor_copy(ob.ap, po.ap),
                         reads=[po], writes=[ob])
                    p.dma("pool", None, o_un[64 * h:64 * h + 64, g * 512:(g + 1) * 512],
                          ob, ob.ap, "ost")
            p.wait_all("pool", "zst")
            p.wait_all("pool", "ost")
            p.barrier()
        if standalone:
            blk = stack.enter_context(nc.Block())
            p.play(blk)
    return nc



def build_s2b(ctx=None, pre="", io=None):
    standalone = ctx is None
    nc = bass.Bass("TRN2", target_bir_lowering=False) if standalone else ctx.nc
    DI, DO, DN = _mkio(nc, pre, io or {})
    x_own = DI("x_own", [D, T], F32)
    o_sl = DI("o_sl", [D, T], BF16)
    zr = DI("zr", [16, T], F32)
    wo = DI("wo", [D, D], F32)
    bo = DI("bo", [D, 1], F32)
    c_zsel = DI("c_zsel", [16, 8 * 128], F32)
    x_new = DO("x_new", [D, T], F32)

    with contextlib.ExitStack() as stack:
        p = ctx.p if not standalone else Prog(nc, stack)
        if standalone:
            p.fconst(0.0)
        zs = sbuf(p, stack, "zs", [16, T], F32)
        czs = sbuf(p, stack, "czs", [16, 8 * 128], F32)
        wos = [sbuf(p, stack, f"wos{i}", [128, D], BF16) for i in range(NCHUNK)]
        bos = sbuf(p, stack, "bos", [128, NCHUNK], F32)
        xo = [sbuf(p, stack, f"xo{c}", [128, T], F32) for c in range(NCHUNK)]
        ob = [sbuf(p, stack, f"ob{c}", [128, T], BF16) for c in range(NCHUNK)]
        p.dma("sp", zs, zs.ap, None, zr[:, :], "cld")
        p.dma("sp", czs, czs.ap, None, c_zsel[:, :], "cld")
        for i in range(NCHUNK):
            p.dma("pool", wos[i], wos[i].ap, None, wo[i * 128:(i + 1) * 128, :], "wld")
            p.dma("sp", bos, bos[:, i:i + 1], None, bo[i * 128:(i + 1) * 128, :], "cld")
            p.dma("sp", xo[i], xo[i].ap, None, x_own[i * 128:(i + 1) * 128, :], "xld")
            p.dma("sp", ob[i], ob[i].ap, None, o_sl[i * 128:(i + 1) * 128, :], "old")
        p.barrier()
        with contextlib.ExitStack() as st:
            pR = [psum(p, st, f"pR{i}", [128, 512]) for i in range(2)]
            pX = [psum(p, st, f"pX{i}", [128, 512]) for i in range(2)]
            on = [sbuf(p, st, f"on{c}", [128, 512], BF16) for c in range(NCHUNK)]
            xn = sbuf(p, st, "xn", [128, 512], F32)
            for tt in range(4):
                t0 = tt * 512
                for ic in range(NCHUNK):
                    pr = pR[ic % 2]
                    p.op("pe", lambda e, ic=ic, t0=t0, pr=pr: e.matmul(
                        pr.ap, czs[:, 128 * ic:128 * (ic + 1)],
                        zs[:, t0:t0 + 512], start=True, stop=True),
                        reads=[czs, zs], writes=[pr])
                    p.op("dve", lambda e, ic=ic, t0=t0, pr=pr: e.tensor_mul(
                        on[ic].ap, ob[ic][:, t0:t0 + 512], pr.ap),
                        reads=[ob[ic], pr], writes=[on[ic]])
                for oc in range(NCHUNK):
                    px = pX[oc % 2]
                    for ic in range(NCHUNK):
                        p.op("pe", lambda e, ic=ic, oc=oc, px=px: e.matmul(
                            px.ap, wos[ic][:, oc * 128:(oc + 1) * 128],
                            on[ic].ap, start=(ic == 0), stop=(ic == NCHUNK - 1)),
                            reads=[wos[ic], on[ic]], writes=[px], acc=(ic > 0))
                    p.op("dve", lambda e, oc=oc, t0=t0, px=px: e.scalar_tensor_tensor(
                        xn.ap, px.ap, bos[:, oc:oc + 1], xo[oc][:, t0:t0 + 512],
                        OP.add, OP.add), reads=[px, bos, xo[oc]], writes=[xn])
                    p.dma("sp", None, x_new[oc * 128:(oc + 1) * 128, t0:t0 + 512],
                          xn, xn.ap, "xst")
            p.wait_all("sp", "xst")
            p.barrier()
        if standalone:
            blk = stack.enter_context(nc.Block())
            p.play(blk)
    return nc



def host_consts_s2():
    stats_sel = np.zeros((128, 64), np.float32)
    for s in range(8):
        stats_sel[:, 8 * s + s] = 1.0
    bsel = np.zeros((8, 1024), np.float32)
    for s in range(8):
        bsel[s, 128 * s:128 * (s + 1)] = 1.0
    hsel = np.zeros((128, 32), np.float32)
    for oc in range(4):
        for j in range(2):
            hsel[64 * j:64 * (j + 1), 8 * oc + 2 * oc + j] = 1.0
    dmask = np.zeros((128, 4 * 512), np.float32)
    for sub in range(4):
        k = np.arange(128)[:, None] + 128 * sub
        q = np.arange(512)[None, :]
        dmask[:, 512 * sub:512 * (sub + 1)] = (q >= k).astype(np.float32)
    zsel = np.zeros((16, 8 * 128), np.float32)
    for c in range(8):
        for pp in range(128):
            zsel[2 * c + pp // 64, 128 * c + pp] = 1.0
    return {
        "c_stats": stats_sel, "c_bsel": bsel, "c_hsel": hsel,
        "c_dmask": dmask.astype(BF), "c_zsel": zsel,
    }


def build_s4(ctx=None, pre="", io=None):
    standalone = ctx is None
    nc = bass.Bass("TRN2", target_bir_lowering=False) if standalone else ctx.nc
    DI, DO, DN = _mkio(nc, pre, io or {})
    _io = io if io is not None else {}
    prep_done = "tabb" in _io
    x_own = DI("x_own", [D, T], F32)
    wqT = DI("wqT", [D, 512], F32)
    keysx = DI("keysx", [512, 128], F32)
    wgT = None if prep_done else DI("wgT", [4096, D], F32)
    wuT = None if prep_done else DI("wuT", [4096, D], F32)
    wdT = None if prep_done else DI("wdT", [D, 4096], F32)
    tabc = None if prep_done else DI("tabc", [16384, 2048], F32)
    c_stats = DI("c_stats", [128, 64], F32)
    c_bsel = DI("c_bsel", [8, 1024], F32)
    c_ident = DI("c_ident", [128, 128], BF16)
    x_new = DO("x_new", [D, T], F32)
    lnx = DN("lnx", [D, T], BF16)
    tabb = _io.get("tabb") or DN("tabb", [16384, 2048], BF16)
    edram = DN("edram", [D, T], BF16)
    wgbd = _io.get("wgbd") or DN("wgbd", [4096, D], BF16)
    wubd = _io.get("wubd") or DN("wubd", [4096, D], BF16)
    wdbd = _io.get("wdbd") or DN("wdbd", [D, 4096], BF16)
    if io is not None:
        io["tabb"], io["wgbd"], io["wubd"], io["wdbd"] = tabb, wgbd, wubd, wdbd

    NT = T // 128

    with contextlib.ExitStack() as stack:
        p = ctx.p if not standalone else Prog(nc, stack)
        if standalone:
            p.fconst(0.0)
        cs = sbuf(p, stack, "cs", [128, 64], F32)
        cb = sbuf(p, stack, "cb", [8, 1024], F32)
        cid = sbuf(p, stack, "cid", [128, 128], BF16)
        p.dma("sp", cs, cs.ap, None, c_stats[:, :], "cld")
        p.dma("sp", cb, cb.ap, None, c_bsel[:, :], "cld")
        p.dma("sp", cid, cid.ap, None, c_ident[:, :], "cld")
        consts = {"stats_sel": cs, "bsel": cb}
        p.barrier()

        if not prep_done:
            tflat_in = tabc.t.ap().rearrange("(s p a) d -> s p (a d)", s=16, p=128)
            tflat_out = tabb.t.ap().rearrange("(s p a) d -> s p (a d)", s=16, p=128)
        with contextlib.ExitStack() as st:
          if prep_done:
            pass
          else:
              tb = [sbuf(p, st, f"tb{i}", [128, 16384], BF16) for i in range(2)]
              for s in range(16):
                  b = tb[s % 2]
                  p.dma("pool", b, b.ap, None, tflat_in[s], f"tc{s % 2}")
                  p.dma("sp", None, tflat_out[s], b, b.ap, f"ts{s % 2}")
              for (srcb, src_t, dstb, dst_t) in ((wgT, wgT.t, wgbd, wgbd.t),
                                                 (wuT, wuT.t, wubd, wubd.t),
                                                 (wdT, wdT.t, wdbd, wdbd.t)):
                  fin = src_t.ap().rearrange("(s a p) f -> s p a f", a=4, p=128)
                  fout = dst_t.ap().rearrange("(s a p) f -> s p a f", a=4, p=128)
                  nf = fin.shape[3]
                  for sidx in range(fin.shape[0]):
                      b = tb[sidx % 2]
                      bv = b.t[:, 0:4 * nf].rearrange("p (a f) -> p a f", a=4)
                      p.dma("pool", b, bv, srcb, fin[sidx], f"tc{sidx % 2}")
                      p.dma("sp", None, fout[sidx], b, bv, f"ts{sidx % 2}")
              p.wait_all("pool", "ts0")
              p.wait_all("pool", "ts1")
              p.barrier()

        emit_ln_stream(p, stack, "ln", x_own, lnx, T, consts)

        hfm = [sbuf(p, stack, f"hfm{c}", [128, T], BF16) for c in range(NCHUNK)]
        for c in range(NCHUNK):
            p.dma("sp", hfm[c], hfm[c].ap, None, lnx[c * 128:(c + 1) * 128, :], "hld")
        ht = [sbuf(p, stack, f"ht{t}", [128, D], BF16) for t in range(NT)]
        WR = sbuf(p, stack, "WR", [128, NT * 32], F32)
        GT = sbuf(p, stack, "GT", [128, NT * 32], F32)
        EX = sbuf(p, stack, "EX", [128, NT * 32], U32)
        p.barrier()

        with contextlib.ExitStack() as st:
            pT = [psum(p, st, f"pT{i}", [128, 128], BF16) for i in range(2)]
            for t in range(NT):
                for c in range(NCHUNK):
                    pt = pT[c % 2]
                    p.op("pe", lambda e, t=t, c=c, pt=pt: e.transpose(
                        pt.ap, hfm[c][:, t * 128:(t + 1) * 128], cid.ap),
                        reads=[hfm[c], cid], writes=[pt])
                    p.op("dve", lambda e, t=t, c=c, pt=pt: e.tensor_copy(
                        ht[t][:, c * 128:(c + 1) * 128], pt.ap),
                        reads=[pt], writes=[ht[t]])
            p.barrier()

        with contextlib.ExitStack() as st:
            q = [sbuf(p, st, f"q{c}", [128, T], BF16) for c in range(4)]
            wq_s = [sbuf(p, st, f"wqs{i}", [128, 512], BF16) for i in range(NCHUNK)]
            kb = sbuf(p, st, "kb", [128, 512], BF16)
            pQ = [psum(p, st, f"pQ{i}", [128, 512]) for i in range(2)]
            for i in range(NCHUNK):
                p.dma("pool", wq_s[i], wq_s[i].ap, None,
                      wqT[i * 128:(i + 1) * 128, :], "wld")
            for bb in range(8):
                p.dma("pool", kb, kb[64 * (bb % 2):64 * (bb % 2) + 64,
                                     128 * (bb // 2):128 * (bb // 2) + 128],
                      None, keysx[64 * bb:64 * (bb + 1), :], "wld")
            p.barrier()
            for oc in range(4):
                for tt4 in range(4):
                    pq = pQ[tt4 % 2]
                    for ic in range(NCHUNK):
                        p.op("pe", lambda e, oc=oc, tt4=tt4, ic=ic, pq=pq: e.matmul(
                            pq.ap, wq_s[ic][:, oc * 128:(oc + 1) * 128],
                            hfm[ic][:, tt4 * 512:(tt4 + 1) * 512],
                            start=(ic == 0), stop=(ic == NCHUNK - 1)),
                            reads=[wq_s[ic], hfm[ic]], writes=[pq], acc=(ic > 0))
                    p.op("dve", lambda e, oc=oc, tt4=tt4, pq=pq: e.tensor_copy(
                        q[oc][:, tt4 * 512:(tt4 + 1) * 512], pq.ap),
                        reads=[pq], writes=[q[oc]])
            pS = [psum(p, st, f"pS{i}", [128, 128]) for i in range(2)]
            VX = sbuf(p, st, "VX", [128, NT * 32], F32)
            VY = sbuf(p, st, "VY", [128, NT * 32], F32)
            IX = sbuf(p, st, "IX", [128, NT * 32], U32)
            IY = sbuf(p, st, "IY", [128, NT * 32], U32)
            for t in range(NT):
                for pl in range(2):
                    for h in range(4):
                        b = 4 * pl + h
                        chunk, off = b // 2, 64 * (b % 2)
                        ps = pS[(2 * t + 2 * pl + h) % 2]
                        p.op("pe", lambda e, t=t, b=b, chunk=chunk, off=off, ps=ps:
                             e.matmul(ps.ap,
                                      q[chunk][off:off + 64, t * 128:(t + 1) * 128],
                                      kb[off:off + 64, chunk * 128:(chunk + 1) * 128],
                                      start=True, stop=True),
                             reads=[q[chunk], kb], writes=[ps])
                        VV, II = (VX, IX) if pl == 0 else (VY, IY)
                        col = t * 32 + h * 8
                        p.op("dve", lambda e, ps=ps, VV=VV, col=col: e.max(
                            out=VV[:, col:col + 8], in_=ps.ap),
                            reads=[ps], writes=[VV])
                        p.op("dve", lambda e, ps=ps, VV=VV, II=II, col=col:
                             e.max_index(II[:, col:col + 8], VV[:, col:col + 8],
                                         ps.ap), reads=[ps, VV], writes=[II])
        with contextlib.ExitStack() as st:
            s64 = sbuf(p, st, "s64", [128, 1024], F32)
            i64f = sbuf(p, st, "i64f", [128, 1024], F32)
            v8 = sbuf(p, st, "v8", [128, 128], F32)
            c8 = sbuf(p, st, "c8", [128, 128], U32)
            isel = sbuf(p, st, "isel", [128, 128], U32)
            jsel = sbuf(p, st, "jsel", [128, 128], U32)
            mk = sbuf(p, st, "mk", [128, 128], F32)
            tsel = sbuf(p, st, "tsel", [128, 128], F32)
            xac = sbuf(p, st, "xac", [128, 128], F32)
            yac = sbuf(p, st, "yac", [128, 128], F32)
            smax = sbuf(p, st, "smx", [128, 128], F32)
            ssum = sbuf(p, st, "ssm", [128, 128], F32)
            sexp = sbuf(p, st, "sxp", [128, 1024], F32)
            for g4 in range(4):
                cols = slice(g4 * 4 * 32, (g4 + 1) * 4 * 32)
                vx = VX[:, cols].rearrange("p (t h i) -> p (t h) i", t=4, h=4)
                vy = VY[:, cols].rearrange("p (t h j) -> p (t h) j", t=4, h=4)
                ix = IX[:, cols].rearrange("p (t h i) -> p (t h) i", t=4, h=4)
                iy = IY[:, cols].rearrange("p (t h j) -> p (t h) j", t=4, h=4)
                s64v = s64.t[:].rearrange("p (s i j) -> p (s i) j", s=16, i=8)
                i64v = i64f.t[:].rearrange("p (s i j) -> p (s i) j", s=16, i=8)
                for i in range(8):
                    ov = s64v.rearrange("p (s i) j -> p s i j", i=8)[:, :, i, :]
                    p.op("dve", lambda e, vx=vx, vy=vy, ov=ov, i=i: e.tensor_tensor(
                        ov, vx[:, :, i:i + 1].to_broadcast([128, 16, 8]),
                        vy, OP.add), reads=[VX, VY], writes=[s64])
                    oiv = i64v.rearrange("p (s i) j -> p s i j", i=8)[:, :, i, :]
                    p.op("dve", lambda e, ix=ix, iy=iy, oiv=oiv, i=i:
                         e.scalar_tensor_tensor(
                             oiv, ix[:, :, i:i + 1].to_broadcast([128, 16, 8]),
                             128.0, iy, OP.mult, OP.add),
                         reads=[IX, IY], writes=[i64f])
                for sgm in range(16):
                    p.op("dve", lambda e, sgm=sgm: e.max(
                        out=v8[:, 8 * sgm:8 * sgm + 8],
                        in_=s64[:, 64 * sgm:64 * (sgm + 1)]),
                        reads=[s64], writes=[v8])
                    p.op("dve", lambda e, sgm=sgm: e.max_index(
                        c8[:, 8 * sgm:8 * sgm + 8], v8[:, 8 * sgm:8 * sgm + 8],
                        s64[:, 64 * sgm:64 * (sgm + 1)]),
                        reads=[s64, v8], writes=[c8])
                p.op("dve", lambda e: e.tensor_scalar(
                    isel.ap, c8.ap, 3, None, OP.logical_shift_right),
                    reads=[c8], writes=[isel])
                p.op("dve", lambda e: e.tensor_scalar(
                    jsel.ap, c8.ap, 7, None, OP.bitwise_and),
                    reads=[c8], writes=[jsel])
                p.op("dve", lambda e: e.memset(xac.ap, 0.0), writes=[xac])
                p.op("dve", lambda e: e.memset(yac.ap, 0.0), writes=[yac])
                for i in range(8):
                    for (SEL, SRC, ACC) in ((isel, ix, xac), (jsel, iy, yac)):
                        p.op("dve", lambda e, SEL=SEL, i=i: e.tensor_scalar(
                            mk.ap, SEL.ap, i, None, OP.is_equal),
                            reads=[SEL], writes=[mk])
                        srcb = SRC[:, :, i:i + 1].to_broadcast([128, 16, 8])
                        mkv = mk.t[:].rearrange("p (s k) -> p s k", s=16)
                        p.op("dve", lambda e, srcb=srcb, mkv=mkv: e.tensor_tensor(
                            tsel.t[:].rearrange("p (s k) -> p s k", s=16),
                            mkv, srcb, OP.mult),
                            reads=[mk, IX if SRC is ix else IY], writes=[tsel])
                        p.op("dve", lambda e, ACC=ACC: e.tensor_add(
                            ACC.ap, ACC.ap, tsel.ap), reads=[tsel, ACC], writes=[ACC])
                p.op("dve", lambda e: e.scalar_tensor_tensor(
                    tsel.ap, xac.ap, 128.0, yac.ap, OP.mult, OP.add),
                    reads=[xac, yac], writes=[tsel])
                p.op("dve", lambda e, g4=g4: e.tensor_copy(
                    EX[:, g4 * 128:(g4 + 1) * 128], tsel.ap),
                    reads=[tsel], writes=[EX])
                v8v = v8.t[:].rearrange("p (s k) -> p s k", s=16)
                p.op("dve", lambda e, v8v=v8v: e.tensor_reduce(
                    smax.t[:].rearrange("p (s o) -> p s o", s=16, o=1)
                    if False else smax[:, 0:16], v8v, AX.X, OP.max),
                    reads=[v8], writes=[smax])
                p.op("dve", lambda e, v8v=v8v: e.tensor_sub(
                    sexp[:, 0:128].rearrange("p (s k) -> p s k", s=16), v8v,
                    smax[:, 0:16].unsqueeze(2).to_broadcast([128, 16, 8])),
                    reads=[v8, smax], writes=[sexp])
                p.op("act", lambda e: e.activation(
                    sexp[:, 0:128], sexp[:, 0:128], AF.Exp),
                    reads=[sexp], writes=[sexp])
                p.op("dve", lambda e: e.tensor_reduce(
                    ssum[:, 0:16], sexp[:, 0:128].rearrange(
                        "p (s k) -> p s k", s=16), AX.X, OP.add),
                    reads=[sexp], writes=[ssum])
                p.op("dve", lambda e: e.reciprocal(ssum[:, 0:16], ssum[:, 0:16]),
                     reads=[ssum], writes=[ssum])
                p.op("dve", lambda e, g4=g4: e.tensor_mul(
                    GT[:, g4 * 128:(g4 + 1) * 128].rearrange(
                        "p (s k) -> p s k", s=16),
                    sexp[:, 0:128].rearrange("p (s k) -> p s k", s=16),
                    ssum[:, 0:16].unsqueeze(2).to_broadcast([128, 16, 8])),
                    reads=[sexp, ssum], writes=[GT])
            p.barrier()

        with contextlib.ExitStack() as st:
            gbs = [[sbuf(p, st, f"gb{g}_{i}", [128, 2048], BF16) for i in range(8)]
                   for g in range(2)]
            scr = sbuf(p, st, "scr", [128, 1024], BF16)
            scr2 = sbuf(p, st, "scr2", [128, 1024], BF16)
            eacc = [sbuf(p, st, f"eacc{i}", [128, 1024], F32) for i in range(2)]
            ebf = sbuf(p, st, "ebf", [128, 1024], BF16)
            slq = sbuf(p, st, "slq", [128, 32], F32)
            pT2 = [psum(p, st, f"pT2{i}", [128, 128], BF16) for i in range(2)]
            etile = sbuf(p, st, "etile", [128, 128], BF16)
            for t in range(NT):
                ea = eacc[t % 2]
                p.op("dve", lambda e, ea=ea: e.memset(ea.ap, 0.0), writes=[ea])
                for sgrp in range(4):
                    grp = gbs[sgrp % 2]
                    gtok = None
                    for s_i in range(8):
                        s = sgrp * 8 + s_i
                        gtok = p.op("pool", lambda e, t=t, s=s, s_i=s_i, grp=grp:
                                    e.indirect_dma_start(
                                        out=grp[s_i].ap, out_offset=None,
                                        in_=tabb[:, :],
                                        in_offset=bass.IndirectOffsetOnAxis(
                                            ap=EX[:, t * 32 + s:t * 32 + s + 1],
                                            axis=0)),
                                    reads=[EX, tabb], writes=[grp[s_i]],
                                    inc_sem=f"g{sgrp % 2}", inc=16)
                    for s_i in range(8):
                        s = sgrp * 8 + s_i
                        col = t * 32 + s
                        p.op("dve", lambda e, s_i=s_i, t=t, grp=grp:
                             e.tensor_mul(scr.ap, grp[s_i][:, 0:1024], ht[t].ap),
                             reads=[grp[s_i], ht[t]], writes=[scr], deps=[gtok])
                        p.op("act", lambda e, col=col: e.activation(
                            scr2.ap, scr.ap, AF.Identity,
                            accum_out=WR[:, col:col + 1]),
                            reads=[scr], writes=[scr2, WR])
                    c0 = t * 32 + sgrp * 8
                    p.op("act", lambda e, c0=c0: e.activation(
                        slq[:, 0:8], WR[:, c0:c0 + 8], AF.Silu),
                        reads=[WR], writes=[slq])
                    p.op("dve", lambda e, c0=c0: e.scalar_tensor_tensor(
                        GT[:, c0:c0 + 8], slq[:, 0:8], 0.5,
                        GT[:, c0:c0 + 8], OP.mult, OP.mult),
                        reads=[slq, GT], writes=[GT])
                    for s_i in range(8):
                        s = sgrp * 8 + s_i
                        col = t * 32 + s
                        p.op("dve", lambda e, s_i=s_i, col=col, grp=grp, ea=ea:
                             e.scalar_tensor_tensor(
                                 ea.ap, grp[s_i][:, 1024:2048],
                                 GT[:, col:col + 1], ea.ap, OP.mult, OP.add),
                             reads=[grp[s_i], GT, ea], writes=[ea])
                p.op("dve", lambda e, ea=ea: e.tensor_copy(ebf.ap, ea.ap),
                     reads=[ea], writes=[ebf])
                for c in range(NCHUNK):
                    pt = pT2[c % 2]
                    p.op("pe", lambda e, c=c, pt=pt: e.transpose(
                        pt.ap, ebf[:, c * 128:(c + 1) * 128], cid.ap),
                        reads=[ebf, cid], writes=[pt])
                    p.op("dve", lambda e, pt=pt: e.tensor_copy(etile.ap, pt.ap),
                         reads=[pt], writes=[etile])
                    p.dma("sp", None, edram[c * 128:(c + 1) * 128,
                                           t * 128:(t + 1) * 128],
                          etile, etile.ap, "est")
            p.wait_all("sp", "est")
            p.barrier()

        with contextlib.ExitStack() as st:
            ag = [sbuf(p, st, f"ag{o}", [128, 1024], BF16) for o in range(32)]
            wgb = [sbuf(p, st, f"wgb{i}", [128, 1024], BF16) for i in range(2)]
            wub = [sbuf(p, st, f"wub{i}", [128, 1024], BF16) for i in range(2)]
            wdb = [sbuf(p, st, f"wdb{i}", [128, 4096], BF16) for i in range(2)]
            pG = psum(p, st, "pG", [128, 512])
            pU = psum(p, st, "pU", [128, 512])
            pM = [psum(p, st, f"pM{i}", [128, 512]) for i in range(2)]
            sg = sbuf(p, st, "sg", [128, 512], BF16)
            eb = sbuf(p, st, "eb", [128, 512], BF16)
            xoc = sbuf(p, st, "xoc", [128, 512], F32)
            xn = sbuf(p, st, "xn", [128, 512], F32)
            for big in range(2):
                b0 = big * 1024
                for oc in range(32):
                    wg_ = wgb[oc % 2]
                    wu_ = wub[oc % 2]
                    p.dma("sp", wg_, wg_.ap, wgbd,
                          wgbd[oc * 128:(oc + 1) * 128, :], "wmg")
                    p.dma("sp", wu_, wu_.ap, wubd,
                          wubd[oc * 128:(oc + 1) * 128, :], "wmu")
                    for h2 in range(2):
                        for ic in range(NCHUNK):
                            p.op("pe", lambda e, ic=ic, h2=h2, b0=b0, wg_=wg_:
                                 e.matmul(pG.ap,
                                          wg_[:, ic * 128:(ic + 1) * 128],
                                          hfm[ic][:, b0 + h2 * 512:b0 + (h2 + 1) * 512],
                                          start=(ic == 0), stop=(ic == NCHUNK - 1)),
                                 reads=[wg_, hfm[ic]], writes=[pG], acc=(ic > 0))
                        for ic in range(NCHUNK):
                            p.op("pe", lambda e, ic=ic, h2=h2, b0=b0, wu_=wu_:
                                 e.matmul(pU.ap,
                                          wu_[:, ic * 128:(ic + 1) * 128],
                                          hfm[ic][:, b0 + h2 * 512:b0 + (h2 + 1) * 512],
                                          start=(ic == 0), stop=(ic == NCHUNK - 1)),
                                 reads=[wu_, hfm[ic]], writes=[pU], acc=(ic > 0))
                        p.op("act", lambda e: e.activation(sg.ap, pG.ap, AF.Silu),
                             reads=[pG], writes=[sg])
                        p.op("dve", lambda e, oc=oc, h2=h2: e.tensor_mul(
                            ag[oc][:, h2 * 512:(h2 + 1) * 512], sg.ap, pU.ap),
                            reads=[sg, pU], writes=[ag[oc]])
                for oc8 in range(NCHUNK):
                    wd_ = wdb[oc8 % 2]
                    p.dma("sp", wd_, wd_.ap, wdbd,
                          wdbd[oc8 * 128:(oc8 + 1) * 128, :], "wmd")
                    for h2 in range(2):
                        for ic in range(32):
                            p.op("pe", lambda e, ic=ic, h2=h2, oc8=oc8, wd_=wd_:
                                 e.matmul(pM[h2].ap,
                                          wd_[:, ic * 128:(ic + 1) * 128],
                                          ag[ic][:, h2 * 512:(h2 + 1) * 512],
                                          start=(ic == 0), stop=(ic == 31)),
                                 reads=[wd_, ag[ic]], writes=[pM[h2]], acc=(ic > 0))
                        t0 = b0 + h2 * 512
                        p.dma("sp", xoc, xoc.ap, None,
                              x_own[oc8 * 128:(oc8 + 1) * 128, t0:t0 + 512], "xld")
                        p.dma("sp", eb, eb.ap, edram,
                              edram[oc8 * 128:(oc8 + 1) * 128, t0:t0 + 512], "eld")
                        p.op("dve", lambda e, h2=h2: e.scalar_tensor_tensor(
                            xn.ap, pM[h2].ap, 0.5, xoc.ap, OP.mult, OP.add),
                            reads=[pM[h2], xoc], writes=[xn])
                        p.op("dve", lambda e: e.tensor_add(xn.ap, xn.ap, eb.ap),
                             reads=[xn, eb], writes=[xn])
                        p.dma("sp", None, x_new[oc8 * 128:(oc8 + 1) * 128,
                                                t0:t0 + 512], xn, xn.ap, "xst")
            p.wait_all("sp", "xst")
            p.wait_all("pool", "xst")
            p.barrier()
        if standalone:
            blk = stack.enter_context(nc.Block())
            p.play(blk)
    return nc


def build_s5a(ctx=None, pre="", io=None):
    standalone = ctx is None
    nc = bass.Bass("TRN2", target_bir_lowering=False) if standalone else ctx.nc
    DI, DO, DN = _mkio(nc, pre, io or {})
    x5 = DI("x5", [D, T + 1], F32)
    cwv = DI("cwv", [D, 3], F32)
    cbv = DI("cbv", [D, 1], F32)
    kwT = DI("kwT", [D, D], F32)
    kbv = DI("kbv", [D, 1], F32)
    vwT = DI("vwT", [D, D], F32)
    vbv = DI("vbv", [D, 1], F32)
    c_stats = DI("c_stats", [128, 64], F32)
    c_bsel = DI("c_bsel", [8, 1024], F32)
    o_k = DO("o_k", [D, 1024], BF16)
    o_v = DO("o_v", [1024, D], BF16)
    o_lnx = DO("o_lnx", [D, T], BF16)
    lnx5 = DN("lnx5", [D, T + 1], BF16)
    TD = 1024

    with contextlib.ExitStack() as stack:
        p = ctx.p if not standalone else Prog(nc, stack)
        if standalone:
            p.fconst(0.0)
        cs = sbuf(p, stack, "cs", [128, 64], F32)
        cb = sbuf(p, stack, "cb", [8, 1024], F32)
        p.dma("sp", cs, cs.ap, None, c_stats[:, :], "cld")
        p.dma("sp", cb, cb.ap, None, c_bsel[:, :], "cld")
        consts = {"stats_sel": cs, "bsel": cb}
        p.barrier()
        emit_ln_stream(p, stack, "ln", x5, lnx5, T + 1, consts)
        with contextlib.ExitStack() as st:
            cpb = sbuf(p, st, "cpb", [128, T], BF16)
            for c in range(NCHUNK):
                p.dma("sp", cpb, cpb.ap, None, lnx5[c * 128:(c + 1) * 128, 1:T + 1],
                      "cpl")
                p.dma("sp", None, o_lnx[c * 128:(c + 1) * 128, :], cpb, cpb.ap, "cps")
            p.wait_all("sp", "cps")
            p.barrier()
        with contextlib.ExitStack() as st:
            lx = [sbuf(p, st, f"lx{c}", [128, T + 2], BF16) for c in range(NCHUNK)]
            xd = [sbuf(p, st, f"xd{c}", [128, TD], BF16) for c in range(NCHUNK)]
            acc = sbuf(p, st, "acc", [128, TD], F32)
            cw = sbuf(p, st, "cw", [128, 3 * NCHUNK], F32)
            cbb = sbuf(p, st, "cbb", [128, NCHUNK], F32)
            for c in range(NCHUNK):
                p.dma("sp", lx[c], lx[c][:, 0:T + 1], None,
                      lnx5[c * 128:(c + 1) * 128, :], "lxl")
                p.dma("sp", cw, cw[:, 3 * c:3 * c + 3], None,
                      cwv[c * 128:(c + 1) * 128, :], "cld")
                p.dma("sp", cbb, cbb[:, c:c + 1], None,
                      cbv[c * 128:(c + 1) * 128, :], "cld")
            p.barrier()

            def tap(c, off):
                return lx[c][:, off:off + 2 * TD].rearrange(
                    "p (n two) -> p n two", two=2)[:, :, 0:1].squeeze(2)

            for c in range(NCHUNK):
                p.op("dve", lambda e, c=c: e.tensor_scalar(
                    acc.ap, tap(c, 0), cw[:, 3 * c:3 * c + 1], None, OP.mult),
                    reads=[lx[c], cw], writes=[acc])
                p.op("dve", lambda e, c=c: e.scalar_tensor_tensor(
                    acc.ap, tap(c, 1), cw[:, 3 * c + 1:3 * c + 2], acc.ap,
                    OP.mult, OP.add), reads=[lx[c], cw, acc], writes=[acc])
                p.op("dve", lambda e, c=c: e.scalar_tensor_tensor(
                    acc.ap, tap(c, 2), cw[:, 3 * c + 2:3 * c + 3], acc.ap,
                    OP.mult, OP.add), reads=[lx[c], cw, acc], writes=[acc])
                p.op("dve", lambda e, c=c: e.tensor_scalar(
                    xd[c].ap, acc.ap, cbb[:, c:c + 1], None, OP.add),
                    reads=[acc, cbb], writes=[xd[c]])
            p.barrier()
            kw = [sbuf(p, st, f"kw{i}", [128, D], BF16) for i in range(NCHUNK)]
            vw = [sbuf(p, st, f"vw{i}", [128, D], BF16) for i in range(NCHUNK)]
            kbs = sbuf(p, st, "kbs", [128, NCHUNK], F32)
            vbr = sbuf(p, st, "vbr", [1, D], F32)
            onesr = sbuf(p, st, "onesr", [1, 128], F32)
            pP = [psum(p, st, f"pP{i}", [128, 512]) for i in range(2)]
            pBv = [psum(p, st, f"pBv{i}", [128, 512]) for i in range(2)]
            sBv = sbuf(p, st, "sBv", [128, D], F32)
            kout = sbuf(p, st, "kout", [128, TD], BF16)
            vout = sbuf(p, st, "vout", [128, 512], BF16)
            for i in range(NCHUNK):
                p.dma("pool", kw[i], kw[i].ap, None, kwT[i * 128:(i + 1) * 128, :], "wld")
                p.dma("pool", vw[i], vw[i].ap, None, vwT[i * 128:(i + 1) * 128, :], "wld")
                p.dma("sp", kbs, kbs[:, i:i + 1], None, kbv[i * 128:(i + 1) * 128, :], "cld")
            p.dma("sp", vbr, vbr.ap, None, vbv[:, 0:1].rearrange("a b -> b a"), "cld")
            p.op("dve", lambda e: e.memset(onesr.ap, 1.0), writes=[onesr])
            for half in range(2):
                p.op("pe", lambda e, half=half: e.matmul(
                    pBv[half].ap, onesr.ap, vbr[:, half * 512:(half + 1) * 512],
                    start=True, stop=True), reads=[onesr, vbr], writes=[pBv[half]])
                p.op("dve", lambda e, half=half: e.tensor_copy(
                    sBv[:, half * 512:(half + 1) * 512], pBv[half].ap),
                    reads=[pBv[half]], writes=[sBv])
            for oc in range(NCHUNK):
                for tt in range(2):
                    pp = pP[tt % 2]
                    for ic in range(NCHUNK):
                        p.op("pe", lambda e, oc=oc, tt=tt, ic=ic, pp=pp: e.matmul(
                            pp.ap, kw[ic][:, oc * 128:(oc + 1) * 128],
                            xd[ic][:, tt * 512:(tt + 1) * 512],
                            start=(ic == 0), stop=(ic == NCHUNK - 1)),
                            reads=[kw[ic], xd[ic]], writes=[pp], acc=(ic > 0))
                    p.op("act", lambda e, oc=oc, tt=tt, pp=pp: e.activation(
                        kout[:, tt * 512:(tt + 1) * 512], pp.ap, AF.Identity,
                        bias=kbs[:, oc:oc + 1]), reads=[pp, kbs], writes=[kout])
                p.dma("sp", None, o_k[oc * 128:(oc + 1) * 128, :], kout, kout.ap,
                      "kst")
            for vt in range(8):
                for half in range(2):
                    pp = pP[half % 2]
                    for ic in range(NCHUNK):
                        p.op("pe", lambda e, vt=vt, half=half, ic=ic, pp=pp:
                             e.matmul(pp.ap,
                                      xd[ic][:, vt * 128:(vt + 1) * 128],
                                      vw[ic][:, half * 512:(half + 1) * 512],
                                      start=(ic == 0), stop=(ic == NCHUNK - 1)),
                             reads=[vw[ic], xd[ic]], writes=[pp], acc=(ic > 0))
                    p.op("dve", lambda e, half=half, pp=pp: e.tensor_add(
                        vout.ap, pp.ap, sBv[:, half * 512:(half + 1) * 512]),
                        reads=[pp, sBv], writes=[vout])
                    p.dma("sp", None, o_v[vt * 128:(vt + 1) * 128,
                                          half * 512:(half + 1) * 512],
                          vout, vout.ap, "vst")
            p.wait_all("sp", "kst")
            p.wait_all("sp", "vst")
            p.wait_all("pool", "vst")
            p.barrier()
        if standalone:
            blk = stack.enter_context(nc.Block())
            p.play(blk)
    return nc


def build_s5b(ctx=None, pre="", io=None):
    standalone = ctx is None
    nc = bass.Bass("TRN2", target_bir_lowering=False) if standalone else ctx.nc
    DI, DO, DN = _mkio(nc, pre, io or {})
    x_own = DI("x_own", [D, T], F32)
    lnx = DI("lnx", [D, T], BF16)
    kf = DI("kf", [D, T], BF16)
    vf = DI("vf", [T, D], BF16)
    qwT = DI("qwT", [D, D], F32)
    qbv = DI("qbv", [D, 1], F32)
    owT = DI("owT", [D, D], F32)
    obv = DI("obv", [D, 1], F32)
    c_hs16 = DI("c_hs16", [128, 16 * NCHUNK], F32)
    c_bs16 = DI("c_bs16", [16, 16 * 128], F32)
    x_new = DO("x_new", [D, T], F32)
    NH, NG, NKT = 16, 4, 16

    with contextlib.ExitStack() as stack:
        p = ctx.p if not standalone else Prog(nc, stack)
        if standalone:
            p.fconst(0.0)
        ch16 = sbuf(p, stack, "ch16", [128, 16 * NCHUNK], F32)
        cb16 = sbuf(p, stack, "cb16", [16, 16 * 128], F32)
        p.dma("sp", ch16, ch16.ap, None, c_hs16[:, :], "cld")
        p.dma("sp", cb16, cb16.ap, None, c_bs16[:, :], "cld")
        Q = [sbuf(p, stack, f"Q{c}", [128, T], BF16) for c in range(NCHUNK)]
        K = [sbuf(p, stack, f"K{c}", [128, T], BF16) for c in range(NCHUNK)]
        V = [sbuf(p, stack, f"V{t}", [128, D], BF16) for t in range(NKT)]
        hx = [sbuf(p, stack, f"hx{c}", [128, T], BF16) for c in range(NCHUNK)]
        qw = [sbuf(p, stack, f"qw{i}", [128, D], BF16) for i in range(NCHUNK)]
        qbs = sbuf(p, stack, "qbs", [128, NCHUNK], F32)
        qn = sbuf(p, stack, "qn", [16, NG], F32)
        kn = sbuf(p, stack, "kn", [16, NKT // 4], F32)
        knm = sbuf(p, stack, "knm", [16, 1], F32)
        negM = sbuf(p, stack, "negM", [16, NG], F32)
        Mb = sbuf(p, stack, "Mb", [128, NH * NG], F32)
        onesc = sbuf(p, stack, "onesc", [128, 1], BF16)
        onesr = sbuf(p, stack, "onesr", [1, 128], F32)
        ofm = [sbuf(p, stack, f"ofm{c}", [128, T], BF16) for c in range(NCHUNK)]
        p.op("dve", lambda e: e.memset(onesc.ap, 1.0), writes=[onesc])
        p.op("dve", lambda e: e.memset(onesr.ap, 1.0), writes=[onesr])
        for c in range(NCHUNK):
            p.dma("sp", hx[c], hx[c].ap, None, lnx[c * 128:(c + 1) * 128, :], "hld")
            p.dma("sp", K[c], K[c].ap, None, kf[c * 128:(c + 1) * 128, :], "kld")
            p.dma("pool", qw[c], qw[c].ap, None, qwT[c * 128:(c + 1) * 128, :], "wld")
            p.dma("sp", qbs, qbs[:, c:c + 1], None, qbv[c * 128:(c + 1) * 128, :],
                  "cld")
        for t in range(NKT):
            p.dma("sp", V[t], V[t].ap, None, vf[t * 128:(t + 1) * 128, :], "vld")
        p.barrier()
        with contextlib.ExitStack() as st:
            pP = [psum(p, st, f"pP{i}", [128, 512]) for i in range(2)]
            pN = psum(p, st, "pN", [16, 512])
            sqb = [sbuf(p, st, f"sqb{i}", [128, 512], F32) for i in range(2)]
            nmax = sbuf(p, st, "nmax", [16, 1], F32)
            for g in range(NG):
                for oc in range(NCHUNK):
                    pp = pP[oc % 2]
                    for ic in range(NCHUNK):
                        p.op("pe", lambda e, g=g, oc=oc, ic=ic, pp=pp: e.matmul(
                            pp.ap, qw[ic][:, oc * 128:(oc + 1) * 128],
                            hx[ic][:, g * 512:(g + 1) * 512],
                            start=(ic == 0), stop=(ic == NCHUNK - 1)),
                            reads=[qw[ic], hx[ic]], writes=[pp], acc=(ic > 0))
                    p.op("act", lambda e, g=g, oc=oc, pp=pp: e.activation(
                        Q[oc][:, g * 512:(g + 1) * 512], pp.ap, AF.Identity,
                        bias=qbs[:, oc:oc + 1]), reads=[pp, qbs], writes=[Q[oc]])
                    sq = sqb[oc % 2]
                    p.op("act", lambda e, g=g, oc=oc, sq=sq: e.activation(
                        sq.ap, Q[oc][:, g * 512:(g + 1) * 512], AF.Square),
                        reads=[Q[oc]], writes=[sq])
                    p.op("pe", lambda e, oc=oc, sq=sq: e.matmul(
                        pN.ap, ch16[:, 16 * oc:16 * oc + 16], sq.ap,
                        start=(oc == 0), stop=(oc == NCHUNK - 1)),
                        reads=[ch16, sq], writes=[pN], acc=(oc > 0))
                p.op("dve", lambda e: e.reduce_max(nmax.ap, pN.ap, axis=AX.X),
                     reads=[pN], writes=[nmax])
                p.op("dve", lambda e, g=g: e.tensor_copy(qn[:, g:g + 1], nmax.ap),
                     reads=[nmax], writes=[qn])
            for g in range(NKT // 4):
                for oc in range(NCHUNK):
                    sq = sqb[oc % 2]
                    p.op("act", lambda e, g=g, oc=oc, sq=sq: e.activation(
                        sq.ap, K[oc][:, g * 512:(g + 1) * 512], AF.Square),
                        reads=[K[oc]], writes=[sq])
                    p.op("pe", lambda e, oc=oc, sq=sq: e.matmul(
                        pN.ap, ch16[:, 16 * oc:16 * oc + 16], sq.ap,
                        start=(oc == 0), stop=(oc == NCHUNK - 1)),
                        reads=[ch16, sq], writes=[pN], acc=(oc > 0))
                p.op("dve", lambda e: e.reduce_max(nmax.ap, pN.ap, axis=AX.X),
                     reads=[pN], writes=[nmax])
                p.op("dve", lambda e, g=g: e.tensor_copy(kn[:, g:g + 1], nmax.ap),
                     reads=[nmax], writes=[kn])
            p.op("dve", lambda e: e.reduce_max(knm.ap, kn.ap, axis=AX.X),
                 reads=[kn], writes=[knm])
            p.op("dve", lambda e: e.tensor_mul(
                negM.ap, qn.ap, knm[:, 0:1].to_broadcast([16, NG])),
                reads=[qn, knm], writes=[negM])
            tinyc, tinyb = p.fconstn(1e-30, 16)
            p.op("act", lambda e: e.activation(negM.ap, negM.ap, AF.Ln, bias=tinyc),
                 reads=[negM, tinyb], writes=[negM])
            p.op("act", lambda e: e.activation(negM.ap, negM.ap, AF.Exp, scale=0.5),
                 reads=[negM], writes=[negM])
            p.op("dve", lambda e: e.tensor_scalar(
                negM.ap, negM.ap, -0.125, 55.0, OP.mult, OP.add),
                reads=[negM], writes=[negM])
            pMb = psum(p, st, "pMb", [128, NH * NG])
            for h in range(NH):
                p.op("pe", lambda e, h=h: e.matmul(
                    pMb[:, NG * h:NG * h + NG], cb16[:, 128 * h:128 * h + 128],
                    negM.ap, start=True, stop=True),
                    reads=[cb16, negM], writes=[pMb], acc=(h > 0))
            p.op("dve", lambda e: e.tensor_copy(Mb.ap, pMb.ap),
                 reads=[pMb], writes=[Mb])
            p.barrier()
        with contextlib.ExitStack() as st:
            pS = [psum(p, st, f"pS{i}", [128, 512]) for i in range(2)]
            pZ = psum(p, st, "pZ", [1, 512])
            pO = [psum(p, st, f"pO{i}", [64, 512]) for i in range(2)]
            pB = psum(p, st, "pB", [64, 512])
            PH = [sbuf(p, st, f"PH{i}", [128, 512], BF16) for i in range(3)]
            rzb = sbuf(p, st, "rzb", [1, 512], F32)
            rb = sbuf(p, st, "rb", [64, 512], F32)
            phi = 0
            for h in range(NH):
                kc_chunk, koff = h // 2, 64 * (h % 2)
                for g in range(NG):
                    po = pO[g % 2]
                    for s_i in range(NKT):
                        ps = pS[s_i % 2]
                        p.op("pe", lambda e, h=h, g=g, s_i=s_i, ps=ps,
                             kc_chunk=kc_chunk, koff=koff: e.matmul(
                                 ps.ap,
                                 K[kc_chunk][koff:koff + 64,
                                             s_i * 128:(s_i + 1) * 128],
                                 Q[kc_chunk][koff:koff + 64,
                                             g * 512:(g + 1) * 512],
                                 start=True, stop=True),
                             reads=[K[kc_chunk], Q[kc_chunk]], writes=[ps])
                        ph = PH[phi % 3]
                        phi += 1
                        col = h * NG + g
                        p.op("act", lambda e, ps=ps, ph=ph, col=col: e.activation(
                            ph.ap, ps.ap, AF.Exp, scale=0.125,
                            bias=Mb[:, col:col + 1]), reads=[ps, Mb], writes=[ph])
                        p.op("pe", lambda e, ph=ph, s_i=s_i: e.matmul(
                            pZ.ap, onesc.ap, ph.ap,
                            start=(s_i == 0), stop=(s_i == NKT - 1)),
                            reads=[onesc, ph], writes=[pZ], acc=(s_i > 0))
                        p.op("pe", lambda e, ph=ph, s_i=s_i, po=po, h=h: e.matmul(
                            po.ap, V[s_i][:, 64 * h:64 * h + 64], ph.ap,
                            start=(s_i == 0), stop=(s_i == NKT - 1)),
                            reads=[V[s_i], ph], writes=[po], acc=(s_i > 0))
                    p.op("dve", lambda e: e.reciprocal(rzb.ap, pZ.ap),
                         reads=[pZ], writes=[rzb])
                    p.op("pe", lambda e: e.matmul(pB.ap, onesr[:, 0:64], rzb.ap,
                                                  start=True, stop=True),
                         reads=[onesr, rzb], writes=[pB])
                    p.op("dve", lambda e: e.tensor_copy(rb.ap, pB.ap),
                         reads=[pB], writes=[rb])
                    oc, ooff = h // 2, 64 * (h % 2)
                    p.op("dve", lambda e, po=po, oc=oc, ooff=ooff, g=g:
                         e.tensor_mul(ofm[oc][ooff:ooff + 64,
                                              g * 512:(g + 1) * 512],
                                      po.ap, rb.ap),
                         reads=[po, rb], writes=[ofm[oc]])
            p.barrier()
        with contextlib.ExitStack() as st:
            ow = [sbuf(p, st, f"ow{i}", [128, D], BF16) for i in range(NCHUNK)]
            obs = sbuf(p, st, "obs", [128, NCHUNK], F32)
            xoc = sbuf(p, st, "xoc", [128, 512], F32)
            xn = sbuf(p, st, "xn", [128, 512], F32)
            pX = [psum(p, st, f"pX{i}", [128, 512]) for i in range(2)]
            for i in range(NCHUNK):
                p.dma("pool", ow[i], ow[i].ap, None, owT[i * 128:(i + 1) * 128, :],
                      "wld")
                p.dma("sp", obs, obs[:, i:i + 1], None,
                      obv[i * 128:(i + 1) * 128, :], "cld")
            p.barrier()
            for tt in range(4):
                t0 = tt * 512
                for oc in range(NCHUNK):
                    px = pX[oc % 2]
                    for ic in range(NCHUNK):
                        p.op("pe", lambda e, ic=ic, oc=oc, t0=t0, px=px: e.matmul(
                            px.ap, ow[ic][:, oc * 128:(oc + 1) * 128],
                            ofm[ic][:, t0:t0 + 512],
                            start=(ic == 0), stop=(ic == NCHUNK - 1)),
                            reads=[ow[ic], ofm[ic]], writes=[px], acc=(ic > 0))
                    p.dma("sp", xoc, xoc.ap, None,
                          x_own[oc * 128:(oc + 1) * 128, t0:t0 + 512], "xld")
                    p.op("dve", lambda e, oc=oc, px=px: e.scalar_tensor_tensor(
                        xn.ap, px.ap, obs[:, oc:oc + 1], xoc.ap, OP.add, OP.add),
                        reads=[px, obs, xoc], writes=[xn])
                    p.dma("sp", None, x_new[oc * 128:(oc + 1) * 128, t0:t0 + 512],
                          xn, xn.ap, "xst")
            p.wait_all("sp", "xst")
            p.wait_all("pool", "xst")
            p.barrier()
        if standalone:
            blk = stack.enter_context(nc.Block())
            p.play(blk)
    return nc


def host_consts_s5():
    hs16 = np.zeros((128, 16 * NCHUNK), np.float32)
    for oc in range(NCHUNK):
        for j in range(2):
            hs16[64 * j:64 * (j + 1), 16 * oc + 2 * oc + j] = 1.0
    bs16 = np.zeros((16, 16 * 128), np.float32)
    for h in range(16):
        bs16[h, 128 * h:128 * (h + 1)] = 1.0
    return {"c_hs16": hs16, "c_bs16": bs16}

EXT = 4096
NQT = EXT // 128


def build_l0ln(ctx=None, pre="", io=None):
    standalone = ctx is None
    nc = bass.Bass("TRN2", target_bir_lowering=False) if standalone else ctx.nc
    DI, DO, DN = _mkio(nc, pre, io or {})
    x_ext = DI("x_ext", [D, EXT], F32)
    c_stats = DI("c_stats", [128, 64], F32)
    c_bsel = DI("c_bsel", [8, 1024], F32)
    y0 = DO("y0", [D, EXT + 1], F32)
    with contextlib.ExitStack() as stack:
        p = ctx.p if not standalone else Prog(nc, stack)
        if standalone:
            p.fconst(0.0)
        cs = sbuf(p, stack, "cs", [128, 64], F32)
        cb = sbuf(p, stack, "cb", [8, 1024], F32)
        zc = sbuf(p, stack, "zc", [128, 1], F32)
        p.dma("sp", cs, cs.ap, None, c_stats[:, :], "cld")
        p.dma("sp", cb, cb.ap, None, c_bsel[:, :], "cld")
        p.op("dve", lambda e: e.memset(zc.ap, 0.0), writes=[zc])
        consts = {"stats_sel": cs, "bsel": cb}
        p.barrier()
        yv = Buf(y0.t[:, 1:EXT + 1], "yv")
        emit_ln_stream(p, stack, "ln", x_ext, yv, EXT, consts, out_dt=F32)
        for c in range(NCHUNK):
            p.dma("sp", None, y0[c * 128:(c + 1) * 128, 0:1], zc, zc.ap, "z0",
                  allow_slow_non_contiguous=True)
        p.wait_all("sp", "z0")
        p.barrier()
        if standalone:
            blk = stack.enter_context(nc.Block())
            p.play(blk)
    return nc


def build_l0comb(ctx=None, pre="", io=None):
    standalone = ctx is None
    nc = bass.Bass("TRN2", target_bir_lowering=False) if standalone else ctx.nc
    DI, DO, DN = _mkio(nc, pre, io or {})
    x_own = DI("x_own", [D, T], F32)
    y2 = DI("y2", [D, T], F32)
    x_new = DO("x_new", [D, T], F32)
    with contextlib.ExitStack() as stack:
        p = ctx.p if not standalone else Prog(nc, stack)
        xa = sbuf(p, stack, "xa", [128, T], F32)
        ya = sbuf(p, stack, "ya", [128, T], F32)
        xn = sbuf(p, stack, "xn", [128, T], F32)
        for c in range(NCHUNK):
            p.dma("sp", xa, xa.ap, None, x_own[c * 128:(c + 1) * 128, :], "ld")
            p.dma("sp", ya, ya.ap, None, y2[c * 128:(c + 1) * 128, :], "ld")
            p.op("dve", lambda e: e.scalar_tensor_tensor(
                xn.ap, ya.ap, 0.5, xa.ap, OP.mult, OP.add),
                reads=[xa, ya], writes=[xn])
            p.dma("sp", None, x_new[c * 128:(c + 1) * 128, :], xn, xn.ap, "st")
        p.wait_all("sp", "st")
        p.barrier()
        if standalone:
            blk = stack.enter_context(nc.Block())
            p.play(blk)
    return nc


def build_l0inner(ctx=None, pre="", io=None, phases=4, p3m=32, p3wo=True, p3bits=31, dbg=False):
    standalone = ctx is None
    nc = bass.Bass("TRN2", target_bir_lowering=False) if standalone else ctx.nc
    DI, DO, DN = _mkio(nc, pre, io or {})
    y_in = DI("y_in", [D, EXT + 1], F32)
    wqkvp = DI("wqkvp", [D, 1536], F32)
    qsc = DI("qsc", [128, 4], F32)
    c1q = DI("c1q", [128, EXT], F32)
    c2q = DI("c2q", [128, EXT], F32)
    c1k = DI("c1k", [128, EXT], F32)
    c2k = DI("c2k", [128, EXT], F32)
    wo = DI("wo", [512, D], F32)
    lng = DI("lng", [D, 1], F32)
    lnb = DI("lnb", [D, 1], F32)
    w1 = DI("w1", [D, 5460], F32)
    b1 = DI("b1", [5460, 1], F32)
    w2 = DI("w2", [2730, D], F32)
    b2 = DI("b2", [D, 1], F32)
    masks = DI("masks", [128, NQT * 5 * 128], BF16)
    c_stats = DI("c_stats", [128, 64], F32)
    c_bsel = DI("c_bsel", [8, 1024], F32)
    c_hsp = DI("c_hsp", [128, 16], F32)
    c_bs4 = DI("c_bs4", [8, 256], F32)
    c_perm = DI("c_perm", [128, 512], BF16)
    c_hs64 = DI("c_hs64", [128, 32], BF16)
    x_out = DO("x_out", [D, EXT], F32)
    kind2 = "Internal" if not dbg else "ExternalOutput"
    if dbg:
        xmid = DO("xmid", [D, EXT + 1], F32)
        hffn = DO("hffn", [D, EXT], BF16)
        qdbg = DO("qdbg", [D // 2, EXT], BF16)
        vdbg = DO("vdbg", [EXT, 512], BF16)
        mdbg = DO("mdbg", [128, 8 * NQT], F32)
        rdbg = DO("rdbg", [8, 16 * 512], F32)
        ardbg = DO("ardbg", [128, 16 * 512], BF16)
        apdbg = DO("apdbg", [128, 16 * 512], BF16)
        rbdbg = DO("rbdbg", [128, 16 * 512], F32)
        cbdbg = DO("cbdbg", [128, 16 * 512], F32)
        tadbg = DO("tadbg", [128, 16 * 512], F32)
        kdbg = DO("kdbg", [D // 2, EXT], BF16)
        odbg = DO("odbg", [D // 2, EXT], BF16)
    else:
        xmid = DN("xmid", [D, EXT + 1], F32)
        hffn = DN("hffn", [D, EXT], BF16)

    NS = EXT // 512

    with contextlib.ExitStack() as stack:
        p = ctx.p if not standalone else Prog(nc, stack)
        if standalone:
            p.fconst(0.0)
        cs = sbuf(p, stack, "cs", [128, 64], F32)
        cb = sbuf(p, stack, "cb", [8, 1024], F32)
        chp = sbuf(p, stack, "chp", [128, 16], F32)
        cb4 = sbuf(p, stack, "cb4", [8, 256], F32)
        qscb = sbuf(p, stack, "qscb", [128, 4], F32)
        cpm = sbuf(p, stack, "cpm", [128, 512], BF16)
        ch64 = sbuf(p, stack, "ch64", [128, 32], BF16)
        onesc = sbuf(p, stack, "onesc", [128, 1], BF16)
        onesr = sbuf(p, stack, "onesr", [1, 128], F32)
        p.dma("sp", cs, cs.ap, None, c_stats[:, :], "cld")
        p.dma("sp", cb, cb.ap, None, c_bsel[:, :], "cld")
        p.dma("sp", chp, chp.ap, None, c_hsp[:, :], "cld")
        p.dma("sp", cb4, cb4.ap, None, c_bs4[:, :], "cld")
        p.dma("sp", qscb, qscb.ap, None, qsc[:, :], "cld")
        p.dma("sp", cpm, cpm.ap, None, c_perm[:, :], "cld")
        p.dma("sp", ch64, ch64.ap, None, c_hs64[:, :], "cld")
        p.op("dve", lambda e: e.memset(onesc.ap, 1.0), writes=[onesc])
        p.op("dve", lambda e: e.memset(onesr.ap, 1.0), writes=[onesr])
        consts = {"stats_sel": cs, "bsel": cb}
        p.barrier()

        attn_stack = contextlib.ExitStack()
        QH = [sbuf(p, attn_stack, f"QH{g}", [128, EXT], BF16) for g in range(4)]
        KH = [sbuf(p, attn_stack, f"KH{g}", [128, EXT], BF16) for g in range(4)]
        V = [sbuf(p, attn_stack, f"V{t}", [128, 512], BF16) for t in range(NQT)]
        Mb = sbuf(p, attn_stack, "Mb", [128, 8 * NQT], F32)

        def shifted_rhs(c, t0, tl):
            off = 1 if c < 4 else 0
            return y_in[c * 128:(c + 1) * 128, t0 + off:t0 + off + tl]

        with contextlib.ExitStack() as st:
            wb = [sbuf(p, st, f"wb{i}", [128, 1536], BF16) for i in range(NCHUNK)]
            yb = [sbuf(p, st, f"yb{i}", [128, 8 * 512], BF16) for i in range(2)]
            AR = [sbuf(p, st, f"AR{g}", [128, 512], BF16) for g in range(2)]
            BR = [sbuf(p, st, f"BR{g}", [128, 512], BF16) for g in range(2)]
            pP = [psum(p, st, f"pP{i}", [128, 512]) for i in range(2)]
            pN = psum(p, st, "pN", [8, 512])
            pR = psum(p, st, "pR", [128, 512])
            sq = [sbuf(p, st, f"sq{i}", [128, 512], F32) for i in range(2)]
            rstd = sbuf(p, st, "rstd", [8, 512], F32)
            rb = sbuf(p, st, "rb", [128, 512], F32)
            rbA = sbuf(p, st, "rbA", [128, 512], F32)
            rbB = sbuf(p, st, "rbB", [128, 512], F32)
            tA = sbuf(p, st, "tA", [128, 512], F32)
            tB = sbuf(p, st, "tB", [128, 512], F32)
            u1 = sbuf(p, st, "u1", [128, 512], F32)
            u2 = sbuf(p, st, "u2", [128, 512], F32)
            cbuf = [sbuf(p, st, f"cbuf{i}", [128, 512], F32) for i in range(2)]
            for i in range(NCHUNK):
                p.dma("pool", wb[i], wb[i].ap, None,
                      wqkvp[i * 128:(i + 1) * 128, :], "wld")
            p.barrier()
            for s_ in range(NS):
                t0 = s_ * 512
                y = yb[s_ % 2]
                for c in range(NCHUNK):
                    p.dma("pool", y, y[:, c * 512:(c + 1) * 512], None,
                          shifted_rhs(c, t0, 512), "yld")
                for sub in range(4):
                    pp = pP[sub % 2]
                    for ic in range(NCHUNK):
                        p.op("pe", lambda e, sub=sub, ic=ic, pp=pp, y=y: e.matmul(
                            pp.ap,
                            y[:, ic * 512 + sub * 128:ic * 512 + sub * 128 + 128],
                            wb[ic][:, 1024:1536],
                            start=(ic == 0), stop=(ic == NCHUNK - 1)),
                            reads=[wb[ic], y], writes=[pp], acc=(ic > 0))
                    vt = V[s_ * 4 + sub]
                    p.op("dve", lambda e, vt=vt, pp=pp: e.tensor_copy(
                        vt.ap, pp.ap), reads=[pp], writes=[vt])
                for (kind, c1t, c2t, acol, bcol, HT) in (
                        ("q", c1q, c2q, 0, 1, QH), ("k", c1k, c2k, 2, 3, KH)):
                    base_oc = 0 if kind == "q" else 4
                    for g in range(2):
                        for (ocd, dst) in ((base_oc * 0 + g, AR[g]),
                                           (2 + g, BR[g])):
                            oc_abs = (0 if kind == "q" else 4) + ocd
                            pp = pP[ocd % 2]
                            for ic in range(NCHUNK):
                                p.op("pe", lambda e, oc_abs=oc_abs, ic=ic, pp=pp,
                                     y=y: e.matmul(
                                         pp.ap,
                                         wb[ic][:, oc_abs * 128:(oc_abs + 1) * 128],
                                         y[:, ic * 512:(ic + 1) * 512],
                                         start=(ic == 0), stop=(ic == NCHUNK - 1)),
                                     reads=[wb[ic], y], writes=[pp], acc=(ic > 0))
                            p.op("dve", lambda e, dst=dst, pp=pp: e.tensor_copy(
                                dst.ap, pp.ap), reads=[pp], writes=[dst])
                    for g in range(2):
                        sqq = sq[g % 2]
                        p.op("act", lambda e, g=g, sqq=sqq: e.activation(
                            sqq.ap, AR[g].ap, AF.Square),
                            reads=[AR[g]], writes=[sqq])
                        p.op("pe", lambda e, g=g, sqq=sqq: e.matmul(
                            pN.ap, chp[:, 8 * g:8 * g + 8], sqq.ap,
                            start=(g == 0), stop=False),
                            reads=[chp, sqq], writes=[pN], acc=(g > 0))
                        p.op("act", lambda e, g=g, sqq=sqq: e.activation(
                            sqq.ap, BR[g].ap, AF.Square),
                            reads=[BR[g]], writes=[sqq])
                        p.op("pe", lambda e, g=g, sqq=sqq: e.matmul(
                            pN.ap, chp[:, 8 * g:8 * g + 8], sqq.ap,
                            start=False, stop=(g == 1)),
                            reads=[chp, sqq], writes=[pN], acc=True)
                    tinyc, tinyb = p.fconstn(1e-24, 8)
                    p.op("act", lambda e: e.activation(rstd.ap, pN.ap, AF.Ln,
                                                       bias=tinyc),
                         reads=[pN, tinyb], writes=[rstd])
                    p.op("act", lambda e: e.activation(rstd.ap, rstd.ap, AF.Exp,
                                                       scale=-0.5),
                         reads=[rstd], writes=[rstd])
                    if dbg:
                        kk_ = 0 if kind == "q" else 1
                        p.dma("sp", None,
                              rdbg[:, (s_ * 2 + kk_) * 512:(s_ * 2 + kk_ + 1) * 512],
                              rstd, rstd.ap, "dbg")
                        p.dma("sp", None,
                              ardbg[:, (s_ * 2 + kk_) * 512:(s_ * 2 + kk_ + 1) * 512],
                              AR[0], AR[0].ap, "dbg")
                    for i, cc in enumerate((c1t, c2t)):
                        p.dma("sp", cbuf[i], cbuf[i].ap, None,
                              cc[:, t0:t0 + 512], "cbl")
                    for g in range(2):
                        p.op("pe", lambda e, g=g: e.matmul(
                            pR.ap, cb4[:, 128 * g:128 * (g + 1)], rstd.ap,
                            start=True, stop=True), reads=[cb4, rstd], writes=[pR])
                        p.op("dve", lambda e: e.tensor_copy(rb.ap, pR.ap),
                             reads=[pR], writes=[rb])
                        p.op("dve", lambda e, acol=acol: e.tensor_scalar(
                            rbA.ap, rb.ap, qscb[:, acol:acol + 1], None, OP.mult),
                            reads=[rb, qscb], writes=[rbA])
                        p.op("dve", lambda e, bcol=bcol: e.tensor_scalar(
                            rbB.ap, rb.ap, qscb[:, bcol:bcol + 1], None, OP.mult),
                            reads=[rb, qscb], writes=[rbB])
                        p.op("dve", lambda e, g=g: e.tensor_mul(
                            tA.ap, AR[g].ap, rbA.ap), reads=[AR[g], rbA],
                            writes=[tA])
                        p.op("dve", lambda e, g=g: e.tensor_mul(
                            tB.ap, BR[g].ap, rbB.ap), reads=[BR[g], rbB],
                            writes=[tB])
                        p.op("dve", lambda e: e.tensor_mul(u1.ap, tA.ap,
                                                           cbuf[0].ap),
                             reads=[tA, cbuf[0]], writes=[u1])
                        p.op("dve", lambda e: e.tensor_mul(u2.ap, tB.ap,
                                                           cbuf[1].ap),
                             reads=[tB, cbuf[1]], writes=[u2])
                        p.op("dve", lambda e, g=g: e.tensor_sub(
                            AR[g].ap, u1.ap, u2.ap), reads=[u1, u2],
                            writes=[AR[g]])
                        if dbg and g == 0:
                            kk_ = 0 if kind == "q" else 1
                            p.dma("sp", None,
                                  apdbg[:, (s_ * 2 + kk_) * 512:
                                        (s_ * 2 + kk_ + 1) * 512],
                                  AR[0], AR[0].ap, "dbg")
                            p.dma("sp", None,
                                  cbdbg[:, (s_ * 2 + kk_) * 512:
                                        (s_ * 2 + kk_ + 1) * 512],
                                  cbuf[0], cbuf[0].ap, "dbg")
                            p.dma("sp", None,
                                  tadbg[:, (s_ * 2 + kk_) * 512:
                                        (s_ * 2 + kk_ + 1) * 512],
                                  tA, tA.ap, "dbg")
                            p.dma("sp", None,
                                  rbdbg[:, (s_ * 2 + kk_) * 512:
                                        (s_ * 2 + kk_ + 1) * 512],
                                  rbA, rbA.ap, "dbg")
                        p.op("dve", lambda e: e.tensor_mul(u1.ap, tB.ap,
                                                           cbuf[0].ap),
                             reads=[tB, cbuf[0]], writes=[u1])
                        p.op("dve", lambda e: e.tensor_mul(u2.ap, tA.ap,
                                                           cbuf[1].ap),
                             reads=[tA, cbuf[1]], writes=[u2])
                        p.op("dve", lambda e, g=g: e.tensor_add(
                            BR[g].ap, u1.ap, u2.ap), reads=[u1, u2],
                            writes=[BR[g]])
                        for u in range(2):
                            p.op("pe", lambda e, g=g, u=u: e.matmul(
                                pR.ap, cpm[:, u * 128:(u + 1) * 128], AR[g].ap,
                                start=True, stop=False),
                                reads=[cpm, AR[g]], writes=[pR])
                            p.op("pe", lambda e, g=g, u=u: e.matmul(
                                pR.ap, cpm[:, 256 + u * 128:256 + (u + 1) * 128],
                                BR[g].ap, start=False, stop=True),
                                reads=[cpm, BR[g]], writes=[pR], acc=True)
                            p.op("dve", lambda e, HT=HT, g=g, u=u, t0=t0:
                                 e.tensor_copy(HT[2 * g + u][:, t0:t0 + 512],
                                               pR.ap),
                                 reads=[pR], writes=[HT[2 * g + u]])
            p.barrier()
        if dbg:
            for g in range(4):
                p.dma("sp", None, qdbg[g * 128:(g + 1) * 128, :], QH[g],
                      QH[g].ap, "dbg")
                p.dma("sp", None, kdbg[g * 128:(g + 1) * 128, :], KH[g],
                      KH[g].ap, "dbg")
            for t in range(NQT):
                p.dma("sp", None, vdbg[t * 128:(t + 1) * 128, :], V[t],
                      V[t].ap, "dbg")
            p.wait_all("sp", "dbg")
            p.barrier()

        if phases < 2:
            attn_stack.close()
            blk = stack.enter_context(nc.Block())
            p.play(blk)
            return nc

        with contextlib.ExitStack() as st:
            KW = sbuf(p, st, "KW", [128, 4 * NQT], F32)
            QM = sbuf(p, st, "QM", [128, 4 * NQT], F32)
            prod = sbuf(p, st, "prod", [128, NQT], BF16)
            negM = sbuf(p, st, "negM", [8, NQT], F32)
            pM8 = psum(p, st, "pM8", [8, NQT])
            pMb = psum(p, st, "pMb", [128, 8 * NQT])
            for g in range(4):
                p.op("dve", lambda e, g=g: e.tensor_reduce(
                    KW[:, g * NQT:(g + 1) * NQT],
                    KH[g].t[:].rearrange("p (m q) -> p m q", m=NQT),
                    AX.X, OP.max, apply_absolute_value=True),
                    reads=[KH[g]], writes=[KW])
                p.op("dve", lambda e, g=g: e.tensor_reduce(
                    QM[:, g * NQT:(g + 1) * NQT],
                    QH[g].t[:].rearrange("p (m q) -> p m q", m=NQT),
                    AX.X, OP.max, apply_absolute_value=True),
                    reads=[QH[g]], writes=[QM])
                for dd in range(1, 5):
                    p.op("dve", lambda e, g=g, dd=dd: e.tensor_max(
                        KW[:, g * NQT + dd:(g + 1) * NQT],
                        KW[:, g * NQT + dd:(g + 1) * NQT],
                        KW[:, g * NQT:(g + 1) * NQT - dd]),
                        reads=[KW], writes=[KW])
            for g in range(4):
                p.op("dve", lambda e, g=g: e.tensor_mul(
                    prod.ap, QM[:, g * NQT:(g + 1) * NQT],
                    KW[:, g * NQT:(g + 1) * NQT]),
                    reads=[QM, KW], writes=[prod])
                p.op("pe", lambda e, g=g: e.matmul(
                    pM8.ap, ch64[:, 8 * g:8 * g + 8], prod.ap,
                    start=(g == 0), stop=(g == 3)),
                    reads=[ch64, prod], writes=[pM8], acc=(g > 0))
            p.op("dve", lambda e: e.tensor_scalar(
                negM.ap, pM8.ap, -1.0, 68.0, OP.mult, OP.add),
                reads=[pM8], writes=[negM])
            for h in range(8):
                p.op("pe", lambda e, h=h: e.matmul(
                    pMb[:, NQT * h:NQT * (h + 1)], cb[:, 128 * h:128 * (h + 1)],
                    negM.ap, start=True, stop=True),
                    reads=[cb, negM], writes=[pMb], acc=(h > 0))
            p.op("dve", lambda e: e.tensor_copy(Mb.ap, pMb.ap),
                 reads=[pMb], writes=[Mb])
            if dbg:
                p.dma("sp", None, mdbg[:, :], Mb, Mb.ap, "dbg")
                p.wait_all("sp", "dbg")
            p.barrier()

        if phases < 3:
            attn_stack.close()
            blk = stack.enter_context(nc.Block())
            p.play(blk)
            return nc

        OC = [sbuf(p, attn_stack, f"OC{c}", [128, EXT], BF16) for c in range(4)]
        with contextlib.ExitStack() as st:
            pS = psum(p, st, "pS", [128, 4 * 512])
            pZ = psum(p, st, "pZ", [1, 512])
            pO = [psum(p, st, f"pO{i}", [128, 256]) for i in range(2)]
            pB = psum(p, st, "pB", [128, 128])
            PH = sbuf(p, st, "PH", [128, 5 * 1024], BF16)
            mbuf = sbuf(p, st, "mbuf", [128, 5 * 128], BF16)
            rz0 = sbuf(p, st, "rz0", [1, 512], F32)
            rz1 = sbuf(p, st, "rz1", [1, 512], F32)
            rbb = sbuf(p, st, "rbb", [128, 128], F32)
            for m in range(min(NQT, p3m)):
                kbs = list(range(max(0, m - 4), m + 1))
                nk = len(kbs)
                q0 = m * 128
                p.dma("sp", mbuf, mbuf[:, 0:nk * 128], None,
                      masks[:, (m * 5) * 128:(m * 5 + nk) * 128], "mld")
                for g2 in range(2):
                    for ci, lo in enumerate(range(0, nk, 4)):
                        hi = min(nk, lo + 4)
                        for kb_i in range(lo, hi):
                            kb = kbs[kb_i]
                            k0 = kb * 128
                            for hp in range(4):
                                h = 4 * g2 + hp
                                ch, off = h // 2, 64 * (h % 2)
                                p.op("pe", lambda e, ch=ch, off=off, k0=k0,
                                     q0=q0, kb_i=kb_i, lo=lo, hp=hp: e.matmul(
                                         pS[:, hp * 512 + (kb_i - lo) * 128:
                                            hp * 512 + (kb_i - lo) * 128 + 128],
                                         KH[ch][off:off + 64, k0:k0 + 128],
                                         QH[ch][off:off + 64, q0:q0 + 128],
                                         start=True, stop=True),
                                     reads=[KH[ch], QH[ch]], writes=[pS])
                        if not (p3bits & 1):
                            continue
                        for hp in range(4):
                            h = 4 * g2 + hp
                            col = h * NQT + m
                            nkb = hi - lo
                            src = bass.AP(pS.t, hp * 512,
                                          [[4 * 512, 128], [128, nkb], [1, 128]])
                            dst = bass.AP(PH.t, lo * 1024 + g2 * 512 + hp * 128,
                                          [[5 * 1024, 128], [1024, nkb], [1, 128]])
                            p.op("act", lambda e, src=src, dst=dst, col=col:
                                 e.activation(dst, src, AF.Exp,
                                              bias=Mb[:, col:col + 1]),
                                 reads=[pS, Mb], writes=[PH])
                if p3bits & 2:
                    for kb_i in range(nk):
                        phv = PH.t[:, kb_i * 1024:(kb_i + 1) * 1024].rearrange(
                            "p (h q) -> p h q", h=8)
                        p.op("dve", lambda e, phv=phv, kb_i=kb_i: e.tensor_mul(
                            phv, phv,
                            mbuf[:, kb_i * 128:(kb_i + 1) * 128].unsqueeze(1)
                            .to_broadcast([128, 8, 128])),
                            reads=[mbuf, PH], writes=[PH])
                for g2 in range(2):
                    if not (p3bits & 4):
                        break
                    rzb = rz0 if g2 == 0 else rz1
                    for hp in range(4):
                        base = g2 * 512 + hp * 128
                        for kb_i in range(nk):
                            off = kb_i * 1024 + base
                            p.op("pe", lambda e, off=off, hp=hp, kb_i=kb_i,
                                 nk=nk: e.matmul(
                                     pZ[:, 128 * hp:128 * (hp + 1)],
                                     onesc.ap, PH[:, off:off + 128],
                                     start=(kb_i == 0), stop=(kb_i == nk - 1)),
                                 reads=[PH, onesc], writes=[pZ], acc=(kb_i > 0))
                    p.op("dve", lambda e, rzb=rzb: e.tensor_scalar(
                        rzb.ap, pZ.ap, 1e-36, None, OP.add),
                        reads=[pZ], writes=[rzb])
                    p.op("dve", lambda e, rzb=rzb: e.reciprocal(rzb.ap, rzb.ap),
                         reads=[rzb], writes=[rzb])
                for g2 in range(2):
                    if not (p3bits & 8):
                        break
                    po = pO[g2]
                    for hp in range(4):
                        h = 4 * g2 + hp
                        for kb_i, kb in enumerate(kbs):
                            p.op("pe", lambda e, h=h, hp=hp, kb=kb, kb_i=kb_i,
                                 g2=g2, po=po, nk=nk: e.matmul(
                                     po[64 * (hp % 2):64 * (hp % 2) + 64,
                                        128 * (hp // 2):128 * (hp // 2) + 128],
                                     V[kb][:, 64 * h:64 * h + 64],
                                     PH[:, kb_i * 1024 + g2 * 512 + hp * 128:
                                        kb_i * 1024 + g2 * 512 + hp * 128 + 128],
                                     start=(kb_i == 0), stop=(kb_i == nk - 1)),
                                 reads=[V[kb], PH], writes=[po], acc=(kb_i > 0))
                    rzb = rz0 if g2 == 0 else rz1
                    for pr in range(2):
                        if not (p3bits & 16):
                            break
                        p.op("pe", lambda e, pr=pr, rzb=rzb: e.matmul(
                            pB[0:64, :], onesr[:, 0:64],
                            rzb[:, 128 * (2 * pr):128 * (2 * pr) + 128],
                            start=True, stop=True), reads=[onesr, rzb],
                            writes=[pB])
                        p.op("pe", lambda e, pr=pr, rzb=rzb: e.matmul(
                            pB[64:128, :], onesr[:, 0:64],
                            rzb[:, 128 * (2 * pr + 1):128 * (2 * pr + 1) + 128],
                            start=True, stop=True, tile_position=(0, 64)),
                            reads=[onesr, rzb], writes=[pB], acc=True)
                        p.op("dve", lambda e: e.tensor_copy(rbb.ap, pB.ap),
                             reads=[pB], writes=[rbb])
                        c = 2 * g2 + pr
                        p.op("dve", lambda e, c=c, q0=q0, po=po, pr=pr:
                             e.tensor_mul(OC[c][:, q0:q0 + 128],
                                          po[:, 128 * pr:128 * (pr + 1)], rbb.ap),
                             reads=[po, rbb], writes=[OC[c]])
            p.barrier()
            if dbg:
                for c in range(4):
                    p.dma("sp", None, odbg[c * 128:(c + 1) * 128, :], OC[c],
                          OC[c].ap, "dbg")
                p.wait_all("sp", "dbg")
                p.barrier()
        if p3wo:
            with contextlib.ExitStack() as st:
                wob = [sbuf(p, st, f"wob{i}", [128, D], BF16) for i in range(4)]
                ysl = sbuf(p, st, "ysl", [128, 512], F32)
                xms = sbuf(p, st, "xms", [128, 512], F32)
                pX = [psum(p, st, f"pX{i}", [128, 512]) for i in range(2)]
                zc = sbuf(p, st, "zc", [128, 1], F32)
                p.op("dve", lambda e: e.memset(zc.ap, 0.0), writes=[zc])
                for i in range(4):
                    p.dma("pool", wob[i], wob[i].ap, None,
                          wo[i * 128:(i + 1) * 128, :], "wld")
                for c in range(NCHUNK):
                    p.dma("sp", None, xmid[c * 128:(c + 1) * 128, 0:1], zc, zc.ap,
                          "xms0", allow_slow_non_contiguous=True)
                p.barrier()
                for s_ in range(NS):
                    t0 = s_ * 512
                    for oc in range(NCHUNK):
                        px = pX[oc % 2]
                        for ic in range(4):
                            p.op("pe", lambda e, ic=ic, oc=oc, t0=t0, px=px:
                                 e.matmul(px.ap,
                                          wob[ic][:, oc * 128:(oc + 1) * 128],
                                          OC[ic][:, t0:t0 + 512],
                                          start=(ic == 0), stop=(ic == 3)),
                                 reads=[wob[ic], OC[ic]], writes=[px],
                                 acc=(ic > 0))
                        p.dma("sp", ysl, ysl.ap, None,
                              y_in[oc * 128:(oc + 1) * 128, t0 + 1:t0 + 513],
                              "yld2")
                        p.op("dve", lambda e, px=px: e.tensor_add(
                            xms.ap, px.ap, ysl.ap), reads=[px, ysl], writes=[xms])
                        p.dma("sp", None, xmid[oc * 128:(oc + 1) * 128,
                                               t0 + 1:t0 + 513], xms, xms.ap,
                              "xmst")
                p.wait_all("sp", "xmst")
                p.barrier()
        attn_stack.close()

        if phases < 4:
            blk = stack.enter_context(nc.Block())
            p.play(blk)
            return nc
        lngc = sbuf(p, stack, "lngc", [128, NCHUNK], F32)
        lnbc = sbuf(p, stack, "lnbc", [128, NCHUNK], F32)
        for c in range(NCHUNK):
            p.dma("sp", lngc, lngc[:, c:c + 1], None,
                  lng[c * 128:(c + 1) * 128, :], "cld")
            p.dma("sp", lnbc, lnbc[:, c:c + 1], None,
                  lnb[c * 128:(c + 1) * 128, :], "cld")
        p.barrier()
        emit_ln_stream(p, stack, "fln", xmid, hffn, EXT, consts,
                       col_off=[1] * 4 + [0] * 4, gamma=lngc, beta=lnbc)

        ACH = chunks_of(2730, 128)
        with contextlib.ExitStack() as st:
            hf = [sbuf(p, st, f"hf{j}", [128, 2048], BF16) for j in range(22)]
            hx = [sbuf(p, st, f"hx{i}", [128, 8 * 512], BF16) for i in range(2)]
            w1a = [sbuf(p, st, f"w1a{i}", [128, 8 * 128], BF16) for i in range(2)]
            w1g = [sbuf(p, st, f"w1g{i}", [128, 8 * 128], BF16) for i in range(2)]
            w2b = [sbuf(p, st, f"w2b{i}", [128, 22 * 128], BF16) for i in range(2)]
            b1a = sbuf(p, st, "b1a", [128, 22], F32)
            b1g = sbuf(p, st, "b1g", [128, 22], F32)
            b2c = sbuf(p, st, "b2c", [128, NCHUNK], F32)
            pA = psum(p, st, "pA", [128, 512])
            pG = psum(p, st, "pG", [128, 512])
            pX2 = [psum(p, st, f"pX2{i}", [128, 512]) for i in range(2)]
            gg = sbuf(p, st, "gg", [128, 512], BF16)
            aa = sbuf(p, st, "aa", [128, 512], BF16)
            xms2 = sbuf(p, st, "xms2", [128, 512], F32)
            xout = sbuf(p, st, "xout", [128, 512], F32)
            for j, (a0, sz) in enumerate(ACH):
                p.dma("sp", b1a, b1a[:sz if sz < 128 else 128, j:j + 1], None,
                      b1[a0:a0 + sz, :], "cld", allow_slow_non_contiguous=True)
                p.dma("sp", b1g, b1g[:sz if sz < 128 else 128, j:j + 1], None,
                      b1[2730 + a0:2730 + a0 + sz, :], "cld",
                      allow_slow_non_contiguous=True)
            for c in range(NCHUNK):
                p.dma("sp", b2c, b2c[:, c:c + 1], None,
                      b2[c * 128:(c + 1) * 128, :], "cld")
            p.barrier()
            hxm = sbuf(p, st, "hxm", [128, 8 * 2048], BF16)
            xmb = sbuf(p, st, "xmb", [128, 512], F32)
            for macro in range(2):
                mb0 = macro * 2048
                for c in range(NCHUNK):
                    p.dma("sp", hxm, hxm[:, c * 2048:(c + 1) * 2048], None,
                          hffn[c * 128:(c + 1) * 128, mb0:mb0 + 2048], "hxl")
                for j, (a0, sz) in enumerate(ACH):
                    w1a_ = w1a[j % 2]
                    w1g_ = w1g[j % 2]
                    va = w1a_.t[:, 0:8 * sz].rearrange("p (i o) -> p i o", i=8)
                    p.dma("pool", w1a_, va, None,
                          w1[:, a0:a0 + sz].rearrange("(i p) o -> p i o", p=128),
                          "w1l")
                    p.dma("pool", w1g_, w1g_.t[:, 0:8 * sz].rearrange(
                        "p (i o) -> p i o", i=8), None,
                        w1[:, 2730 + a0:2730 + a0 + sz].rearrange(
                            "(i p) o -> p i o", p=128), "w1l")
                    for tt in range(4):
                        for ic in range(NCHUNK):
                            p.op("pe", lambda e, ic=ic, tt=tt, sz=sz, w1a_=w1a_:
                                 e.matmul(pA[0:sz, :],
                                          w1a_[:, ic * sz:(ic + 1) * sz],
                                          hxm[:, ic * 2048 + tt * 512:
                                              ic * 2048 + (tt + 1) * 512],
                                          start=(ic == 0), stop=(ic == 7)),
                                 reads=[w1a_, hxm], writes=[pA], acc=(ic > 0))
                        for ic in range(NCHUNK):
                            p.op("pe", lambda e, ic=ic, tt=tt, sz=sz, w1g_=w1g_:
                                 e.matmul(pG[0:sz, :],
                                          w1g_[:, ic * sz:(ic + 1) * sz],
                                          hxm[:, ic * 2048 + tt * 512:
                                              ic * 2048 + (tt + 1) * 512],
                                          start=(ic == 0), stop=(ic == 7)),
                                 reads=[w1g_, hxm], writes=[pG], acc=(ic > 0))
                        p.op("act", lambda e, sz=sz, j=j: e.activation(
                            aa[0:sz, :], pA[0:sz, :], AF.Identity,
                            bias=b1a[0:sz, j:j + 1]),
                            reads=[pA, b1a], writes=[aa])
                        p.op("act", lambda e, sz=sz, j=j: e.activation(
                            gg[0:sz, :], pG[0:sz, :], AF.Gelu,
                            bias=b1g[0:sz, j:j + 1]),
                            reads=[pG, b1g], writes=[gg])
                        p.op("dve", lambda e, sz=sz, j=j, tt=tt: e.tensor_mul(
                            hf[j][0:sz, tt * 512:(tt + 1) * 512],
                            aa[0:sz, :], gg[0:sz, :]),
                            reads=[aa, gg], writes=[hf[j]])
                for oc in range(NCHUNK):
                    w2_ = w2b[oc % 2]
                    for j, (a0, sz) in enumerate(ACH):
                        p.dma("pool", w2_, w2_[0:sz, j * 128:(j + 1) * 128], None,
                              w2[a0:a0 + sz, oc * 128:(oc + 1) * 128], "w2l")
                    for tt in range(4):
                        px = pX2[tt % 2]
                        for j, (a0, sz) in enumerate(ACH):
                            p.op("pe", lambda e, j=j, sz=sz, tt=tt, w2_=w2_, px=px:
                                 e.matmul(px.ap,
                                          w2_[0:sz, j * 128:(j + 1) * 128],
                                          hf[j][0:sz, tt * 512:(tt + 1) * 512],
                                          start=(j == 0), stop=(j == 21)),
                                 reads=[w2_, hf[j]], writes=[px], acc=(j > 0))
                        p.op("act", lambda e, oc=oc, px=px: e.activation(
                            xms2.ap, px.ap, AF.Identity, bias=b2c[:, oc:oc + 1]),
                            reads=[px, b2c], writes=[xms2])
                        p.dma("sp", xmb, xmb.ap, None,
                              xmid[oc * 128:(oc + 1) * 128,
                                   mb0 + tt * 512 + 1:mb0 + (tt + 1) * 512 + 1],
                              "xmbl")
                        p.op("dve", lambda e: e.tensor_add(
                            xout.ap, xms2.ap, xmb.ap),
                            reads=[xms2, xmb], writes=[xout])
                        p.dma("sp", None,
                              x_out[oc * 128:(oc + 1) * 128,
                                    mb0 + tt * 512:mb0 + (tt + 1) * 512],
                              xout, xout.ap, "xot")
            p.wait_all("sp", "xot")
            p.wait_all("pool", "xot")
            p.barrier()
        if standalone:
            blk = stack.enter_context(nc.Block())
            p.play(blk)
    return nc


def host_consts_l0():
    hsp = np.zeros((128, 16), np.float32)
    for g in range(2):
        for j in range(4):
            hsp[32 * j:32 * (j + 1), 8 * g + 4 * g + j] = 1.0
    bs4 = np.zeros((8, 256), np.float32)
    for g in range(2):
        for pp in range(128):
            bs4[4 * g + pp // 32, 128 * g + pp] = 1.0
    perm = np.zeros((128, 512), np.float32)
    for u in range(2):
        for j in range(2):
            for n in range(32):
                k = 64 * u + 32 * j + n
                perm[k, 128 * u + 64 * j + n] = 1.0
                perm[k, 256 + 128 * u + 64 * j + 32 + n] = 1.0
    hs64 = np.zeros((128, 32), np.float32)
    for g in range(4):
        for j in range(2):
            hs64[64 * j:64 * (j + 1), 8 * g + 2 * g + j] = 1.0
    return {"c_hsp": hsp, "c_bs4": bs4, "c_perm": perm.astype(BF),
            "c_hs64": hs64.astype(BF)}


def permute_wqkv(w):
    cols = []
    for base in (0, 512):
        for half in (0, 32):
            for h in range(8):
                cols.extend(range(base + h * 64 + half, base + h * 64 + half + 32))
    cols.extend(range(1024, 1536))
    return np.ascontiguousarray(w[:, cols])


def rotary_tables(goff):
    j = np.arange(32, dtype=np.float64)
    inv_freq = 1.0 / 10000.0 ** (2 * j / 64)
    base = (2 * j + 0.4 * 64) / (1.4 * 64)
    t = np.arange(goff, goff + EXT, dtype=np.float64)
    freqs = t[:, None] * inv_freq[None, :]
    power = (t - 2048.0) / 256.0
    scale = base[None, :] ** power[:, None]
    cos, sin = np.cos(freqs), np.sin(freqs)
    c1q = (8.0 * cos * scale).T.astype(np.float32)
    c2q = (8.0 * sin * scale).T.astype(np.float32)
    c1k = (cos / scale).T.astype(np.float32)
    c2k = (sin / scale).T.astype(np.float32)
    tile = lambda a: np.ascontiguousarray(np.tile(a, (4, 1)))
    return tile(c1q), tile(c2q), tile(c1k), tile(c2k)


def l0_masks(goff):
    out = np.zeros((128, NQT * 5 * 128), np.float32)
    for m in range(NQT):
        kbs = list(range(max(0, m - 4), m + 1))
        gq = goff + m * 128 + np.arange(128)[None, :]
        all_garbage = (goff + m * 128 + 127) < 0
        for kb_i, kb in enumerate(kbs):
            gk = goff + kb * 128 + np.arange(128)[:, None]
            if all_garbage:
                mk = np.ones((128, 128), np.float32)
            else:
                mk = ((gq >= gk) & (gq - gk <= 512) & (gk >= 0)).astype(np.float32)
            out[:, (m * 5 + kb_i) * 128:(m * 5 + kb_i + 1) * 128] = mk
    return out.astype(BF)


def qsc_cols(q_scale, k_scale):
    p = np.arange(128) % 32
    return np.ascontiguousarray(np.stack(
        [q_scale[p], q_scale[32 + p], k_scale[p], k_scale[32 + p]],
        axis=1).astype(np.float32))



def prep_mlp_w(WT, n_in_chunks, n_out_blocks):
    tmp = WT.reshape(n_in_chunks, 128, n_out_blocks, 128)
    return np.ascontiguousarray(
        tmp.transpose(2, 1, 0, 3).reshape(n_out_blocks * 128, n_in_chunks * 128))





_BUILD_CACHE = {}


def _get(name, fn):
    if name not in _BUILD_CACHE:
        _BUILD_CACHE[name] = fn()
    return _BUILD_CACHE[name]


def _run(nc, maps):
    return run_bass_kernel_spmd(nc, maps, core_ids=list(range(8)))


def kernel_multilaunch(**inp):
    consts = host_consts_s2()
    cl0 = host_consts_l0()
    c5 = host_consts_s5()
    f32 = np.float32
    C = lambda a: np.ascontiguousarray(np.asarray(a, f32))
    col = lambda a: np.ascontiguousarray(np.asarray(a, f32).reshape(-1, 1))

    x = np.asarray(inp["L0_attn_x"], f32)
    xfm = [np.ascontiguousarray(fm(x[b])) for b in range(B)]

    rot = {}
    for hh in range(2):
        goff = -2048 if hh == 0 else 0
        rot[hh] = (KK_rotary := rotary_tables(goff), l0_masks(goff))
    ident = np.eye(128).astype(BF)

    def l0ffn_sublayer():
        nc_ln = _get("l0ln", build_l0ln)
        maps = []
        for c in range(8):
            b, hh = c // 2, c % 2
            if hh == 0:
                x_ext = np.concatenate(
                    [np.zeros((D, 2048), f32), xfm[b][:, 0:2048]], axis=1)
            else:
                x_ext = xfm[b]
            maps.append({"x_ext": C(x_ext), "c_stats": consts["c_stats"],
                         "c_bsel": consts["c_bsel"]})
        res = _run(nc_ln, maps)
        ycur = [np.array(res.results[c]["y0"]) for c in range(8)]
        for c in range(8):
            if c % 2 == 0:
                ycur[c][:, 0:2049] = 0.0
        nc_in = _get("l0inner", build_l0inner)
        for l in range(2):
            maps = []
            for c in range(8):
                b, hh = c // 2, c % 2
                (c1q, c2q, c1k, c2k), msk = rot[hh]
                maps.append({
                    "y_in": ycur[c],
                    "wqkvp": permute_wqkv(C(inp["L0_ffn_wqkv"][l])),
                    "qsc": qsc_cols(C(inp["L0_ffn_q_scale"][l]),
                                    C(inp["L0_ffn_k_scale"][l])),
                    "c1q": c1q, "c2q": c2q, "c1k": c1k, "c2k": c2k,
                    "wo": C(inp["L0_ffn_wo"][l]),
                    "lng": col(inp["L0_ffn_ln_g"][l]),
                    "lnb": col(inp["L0_ffn_ln_b"][l]),
                    "w1": C(inp["L0_ffn_w1"][l]),
                    "b1": col(inp["L0_ffn_b1"][l]),
                    "w2": C(inp["L0_ffn_w2"][l]),
                    "b2": col(inp["L0_ffn_b2"][l]),
                    "masks": msk,
                    "c_stats": consts["c_stats"], "c_bsel": consts["c_bsel"],
                    "c_hsp": cl0["c_hsp"], "c_bs4": cl0["c_bs4"],
                    "c_perm": cl0["c_perm"], "c_hs64": cl0["c_hs64"],
                })
            res = _run(nc_in, maps)
            ynext = []
            for c in range(8):
                yn = np.zeros((D, EXT + 1), f32)
                yn[:, 1:] = res.results[c]["x_out"]
                if c % 2 == 0:
                    yn[:, 0:2049] = 0.0
                ynext.append(yn)
            ycur = ynext
        nc_cb = _get("l0comb", build_l0comb)
        maps = []
        for c in range(8):
            b, hh = c // 2, c % 2
            tc = slice(T * hh, T * (hh + 1))
            maps.append({"x_own": C(xfm[b][:, tc]),
                         "y2": C(ycur[c][:, 2049:4097])})
        res = _run(nc_cb, maps)
        for c in range(8):
            b, hh = c // 2, c % 2
            xfm[b][:, T * hh:T * (hh + 1)] = res.results[c]["x_new"]

    def l0attn_sublayer():
        nc_a = _get("s2a", build_s2a)
        maps = []
        for c in range(8):
            b, hh = c // 2, c % 2
            cols = slice(512 * hh, 512 * (hh + 1))
            maps.append({
                "x_full": C(xfm[b]),
                "wq": C(inp["L0_attn_W_q"][:, cols]),
                "wk": C(inp["L0_attn_W_k"][:, cols]),
                "wv": C(inp["L0_attn_W_v"][:, cols]),
                "bq": col(inp["L0_attn_b_q"][cols]),
                "bk": col(inp["L0_attn_b_k"][cols]),
                "bv": col(inp["L0_attn_b_v"][cols]),
                "c_stats": consts["c_stats"], "c_bsel": consts["c_bsel"],
                "c_hsel": consts["c_hsel"], "c_dmask": consts["c_dmask"],
            })
        res_a = _run(nc_a, maps)
        nc_b = _get("s2b", build_s2b)
        maps = []
        for c in range(8):
            b, hh = c // 2, c % 2
            o_pair = np.concatenate([res_a.results[2 * b]["o_un"],
                                     res_a.results[2 * b + 1]["o_un"]], axis=0)
            zr_pair = np.concatenate([res_a.results[2 * b]["o_zr"],
                                      res_a.results[2 * b + 1]["o_zr"]], axis=0)
            tc = slice(T * hh, T * (hh + 1))
            maps.append({
                "x_own": C(xfm[b][:, tc]),
                "o_sl": np.ascontiguousarray(o_pair[:, tc]),
                "zr": np.ascontiguousarray(zr_pair[:, tc]),
                "wo": C(inp["L0_attn_W_o"]), "bo": col(inp["L0_attn_b_o"]),
                "c_zsel": consts["c_zsel"],
            })
        res = _run(nc_b, maps)
        for c in range(8):
            b, hh = c // 2, c % 2
            xfm[b][:, T * hh:T * (hh + 1)] = res.results[c]["x_new"]

    def l1ffn_sublayer():
        nc = _get("s4", build_s4)
        keysx = np.zeros((512, 128), f32)
        keys = np.asarray(inp["L1_ffn_keys"], f32)
        for pl in range(2):
            for hh in range(4):
                bidx = 4 * pl + hh
                keysx[64 * bidx:64 * (bidx + 1)] = keys[hh, :, pl, :].T
        tabc = np.ascontiguousarray(np.concatenate(
            [np.asarray(inp["L1_ffn_down_embed"], f32),
             np.asarray(inp["L1_ffn_up_embed"], f32)], axis=1))
        wgT = prep_mlp_w(C(np.asarray(inp["L1_ffn_Wg"], f32).T), 8, 32)
        wuT = prep_mlp_w(C(np.asarray(inp["L1_ffn_Wu"], f32).T), 8, 32)
        wdT = prep_mlp_w(C(np.asarray(inp["L1_ffn_Wd"], f32).T), 32, 8)
        maps = []
        for c in range(8):
            b, hh = c // 2, c % 2
            tc = slice(T * hh, T * (hh + 1))
            maps.append({
                "x_own": C(xfm[b][:, tc]),
                "wqT": C(np.asarray(inp["L1_ffn_Wq"], f32).T), "keysx": keysx,
                "wgT": wgT, "wuT": wuT, "wdT": wdT, "tabc": tabc,
                "c_stats": consts["c_stats"], "c_bsel": consts["c_bsel"],
                "c_ident": ident,
            })
        res = _run(nc, maps)
        for c in range(8):
            b, hh = c // 2, c % 2
            xfm[b][:, T * hh:T * (hh + 1)] = res.results[c]["x_new"]

    def l1attn_sublayer():
        nc_a = _get("s5a", build_s5a)
        maps = []
        for c in range(8):
            b, hh = c // 2, c % 2
            t0 = T * hh
            x5 = np.zeros((D, T + 1), f32)
            x5[:, 1:] = xfm[b][:, t0:t0 + T]
            if hh == 1:
                x5[:, 0] = xfm[b][:, t0 - 1]
            maps.append({
                "x5": x5,
                "cwv": C(np.asarray(inp["L1_attn_conv_w"], f32)[:, 0, :]),
                "cbv": col(inp["L1_attn_conv_b"]),
                "kwT": C(np.asarray(inp["L1_attn_k_w"], f32).T),
                "kbv": col(inp["L1_attn_k_b"]),
                "vwT": C(np.asarray(inp["L1_attn_v_w"], f32).T),
                "vbv": col(inp["L1_attn_v_b"]),
                "c_stats": consts["c_stats"], "c_bsel": consts["c_bsel"],
            })
        res_a = _run(nc_a, maps)
        nc_b = _get("s5b", build_s5b)
        maps = []
        for c in range(8):
            b, hh = c // 2, c % 2
            kf = np.concatenate([res_a.results[2 * b]["o_k"],
                                 res_a.results[2 * b + 1]["o_k"]], axis=1)
            vf = np.concatenate([res_a.results[2 * b]["o_v"],
                                 res_a.results[2 * b + 1]["o_v"]], axis=0)
            tc = slice(T * hh, T * (hh + 1))
            maps.append({
                "x_own": C(xfm[b][:, tc]),
                "lnx": res_a.results[c]["o_lnx"],
                "kf": np.ascontiguousarray(kf), "vf": np.ascontiguousarray(vf),
                "qwT": C(np.asarray(inp["L1_attn_q_w"], f32).T),
                "qbv": col(inp["L1_attn_q_b"]),
                "owT": C(np.asarray(inp["L1_attn_out_w"], f32).T),
                "obv": col(inp["L1_attn_out_b"]),
                "c_hs16": c5["c_hs16"], "c_bs16": c5["c_bs16"],
            })
        res = _run(nc_b, maps)
        for c in range(8):
            b, hh = c // 2, c % 2
            xfm[b][:, T * hh:T * (hh + 1)] = res.results[c]["x_new"]

    l0ffn_sublayer()
    l0attn_sublayer()
    l0ffn_sublayer()
    l1ffn_sublayer()
    l1attn_sublayer()
    l1ffn_sublayer()

    out = np.stack([xfm[b].T for b in range(B)], axis=0)
    return np.ascontiguousarray(out.astype(np.float32))


def _ag(p, src_ap, dst_ap, srcb, dstb):
    p.op("pool", lambda e: e.collective_compute(
        "AllGather", OP.bypass, replica_groups=RG_PAIRS,
        ins=[src_ap.opt()], outs=[dst_ap.opt()]),
        reads=[srcb] if srcb else [], writes=[dstb], inc_sem="coll", inc=1)


def emit_pick_half(p, stack, dst, h0ap, h1ap, mcol, nrows, width, dt,
                   srcbufs=()):
    with contextlib.ExitStack() as st:
        a = sbuf(p, st, "ph_a", [128, width], dt)
        b2 = sbuf(p, st, "ph_b", [128, width], dt)
        u = sbuf(p, st, "ph_u", [128, width], F32)
        o = sbuf(p, st, "ph_o", [128, width], dt)
        nck = ceil_div(nrows, 128)
        for c in range(nck):
            rr = min(128, nrows - c * 128)
            p.dma("sp", a, a[0:rr, :], None, h0ap(c, rr), "phl")
            p.dma("sp", b2, b2[0:rr, :], None, h1ap(c, rr), "phl")
            p.op("dve", lambda e, rr=rr: e.tensor_scalar(
                u[0:rr, :], a[0:rr, :], mcol[0:rr, 0:1], None, OP.mult),
                reads=[a, mcol], writes=[u])
            p.op("dve", lambda e, rr=rr: e.scalar_tensor_tensor(
                o[0:rr, :], b2[0:rr, :], mcol[0:rr, 1:2], u[0:rr, :],
                OP.mult, OP.add), reads=[b2, mcol, u], writes=[o])
            p.dma("sp", None, dst[c * 128:c * 128 + rr, :], o, o[0:rr, :], "phs")
        p.wait_all("sp", "phs")
        p.barrier()


def build_fused():
    nc = bass.Bass("TRN2", target_bir_lowering=False)
    with contextlib.ExitStack() as ostack:
        p = Prog(nc, ostack)
        p.fconst(0.0)
        ctx = _Ctx(nc, p)
        x_in = Buf(nc.dram_tensor("x_own", [D, T], F32, kind="ExternalInput"))
        c_half = Buf(nc.dram_tensor("c_half", [128, 2], F32, kind="ExternalInput"))
        x_final = Buf(nc.dram_tensor("x_final", [D, T], F32, kind="ExternalOutput"))
        xst = Buf(nc.dram_tensor("xst", [D, T], F32))
        agx = Buf(nc.dram_tensor("agx", [2 * D, T], F32))
        xf = Buf(nc.dram_tensor("xf", [D, S], F32))
        mcol = sbuf(p, ostack, "mcol", [128, 2], F32)
        zc8 = sbuf(p, ostack, "zc8", [128, 1], F32)
        p.dma("sp", mcol, mcol.ap, None, c_half[:, :], "cini")
        p.op("dve", lambda e: e.memset(zc8.ap, 0.0), writes=[zc8])
        p.dma("sp", xst, xst[:, :], x_in, x_in[:, :], "cini")
        p.barrier()

        def refresh_xf():
            _ag(p, xst.t[:, :], agx.t[:, :], xst, agx)
            p.wait_all("pool", "coll")
            p.barrier()
            p.dma("sp", xf, xf[0:D, 0:T], agx, agx[0:D, :], "xfc")
            p.dma("sp", xf, xf[0:D, T:S], agx, agx[D:2 * D, :], "xfc")
            p.wait_all("sp", "xfc")
            p.barrier()

        def zero_col0(tens):
            for c in range(NCHUNK):
                p.dma("sp", None, tens[c * 128:(c + 1) * 128, 0:1], zc8,
                      zc8.ap, "zc0", allow_slow_non_contiguous=True)
            p.wait_all("sp", "zc0")
            p.barrier()

        def l0ffn(tag):
            refresh_xf()
            y0 = Buf(nc.dram_tensor(_uq(tag + "y0"), [D, EXT + 1], F32))
            y1 = Buf(nc.dram_tensor(_uq(tag + "y1"), [D, EXT + 1], F32))
            y2 = Buf(nc.dram_tensor(_uq(tag + "y2"), [D, EXT + 1], F32))
            y2own = Buf(nc.dram_tensor(_uq(tag + "y2o"), [D, T], F32))
            build_l0ln(ctx, pre=tag + "ln_", io={"x_ext": xf, "y0": y0})
            p.barrier()
            for l, yi, yo in ((0, y0, y1), (1, y1, y2)):
                zero_col0(yo)
                build_l0inner(ctx, pre=f"{tag}i{l}_",
                              io={"y_in": yi,
                                  "x_out": Buf(yo.t[:, 1:EXT + 1], "xoutv")})
                p.barrier()
            emit_pick_half(p, ostack, y2own.t,
                           lambda c, rr: y2[c * 128:c * 128 + rr, 1:T + 1],
                           lambda c, rr: y2[c * 128:c * 128 + rr, T + 1:S + 1],
                           mcol, D, T, F32, srcbufs=(y2,))
            build_l0comb(ctx, pre=tag + "cb_",
                         io={"x_own": xst, "y2": y2own, "x_new": xst})
            p.barrier()

        def l0attn(tag="s2_"):
            refresh_xf()
            o_un = Buf(nc.dram_tensor(_uq(tag + "oun"), [512, S], BF16))
            o_zr = Buf(nc.dram_tensor(_uq(tag + "ozr"), [8, S], F32))
            ago = Buf(nc.dram_tensor(_uq(tag + "ago"), [1024, S], BF16))
            agz = Buf(nc.dram_tensor(_uq(tag + "agz"), [16, S], F32))
            osl = Buf(nc.dram_tensor(_uq(tag + "osl"), [D, T], BF16))
            zsl = Buf(nc.dram_tensor(_uq(tag + "zsl"), [16, T], F32))
            build_s2a(ctx, pre=tag, io={"x_full": xf, "o_un": o_un,
                                        "o_zr": o_zr})
            p.barrier()
            _ag(p, o_un.t[:, :], ago.t[:, :], o_un, ago)
            _ag(p, o_zr.t[:, :], agz.t[:, :], o_zr, agz)
            p.wait_all("pool", "coll")
            p.barrier()
            emit_pick_half(p, ostack, osl.t,
                           lambda c, rr: ago[c * 128:c * 128 + rr, 0:T],
                           lambda c, rr: ago[c * 128:c * 128 + rr, T:S],
                           mcol, 1024, T, BF16, srcbufs=(ago,))
            emit_pick_half(p, ostack, zsl.t,
                           lambda c, rr: agz[c * 16 * 0:rr, 0:T] if False else
                           agz[0:rr, 0:T],
                           lambda c, rr: agz[0:rr, T:S],
                           mcol, 16, T, F32, srcbufs=(agz,))
            build_s2b(ctx, pre=tag, io={"x_own": xst, "o_sl": osl, "zr": zsl,
                                        "x_new": xst})
            p.barrier()

        s4share = {}

        def l1ffn(tag, final=False):
            io = {"x_own": xst, "x_new": x_final if final else xst}
            io.update(s4share)
            build_s4(ctx, pre=tag, io=io)
            for k in ("tabb", "wgbd", "wubd", "wdbd"):
                s4share[k] = io[k]
            p.barrier()

        def l1attn(tag="s5_"):
            x5b = Buf(nc.dram_tensor(_uq(tag + "x5"), [D, T + 1], F32))
            agp = Buf(nc.dram_tensor(_uq(tag + "agp"), [2 * D, 1], F32))
            o_k = Buf(nc.dram_tensor(_uq(tag + "ok"), [D, 1024], BF16))
            o_v = Buf(nc.dram_tensor(_uq(tag + "ov"), [1024, D], BF16))
            o_ln = Buf(nc.dram_tensor(_uq(tag + "oln"), [D, T], BF16))
            agk = Buf(nc.dram_tensor(_uq(tag + "agk"), [2 * D, 1024], BF16))
            agv = Buf(nc.dram_tensor(_uq(tag + "agv"), [T, D], BF16))
            kfb = Buf(nc.dram_tensor(_uq(tag + "kf"), [D, T], BF16))
            pcsrc = Buf(nc.dram_tensor(_uq(tag + "pcs"), [D, 1], F32))
            p.dma("sp", pcsrc, pcsrc[:, :], xst, xst[:, T - 1:T], "pcx",
                  allow_slow_non_contiguous=True)
            p.wait_all("sp", "pcx")
            p.barrier()
            _ag(p, pcsrc.t[:, :], agp.t[:, :], pcsrc, agp)
            p.wait_all("pool", "coll")
            p.barrier()
            with contextlib.ExitStack() as st:
                pcol = sbuf(p, st, "pcol", [128, 1], F32)
                pcol2 = sbuf(p, st, "pcol2", [128, 1], F32)
                for c in range(NCHUNK):
                    p.dma("sp", pcol, pcol.ap, agp,
                          agp[c * 128:(c + 1) * 128, 0:1], "pcl",
                          allow_slow_non_contiguous=True)
                    p.op("dve", lambda e: e.tensor_scalar(
                        pcol2.ap, pcol.ap, mcol[:, 1:2], None, OP.mult),
                        reads=[pcol, mcol], writes=[pcol2])
                    p.dma("sp", None, x5b[c * 128:(c + 1) * 128, 0:1], pcol2,
                          pcol2.ap, "pcs", allow_slow_non_contiguous=True)
                p.wait_all("sp", "pcs")
            p.dma("sp", x5b, x5b[:, 1:T + 1], xst, xst[:, :], "x5c")
            p.wait_all("sp", "x5c")
            p.barrier()
            build_s5a(ctx, pre=tag, io={"x5": x5b, "o_k": o_k, "o_v": o_v,
                                        "o_lnx": o_ln})
            p.barrier()
            _ag(p, o_k.t[:, :], agk.t[:, :], o_k, agk)
            _ag(p, o_v.t[:, :], agv.t[:, :], o_v, agv)
            p.wait_all("pool", "coll")
            p.barrier()
            p.dma("sp", kfb, kfb[0:D, 0:1024], agk, agk[0:D, :], "kfc")
            p.dma("sp", kfb, kfb[0:D, 1024:T], agk, agk[D:2 * D, :], "kfc")
            p.wait_all("sp", "kfc")
            p.barrier()
            build_s5b(ctx, pre=tag, io={"x_own": xst, "lnx": o_ln, "kf": kfb,
                                        "vf": agv, "x_new": xst})
            p.barrier()

        l0ffn("s1_")
        l0attn("s2_")
        l0ffn("s3_")
        l1ffn("s4_")
        l1attn("s5_")
        l1ffn("s6_", final=True)
        p.barrier()
        blk = ostack.enter_context(nc.Block())
        p.play(blk)
    return nc


def kernel(**inp):
    consts = host_consts_s2()
    cl0 = host_consts_l0()
    c5 = host_consts_s5()
    f32 = np.float32
    C = lambda a: np.ascontiguousarray(np.asarray(a, f32))
    col = lambda a: np.ascontiguousarray(np.asarray(a, f32).reshape(-1, 1))
    x = np.asarray(inp["L0_attn_x"], f32)
    c1q, c2q, c1k, c2k = rotary_tables(0)
    msk0 = l0_masks(0)
    ident = np.eye(128).astype(BF)

    keysx = np.zeros((512, 128), f32)
    keys = np.asarray(inp["L1_ffn_keys"], f32)
    for pl in range(2):
        for hh in range(4):
            bidx = 4 * pl + hh
            keysx[64 * bidx:64 * (bidx + 1)] = keys[hh, :, pl, :].T
    tabc = np.ascontiguousarray(np.concatenate(
        [np.asarray(inp["L1_ffn_down_embed"], f32),
         np.asarray(inp["L1_ffn_up_embed"], f32)], axis=1))
    wgT = prep_mlp_w(C(np.asarray(inp["L1_ffn_Wg"], f32).T), 8, 32)
    wuT = prep_mlp_w(C(np.asarray(inp["L1_ffn_Wu"], f32).T), 8, 32)
    wdT = prep_mlp_w(C(np.asarray(inp["L1_ffn_Wd"], f32).T), 32, 8)

    common = {}
    for tag in ("s1_", "s3_"):
        common[tag + "ln_c_stats"] = consts["c_stats"]
        common[tag + "ln_c_bsel"] = consts["c_bsel"]
        for l in range(2):
            pre = f"{tag}i{l}_"
            common[pre + "wqkvp"] = permute_wqkv(C(inp["L0_ffn_wqkv"][l]))
            common[pre + "qsc"] = qsc_cols(C(inp["L0_ffn_q_scale"][l]),
                                           C(inp["L0_ffn_k_scale"][l]))
            common[pre + "c1q"] = c1q
            common[pre + "c2q"] = c2q
            common[pre + "c1k"] = c1k
            common[pre + "c2k"] = c2k
            common[pre + "wo"] = C(inp["L0_ffn_wo"][l])
            common[pre + "lng"] = col(inp["L0_ffn_ln_g"][l])
            common[pre + "lnb"] = col(inp["L0_ffn_ln_b"][l])
            common[pre + "w1"] = C(inp["L0_ffn_w1"][l])
            common[pre + "b1"] = col(inp["L0_ffn_b1"][l])
            common[pre + "w2"] = C(inp["L0_ffn_w2"][l])
            common[pre + "b2"] = col(inp["L0_ffn_b2"][l])
            common[pre + "masks"] = msk0
            common[pre + "c_stats"] = consts["c_stats"]
            common[pre + "c_bsel"] = consts["c_bsel"]
            common[pre + "c_hsp"] = cl0["c_hsp"]
            common[pre + "c_bs4"] = cl0["c_bs4"]
            common[pre + "c_perm"] = cl0["c_perm"]
            common[pre + "c_hs64"] = cl0["c_hs64"]
    common["s2_c_stats"] = consts["c_stats"]
    common["s2_c_bsel"] = consts["c_bsel"]
    common["s2_c_hsel"] = consts["c_hsel"]
    common["s2_c_dmask"] = consts["c_dmask"]
    common["s2_wo"] = C(inp["L0_attn_W_o"])
    common["s2_bo"] = col(inp["L0_attn_b_o"])
    common["s2_c_zsel"] = consts["c_zsel"]
    for tag in ("s4_", "s6_"):
        common[tag + "wqT"] = C(np.asarray(inp["L1_ffn_Wq"], f32).T)
        common[tag + "keysx"] = keysx
        common[tag + "c_stats"] = consts["c_stats"]
        common[tag + "c_bsel"] = consts["c_bsel"]
        common[tag + "c_ident"] = ident
    common["s4_wgT"] = wgT
    common["s4_wuT"] = wuT
    common["s4_wdT"] = wdT
    common["s4_tabc"] = tabc
    common["s5_cwv"] = C(np.asarray(inp["L1_attn_conv_w"], f32)[:, 0, :])
    common["s5_cbv"] = col(inp["L1_attn_conv_b"])
    common["s5_kwT"] = C(np.asarray(inp["L1_attn_k_w"], f32).T)
    common["s5_kbv"] = col(inp["L1_attn_k_b"])
    common["s5_vwT"] = C(np.asarray(inp["L1_attn_v_w"], f32).T)
    common["s5_vbv"] = col(inp["L1_attn_v_b"])
    common["s5_c_stats"] = consts["c_stats"]
    common["s5_c_bsel"] = consts["c_bsel"]
    common["s5_qwT"] = C(np.asarray(inp["L1_attn_q_w"], f32).T)
    common["s5_qbv"] = col(inp["L1_attn_q_b"])
    common["s5_owT"] = C(np.asarray(inp["L1_attn_out_w"], f32).T)
    common["s5_obv"] = col(inp["L1_attn_out_b"])
    common["s5_c_hs16"] = c5["c_hs16"]
    common["s5_c_bs16"] = c5["c_bs16"]

    nc = _get("fused", build_fused)
    maps = []
    for c in range(8):
        b, hh = c // 2, c % 2
        m = dict(common)
        xfmb = fm(x[b])
        m["x_own"] = np.ascontiguousarray(xfmb[:, T * hh:T * (hh + 1)])
        ch = np.zeros((128, 2), f32)
        ch[:, 0] = 1.0 - hh
        ch[:, 1] = float(hh)
        m["c_half"] = ch
        cols = slice(512 * hh, 512 * (hh + 1))
        m["s2_wq"] = C(np.asarray(inp["L0_attn_W_q"], f32)[:, cols])
        m["s2_wk"] = C(np.asarray(inp["L0_attn_W_k"], f32)[:, cols])
        m["s2_wv"] = C(np.asarray(inp["L0_attn_W_v"], f32)[:, cols])
        m["s2_bq"] = col(np.asarray(inp["L0_attn_b_q"], f32)[cols])
        m["s2_bk"] = col(np.asarray(inp["L0_attn_b_k"], f32)[cols])
        m["s2_bv"] = col(np.asarray(inp["L0_attn_b_v"], f32)[cols])
        maps.append(m)
    res = run_bass_kernel_spmd(nc, maps, core_ids=list(range(8)))
    out = np.zeros((B, S, D), np.float32)
    for c in range(8):
        b, hh = c // 2, c % 2
        out[b, T * hh:T * (hh + 1)] = res.results[c]["x_final"].T
    return np.ascontiguousarray(out)


# version 06 (final)
# speedup vs baseline: 1.1427x; 1.1427x over previous
import contextlib
import numpy as np
import ml_dtypes

import concourse.bass as bass
import concourse.mybir as mybir
from concourse.alu_op_type import AluOpType
from concourse.bass_utils import run_bass_kernel_spmd

F32 = mybir.dt.float32
BF16 = mybir.dt.bfloat16
U32 = mybir.dt.uint32
AF = mybir.ActivationFunctionType
AX = mybir.AxisListType
OP = AluOpType

BF = ml_dtypes.bfloat16

B, S, D = 4, 4096, 1024
T = 2048
NCHUNK = D // 128
RG_PAIRS = [[0, 1], [2, 3], [4, 5], [6, 7]]


ENGINES = ("pe", "act", "dve", "pool", "sp")


class Buf:
    __slots__ = ("t", "w", "readers", "name")

    def __init__(self, t, name=""):
        self.t = t
        self.w = None
        self.readers = []
        self.name = name

    def __getitem__(self, idx):
        return self.t[idx]

    @property
    def ap(self):
        return self.t[:]


class Prog:
    def __init__(self, nc, stack: contextlib.ExitStack):
        self.nc = nc
        self.stack = stack
        self.cmds = {e: [] for e in ENGINES}
        self.sems = {}
        self.sem_count = {}
        self.watermark = {e: {} for e in ENGINES}
        for e in ("pe", "act", "dve", "pool"):
            self._mk_sem(e)

    def _mk_sem(self, name):
        if name not in self.sems:
            h = self.stack.enter_context(self.nc.semaphore(name))
            self.sems[name] = h
            self.sem_count[name] = 0
        return self.sems[name]

    def dma_sem(self, name):
        return self._mk_sem(name)

    def fconst(self, val):
        if not hasattr(self, "_fc"):
            self._fc = sbuf(self, self.stack, "fcsts", [128, 16], F32)
            self._fc_map = {}
        v = float(val)
        if v not in self._fc_map:
            col = len(self._fc_map)
            assert col < 16
            self._fc_map[v] = col
            self.op("dve", lambda e, col=col, v=v: e.memset(
                self._fc[:, col:col + 1], v), writes=[self._fc])
        col = self._fc_map[v]
        return self._fc[:, col:col + 1], self._fc

    def fconstn(self, val, npart):
        ap, b = self.fconst(val)
        return self._fc[0:npart, self._fc_map[float(val)]:self._fc_map[float(val)] + 1], b

    def _engine_obj(self, e):
        return {"pe": self.nc.tensor, "act": self.nc.scalar, "dve": self.nc.vector,
                "pool": self.nc.gpsimd, "sp": self.nc.sync}[e]

    def _add_wait(self, eng, tok):
        if tok is None:
            return
        sem, val = tok
        wm = self.watermark[eng].get(sem, 0)
        if val <= wm:
            return
        self.watermark[eng][sem] = val
        h = self.sems[sem]
        self.cmds[eng].append(("w", h, val))

    def _deps(self, eng, reads, writes, acc):
        toks = []
        for b in reads:
            if b.w is not None:
                toks.append(b.w)
        for b in writes:
            if not acc and b.w is not None:
                toks.append(b.w)
            toks.extend(b.readers)
        out = []
        for sem, val in toks:
            if eng == "pe" and sem == "pe":
                continue
            out.append((sem, val))
        return out

    def op(self, eng, fn, reads=(), writes=(), acc=False, inc_sem=None, inc=1,
           deps=()):
        sem = inc_sem or eng
        self._mk_sem(sem)
        for tok in deps:
            self._add_wait(eng, tok)
        for tok in self._deps(eng, reads, writes, acc):
            self._add_wait(eng, tok)
        self.sem_count[sem] += inc
        tok = (sem, self.sem_count[sem])
        h = self.sems[sem]
        self.cmds[eng].append(("i", fn, h, inc))
        for b in writes:
            b.w = tok
            b.readers = []
        for b in reads:
            b.readers.append(tok)
        return tok

    NROT = 3
    NPOOL = 20

    def dma(self, queue, out_buf, out_ap, in_buf, in_ap, sem_name, **kw):
        if not hasattr(self, "_dma_uses"):
            self._dma_uses = {}
            self._pool_map = {}
        slot = self._pool_map.setdefault(sem_name, len(self._pool_map) % self.NPOOL)
        k = self._dma_uses.get(sem_name, 0)
        self._dma_uses[sem_name] = k + 1
        sub = f"q{slot}.{k % self.NROT}"
        self._mk_sem(sub)
        prior = self.sem_count[sub]
        if prior:
            self._add_wait(queue, (sub, prior))
        nowaw = kw.pop("nowaw", False)
        reads = [in_buf] if in_buf is not None else []
        writes = [out_buf] if out_buf is not None else []
        if nowaw and out_buf is not None:
            tok_readers = list(out_buf.readers)
            saved_w = out_buf.w
            out_buf.w = None
            t = self.op(queue, lambda e: e.dma_start(out=out_ap, in_=in_ap, **kw),
                        reads=reads, writes=writes, inc_sem=sub, inc=16)
            return t
        return self.op(queue, lambda e: e.dma_start(out=out_ap, in_=in_ap, **kw),
                       reads=reads, writes=writes, inc_sem=sub, inc=16)

    def wait(self, eng, tok):
        self._add_wait(eng, tok)

    def wait_all(self, eng, sem):
        names = [sem]
        if hasattr(self, "_pool_map") and sem in self._pool_map:
            slot = self._pool_map[sem]
            names = [f"q{slot}.{r}" for r in range(self.NROT)]
        for s in names:
            if self.sem_count.get(s):
                self._add_wait(eng, (s, self.sem_count[s]))

    def barrier(self, engines=None):
        engines = engines or ENGINES
        sems = list(self.sem_count.keys())
        for e in engines:
            for s in sems:
                if self.sem_count[s]:
                    self._add_wait(e, (s, self.sem_count[s]))

    def play(self, block):
        for e in ENGINES:
            cmds = self.cmds[e]
            if not cmds:
                continue

            def runner(eng_obj, cmds=cmds):
                for c in cmds:
                    if c[0] == "w":
                        eng_obj.wait_ge(c[1], c[2])
                    else:
                        inst = c[1](eng_obj)
                        if inst is not None:
                            inst.then_inc(c[2], c[3])
                        else:
                            eng_obj.nop().then_inc(c[2], c[3])

            getattr(block, {"pe": "tensor", "act": "scalar", "dve": "vector",
                            "pool": "gpsimd", "sp": "sync"}[e])(runner)



_UNIQ = [0]


def _uq(name):
    _UNIQ[0] += 1
    return f"{name}_u{_UNIQ[0]}"


def sbuf(p, stack, name, shape, dtype):
    t = stack.enter_context(p.nc.sbuf_tensor(_uq(name), shape, dtype))
    return Buf(t, name)


def psum(p, stack, name, shape, dtype=F32):
    t = stack.enter_context(p.nc.psum_tensor(_uq(name), shape, dtype))
    return Buf(t, name)


def dram(p, name, shape, dtype, kind=None):
    if kind:
        t = p.nc.dram_tensor(name, shape, dtype, kind=kind)
    else:
        t = p.nc.dram_tensor(name, shape, dtype)
    return Buf(t, name)


def ceil_div(a, b):
    return (a + b - 1) // b


def chunks_of(n, size):
    out = []
    i = 0
    while i < n:
        out.append((i, min(size, n - i)))
        i += size
    return out



def fm(x):
    return np.ascontiguousarray(np.asarray(x).T)


def as_np(x):
    return np.ascontiguousarray(np.asarray(x))


def _mkio(nc, pre, io):
    def DI(name, shape, dt):
        if name in io:
            return io[name]
        return Buf(nc.dram_tensor(pre + name, shape, dt, kind="ExternalInput"))

    def DO(name, shape, dt):
        if name in io:
            return io[name]
        return Buf(nc.dram_tensor(pre + name, shape, dt, kind="ExternalOutput"))

    def DN(name, shape, dt):
        return Buf(nc.dram_tensor(_uq(pre + name), shape, dt))
    return DI, DO, DN


class _Ctx:
    def __init__(self, nc, p):
        self.nc = nc
        self.p = p



def emit_ln_stream(p, stack, tag, x_dram, y_dram, n_tok, consts, eps=1e-5,
                   out_dt=BF16, col_off=None, gamma=None, beta=None):
    nc = p.nc
    if col_off is None:
        col_off = [0] * NCHUNK
    strips = chunks_of(n_tok, 512)
    ns = len(strips)
    assert ns <= 8
    with contextlib.ExitStack() as st:
        xb = [sbuf(p, st, f"{tag}_x{i}", [128, 8 * 512], F32) for i in range(2)]
        sq = [sbuf(p, st, f"{tag}_sq{i}", [128, 512], F32) for i in range(2)]
        pA = psum(p, st, f"{tag}_pA", [8, 512])
        pB = psum(p, st, f"{tag}_pB", [8, 512])
        mu = sbuf(p, st, f"{tag}_mu", [8, 512], F32)
        ex2 = sbuf(p, st, f"{tag}_ex2", [8, 512], F32)
        var = sbuf(p, st, f"{tag}_var", [8, 512], F32)
        rstd = sbuf(p, st, f"{tag}_rstd", [8, 512], F32)
        nmr = sbuf(p, st, f"{tag}_nmr", [8, 512], F32)
        pR = psum(p, st, f"{tag}_pR", [128, 512])
        pM = psum(p, st, f"{tag}_pM", [128, 512])
        tmp = sbuf(p, st, f"{tag}_tmp", [128, 512], F32)
        yb = sbuf(p, st, f"{tag}_yb", [128, 512], out_dt)
        yb2 = sbuf(p, st, f"{tag}_yb2", [128, 512], out_dt)
        ssel = consts["stats_sel"]
        bsel = consts["bsel"]
        first = True
        for s, (t0, tl) in enumerate(strips):
            x = xb[s % 2]
            for c in range(NCHUNK):
                o = col_off[c]
                p.dma("sp", x, x[:, c * 512:c * 512 + tl],
                      None, x_dram[c * 128:(c + 1) * 128, t0 + o:t0 + o + tl],
                      f"{tag}_ld", allow_slow_non_contiguous=True, nowaw=(c > 0))
            for c in range(NCHUNK):
                xc = x[:, c * 512:c * 512 + tl]
                p.op("pe", lambda e, xc=xc, s=s, f=first, tl=tl: e.matmul(
                    pA[0:8, 0:tl], ssel[:, 8 * s:8 * s + 8], xc, start=f, stop=False),
                    reads=[ssel, x], writes=[pA], acc=not first)
                p.op("act", lambda e, xc=xc, o=sq[c % 2], tl=tl: e.activation(
                    o[:, 0:tl], xc, AF.Square), reads=[x], writes=[sq[c % 2]])
                p.op("pe", lambda e, s=s, c=c, tl=tl, f=first: e.matmul(
                    pB[0:8, 0:tl], ssel[:, 8 * s:8 * s + 8], sq[c % 2][:, 0:tl],
                    start=f, stop=False),
                    reads=[ssel, sq[c % 2]], writes=[pB], acc=not first)
                first = False
        p.op("dve", lambda e: e.tensor_scalar(mu.ap, pA.ap, 1.0 / D, None, OP.mult),
             reads=[pA], writes=[mu])
        p.op("dve", lambda e: e.tensor_scalar(ex2.ap, pB.ap, 1.0 / D, None, OP.mult),
             reads=[pB], writes=[ex2])
        p.op("act", lambda e: e.activation(var.ap, mu.ap, AF.Square),
             reads=[mu], writes=[var])
        p.op("dve", lambda e: e.tensor_sub(var.ap, ex2.ap, var.ap),
             reads=[ex2, var], writes=[var])
        epsc, epsb = p.fconstn(eps, 8)
        p.op("act", lambda e: e.activation(rstd.ap, var.ap, AF.Ln, bias=epsc),
             reads=[var, epsb], writes=[rstd])
        p.op("act", lambda e: e.activation(rstd.ap, rstd.ap, AF.Exp, scale=-0.5),
             reads=[rstd], writes=[rstd])
        p.op("dve", lambda e: e.scalar_tensor_tensor(
            nmr.ap, mu.ap, -1.0, rstd.ap, OP.mult, OP.mult),
            reads=[mu, rstd], writes=[nmr])
        for s, (t0, tl) in enumerate(strips):
            x = xb[s % 2]
            for c in range(NCHUNK):
                o = col_off[c]
                p.dma("sp", x, x[:, c * 512:c * 512 + tl],
                      None, x_dram[c * 128:(c + 1) * 128, t0 + o:t0 + o + tl],
                      f"{tag}_ld", allow_slow_non_contiguous=True, nowaw=(c > 0))
            p.op("pe", lambda e, s=s, tl=tl: e.matmul(
                pR[:, 0:tl], bsel[:, 128 * s:128 * s + 128], rstd[:, 0:tl],
                start=True, stop=True), reads=[bsel, rstd], writes=[pR])
            p.op("pe", lambda e, s=s, tl=tl: e.matmul(
                pM[:, 0:tl], bsel[:, 128 * s:128 * s + 128], nmr[:, 0:tl],
                start=True, stop=True), reads=[bsel, nmr], writes=[pM])
            for c in range(NCHUNK):
                xc = x[:, c * 512:c * 512 + tl]
                p.op("dve", lambda e, xc=xc, tl=tl: e.tensor_mul(
                    tmp[:, 0:tl], xc, pR[:, 0:tl]), reads=[x, pR], writes=[tmp])
                p.op("dve", lambda e, tl=tl: e.tensor_add(
                    yb[:, 0:tl], tmp[:, 0:tl], pM[:, 0:tl]),
                    reads=[tmp, pM], writes=[yb])
                out = yb
                if gamma is not None:
                    p.op("dve", lambda e, tl=tl, c=c: e.tensor_scalar(
                        yb2[:, 0:tl], yb[:, 0:tl], gamma[:, c:c + 1],
                        beta[:, c:c + 1], OP.mult, OP.add),
                        reads=[yb, gamma, beta], writes=[yb2])
                    out = yb2
                p.dma("sp", None, y_dram[c * 128:(c + 1) * 128, t0:t0 + tl],
                      out, out[:, 0:tl], f"{tag}_st",
                      allow_slow_non_contiguous=True)
        p.wait_all("sp", f"{tag}_st")
        p.barrier()




def build_s2a(ctx=None, pre="", io=None):
    standalone = ctx is None
    nc = bass.Bass("TRN2", target_bir_lowering=False) if standalone else ctx.nc
    DI, DO, DN = _mkio(nc, pre, io or {})
    x_full = DI("x_full", [D, S], F32)
    wq = DI("wq", [D, 512], F32)
    wk = DI("wk", [D, 512], F32)
    wv = DI("wv", [D, 512], F32)
    bq = DI("bq", [512, 1], F32)
    bk = DI("bk", [512, 1], F32)
    bv = DI("bv", [512, 1], F32)
    c_stats = DI("c_stats", [128, 64], F32)
    c_bsel = DI("c_bsel", [8, 1024], F32)
    c_hsel = DI("c_hsel", [128, 32], F32)
    c_dmask = DI("c_dmask", [128, 4 * 512], BF16)
    o_un = DO("o_un", [512, S], BF16)
    o_zr = DO("o_zr", [8, S], F32)
    lnx = DN("lnx", [D, S], BF16)

    with contextlib.ExitStack() as stack:
        p = ctx.p if not standalone else Prog(nc, stack)
        p.fconst(0.0)
        cs = sbuf(p, stack, "cs", [128, 64], F32)
        cb = sbuf(p, stack, "cb", [8, 1024], F32)
        ch = sbuf(p, stack, "ch", [128, 32], F32)
        cm = sbuf(p, stack, "cm", [128, 4 * 512], BF16)
        p.dma("sp", cs, cs.ap, None, c_stats[:, :], "cld")
        p.dma("sp", cb, cb.ap, None, c_bsel[:, :], "cld")
        p.dma("sp", ch, ch.ap, None, c_hsel[:, :], "cld")
        p.dma("sp", cm, cm.ap, None, c_dmask[:, :], "cld")
        consts = {"stats_sel": cs, "bsel": cb}
        p.barrier()
        emit_ln_stream(p, stack, "ln", x_full, lnx, S, consts)

        K = [sbuf(p, stack, f"K{c}", [128, S], BF16) for c in range(4)]
        Q = [sbuf(p, stack, f"Q{c}", [128, S], BF16) for c in range(4)]
        V = [sbuf(p, stack, f"V{t}", [128, 512], BF16) for t in range(32)]
        qn = sbuf(p, stack, "qn", [8, 8], F32)
        kn = sbuf(p, stack, "kn", [8, 8], F32)
        Mb = sbuf(p, stack, "Mb", [128, 64], F32)
        wqs = [sbuf(p, stack, f"wqs{i}", [128, 512], BF16) for i in range(NCHUNK)]
        wks = [sbuf(p, stack, f"wks{i}", [128, 512], BF16) for i in range(NCHUNK)]
        wvs = [sbuf(p, stack, f"wvs{i}", [128, 512], BF16) for i in range(NCHUNK)]
        bqs = sbuf(p, stack, "bqs", [128, 4], F32)
        bks = sbuf(p, stack, "bks", [128, 4], F32)
        bvr = sbuf(p, stack, "bvr", [1, 512], F32)
        onesc = sbuf(p, stack, "onesc", [128, 1], BF16)
        p.op("dve", lambda e: e.memset(onesc.ap, 1.0), writes=[onesc])
        for i in range(NCHUNK):
            p.dma("pool", wqs[i], wqs[i].ap, None, wq[i * 128:(i + 1) * 128, :], "wld")
            p.dma("pool", wks[i], wks[i].ap, None, wk[i * 128:(i + 1) * 128, :], "wld")
            p.dma("pool", wvs[i], wvs[i].ap, None, wv[i * 128:(i + 1) * 128, :], "wld")
        for oc in range(4):
            p.dma("sp", bqs, bqs[:, oc:oc + 1], None, bq[oc * 128:(oc + 1) * 128, :], "cld")
            p.dma("sp", bks, bks[:, oc:oc + 1], None, bk[oc * 128:(oc + 1) * 128, :], "cld")
        p.dma("sp", bvr, bvr.ap, None, bv[:, 0:1].rearrange("a b -> b a"), "cld")
        p.barrier()

        with contextlib.ExitStack() as st:
            lx = [sbuf(p, st, f"lx{i}", [128, 8 * 512], BF16) for i in range(2)]
            pP = [psum(p, st, f"pP{i}", [128, 512]) for i in range(2)]
            pN = psum(p, st, "pN", [8, 512])
            sqb = [sbuf(p, st, f"sqb{i}", [128, 512], F32) for i in range(2)]
            nmax = sbuf(p, st, "nmax", [8, 1], F32)
            pBv = psum(p, st, "pBv", [128, 512])
            onesr = sbuf(p, st, "onesr", [1, 128], F32)
            sBv = sbuf(p, st, "sBv", [128, 512], F32)
            p.op("dve", lambda e: e.memset(onesr.ap, 1.0), writes=[onesr])
            p.op("pe", lambda e: e.matmul(pBv.ap, onesr.ap, bvr.ap, start=True,
                                          stop=True), reads=[onesr, bvr], writes=[pBv])
            p.op("dve", lambda e: e.tensor_copy(sBv.ap, pBv.ap),
                 reads=[pBv], writes=[sBv])
            for tt in range(8):
                x = lx[tt % 2]
                for c in range(NCHUNK):
                    p.dma("sp", x, x[:, c * 512:(c + 1) * 512],
                          None, lnx[c * 128:(c + 1) * 128, tt * 512:(tt + 1) * 512],
                          "lxld", nowaw=(c > 0))
                for (W, BIAS, OUT, NTAB) in ((wks, bks, K, kn), (wqs, bqs, Q, qn)):
                    for oc in range(4):
                        pp = pP[oc % 2]
                        for ic in range(NCHUNK):
                            p.op("pe", lambda e, W=W, ic=ic, oc=oc, x=x, pp=pp:
                                 e.matmul(pp.ap,
                                          W[ic][:, oc * 128:(oc + 1) * 128],
                                          x[:, ic * 512:(ic + 1) * 512],
                                          start=(ic == 0), stop=(ic == NCHUNK - 1)),
                                 reads=[W[ic], x], writes=[pp], acc=(ic > 0))
                        dst = OUT[oc]
                        p.op("act", lambda e, dst=dst, tt=tt, oc=oc, pp=pp, BIAS=BIAS:
                             e.activation(dst[:, tt * 512:(tt + 1) * 512], pp.ap,
                                          AF.Identity, bias=BIAS[:, oc:oc + 1]),
                             reads=[pp, BIAS], writes=[dst])
                        sq = sqb[oc % 2]
                        p.op("act", lambda e, dst=dst, tt=tt, sq=sq: e.activation(
                            sq.ap, dst[:, tt * 512:(tt + 1) * 512], AF.Square),
                            reads=[dst], writes=[sq])
                        p.op("pe", lambda e, oc=oc, sq=sq: e.matmul(
                            pN.ap, ch[:, 8 * oc:8 * oc + 8], sq.ap,
                            start=(oc == 0), stop=(oc == 3)),
                            reads=[ch, sq], writes=[pN], acc=(oc > 0))
                    p.op("dve", lambda e: e.reduce_max(nmax.ap, pN.ap, axis=AX.X),
                         reads=[pN], writes=[nmax])
                    p.op("dve", lambda e, NTAB=NTAB, tt=tt: e.tensor_copy(
                        NTAB[:, tt:tt + 1], nmax.ap), reads=[nmax], writes=[NTAB])
                for sub in range(4):
                    pv = pP[sub % 2]
                    for ic in range(NCHUNK):
                        p.op("pe", lambda e, ic=ic, x=x, sub=sub, pv=pv: e.matmul(
                            pv.ap,
                            x[:, ic * 512 + sub * 128:ic * 512 + sub * 128 + 128],
                            wvs[ic].ap, start=(ic == 0), stop=(ic == NCHUNK - 1)),
                            reads=[wvs[ic], x], writes=[pv], acc=(ic > 0))
                    vt = V[tt * 4 + sub]
                    p.op("dve", lambda e, vt=vt, pv=pv: e.tensor_add(
                        vt.ap, pv.ap, sBv.ap), reads=[pv, sBv], writes=[vt])
            p.barrier()

        negM = sbuf(p, stack, "negM", [8, 8], F32)
        for g in range(1, 8):
            p.op("dve", lambda e, g=g: e.tensor_max(
                kn[:, g:g + 1], kn[:, g - 1:g], kn[:, g:g + 1]),
                reads=[kn], writes=[kn])
        p.op("dve", lambda e: e.tensor_mul(negM.ap, qn.ap, kn.ap),
             reads=[qn, kn], writes=[negM])
        tinyc, tinyb = p.fconstn(1e-30, 8)
        p.op("act", lambda e: e.activation(negM.ap, negM.ap, AF.Ln, bias=tinyc),
             reads=[negM, tinyb], writes=[negM])
        p.op("act", lambda e: e.activation(negM.ap, negM.ap, AF.Exp, scale=0.5),
             reads=[negM], writes=[negM])
        p.op("dve", lambda e: e.tensor_scalar(negM.ap, negM.ap, -0.125, 55.0,
                                              OP.mult, OP.add),
             reads=[negM], writes=[negM])
        with contextlib.ExitStack() as st:
            pMb = psum(p, st, "pMb", [128, 64])
            for h in range(8):
                p.op("pe", lambda e, h=h: e.matmul(
                    pMb[:, 8 * h:8 * h + 8], cb[:, 128 * h:128 * h + 128],
                    negM.ap, start=True, stop=True),
                    reads=[cb, negM], writes=[pMb], acc=(h > 0))
            p.op("dve", lambda e: e.tensor_copy(Mb.ap, pMb.ap),
                 reads=[pMb], writes=[Mb])
        p.barrier()

        with contextlib.ExitStack() as st:
            pS = [psum(p, st, f"pS{i}", [128, 512]) for i in range(2)]
            pZ = psum(p, st, "pZ", [1, 512])
            pO = [psum(p, st, f"pO{i}", [64, 512]) for i in range(2)]
            PH = [sbuf(p, st, f"PH{i}", [128, 512], BF16) for i in range(3)]
            rzb = sbuf(p, st, "rzb", [1, 512], F32)
            osb = [sbuf(p, st, f"osb{i}", [64, 512], BF16) for i in range(2)]
            phi = 0
            for h in range(8):
                koff = 64 * (h % 2)
                kc_chunk = h // 2
                for g in range(8):
                    po = pO[g % 2]
                    nsub = 4 * (g + 1)
                    for s_i in range(nsub):
                        kcol = s_i * 128
                        ps = pS[s_i % 2]
                        p.op("pe", lambda e, kcol=kcol, g=g, ps=ps, kc_chunk=kc_chunk,
                             koff=koff: e.matmul(
                                 ps.ap,
                                 K[kc_chunk][koff:koff + 64, kcol:kcol + 128],
                                 Q[kc_chunk][koff:koff + 64, g * 512:(g + 1) * 512],
                                 start=True, stop=True),
                             reads=[K[kc_chunk], Q[kc_chunk]], writes=[ps])
                        ph = PH[phi % 3]
                        phi += 1
                        col = h * 8 + g
                        p.op("act", lambda e, ps=ps, ph=ph, col=col: e.activation(
                            ph.ap, ps.ap, AF.Exp, scale=0.125,
                            bias=Mb[:, col:col + 1]),
                            reads=[ps, Mb], writes=[ph])
                        if s_i // 4 == g:
                            sub = s_i % 4
                            p.op("dve", lambda e, ph=ph, sub=sub: e.tensor_mul(
                                ph.ap, ph.ap, cm[:, 512 * sub:512 * (sub + 1)]),
                                reads=[cm, ph], writes=[ph])
                        p.op("pe", lambda e, ph=ph, s_i=s_i: e.matmul(
                            pZ.ap, onesc.ap, ph.ap,
                            start=(s_i == 0), stop=(s_i == nsub - 1)),
                            reads=[onesc, ph], writes=[pZ], acc=(s_i > 0))
                        p.op("pe", lambda e, ph=ph, s_i=s_i, po=po, h=h: e.matmul(
                            po.ap, V[s_i][:, 64 * h:64 * h + 64], ph.ap,
                            start=(s_i == 0), stop=(s_i == nsub - 1)),
                            reads=[V[s_i], ph], writes=[po], acc=(s_i > 0))
                    p.op("dve", lambda e: e.reciprocal(rzb.ap, pZ.ap),
                         reads=[pZ], writes=[rzb])
                    p.dma("pool", None, o_zr[h:h + 1, g * 512:(g + 1) * 512],
                          rzb, rzb.ap, "zst")
                    ob = osb[g % 2]
                    p.op("dve", lambda e, po=po, ob=ob: e.tensor_copy(ob.ap, po.ap),
                         reads=[po], writes=[ob])
                    p.dma("pool", None, o_un[64 * h:64 * h + 64, g * 512:(g + 1) * 512],
                          ob, ob.ap, "ost")
            p.wait_all("pool", "zst")
            p.wait_all("pool", "ost")
            p.barrier()
        if standalone:
            blk = stack.enter_context(nc.Block())
            p.play(blk)
    return nc



def build_s2b(ctx=None, pre="", io=None):
    standalone = ctx is None
    nc = bass.Bass("TRN2", target_bir_lowering=False) if standalone else ctx.nc
    DI, DO, DN = _mkio(nc, pre, io or {})
    x_own = DI("x_own", [D, T], F32)
    o_sl = DI("o_sl", [D, T], BF16)
    zr = DI("zr", [16, T], F32)
    wo = DI("wo", [D, D], F32)
    bo = DI("bo", [D, 1], F32)
    c_zsel = DI("c_zsel", [16, 8 * 128], F32)
    x_new = DO("x_new", [D, T], F32)

    with contextlib.ExitStack() as stack:
        p = ctx.p if not standalone else Prog(nc, stack)
        if standalone:
            p.fconst(0.0)
        zs = sbuf(p, stack, "zs", [16, T], F32)
        czs = sbuf(p, stack, "czs", [16, 8 * 128], F32)
        wos = [sbuf(p, stack, f"wos{i}", [128, D], BF16) for i in range(NCHUNK)]
        bos = sbuf(p, stack, "bos", [128, NCHUNK], F32)
        xo = [sbuf(p, stack, f"xo{c}", [128, T], F32) for c in range(NCHUNK)]
        ob = [sbuf(p, stack, f"ob{c}", [128, T], BF16) for c in range(NCHUNK)]
        p.dma("sp", zs, zs.ap, None, zr[:, :], "cld")
        p.dma("sp", czs, czs.ap, None, c_zsel[:, :], "cld")
        for i in range(NCHUNK):
            p.dma("pool", wos[i], wos[i].ap, None, wo[i * 128:(i + 1) * 128, :], "wld")
            p.dma("sp", bos, bos[:, i:i + 1], None, bo[i * 128:(i + 1) * 128, :], "cld")
            p.dma("sp", xo[i], xo[i].ap, None, x_own[i * 128:(i + 1) * 128, :], "xld")
            p.dma("sp", ob[i], ob[i].ap, None, o_sl[i * 128:(i + 1) * 128, :], "old")
        p.barrier()
        with contextlib.ExitStack() as st:
            pR = [psum(p, st, f"pR{i}", [128, 512]) for i in range(2)]
            pX = [psum(p, st, f"pX{i}", [128, 512]) for i in range(2)]
            on = [sbuf(p, st, f"on{c}", [128, 512], BF16) for c in range(NCHUNK)]
            xn = sbuf(p, st, "xn", [128, 512], F32)
            for tt in range(4):
                t0 = tt * 512
                for ic in range(NCHUNK):
                    pr = pR[ic % 2]
                    p.op("pe", lambda e, ic=ic, t0=t0, pr=pr: e.matmul(
                        pr.ap, czs[:, 128 * ic:128 * (ic + 1)],
                        zs[:, t0:t0 + 512], start=True, stop=True),
                        reads=[czs, zs], writes=[pr])
                    p.op("dve", lambda e, ic=ic, t0=t0, pr=pr: e.tensor_mul(
                        on[ic].ap, ob[ic][:, t0:t0 + 512], pr.ap),
                        reads=[ob[ic], pr], writes=[on[ic]])
                for oc in range(NCHUNK):
                    px = pX[oc % 2]
                    for ic in range(NCHUNK):
                        p.op("pe", lambda e, ic=ic, oc=oc, px=px: e.matmul(
                            px.ap, wos[ic][:, oc * 128:(oc + 1) * 128],
                            on[ic].ap, start=(ic == 0), stop=(ic == NCHUNK - 1)),
                            reads=[wos[ic], on[ic]], writes=[px], acc=(ic > 0))
                    p.op("dve", lambda e, oc=oc, t0=t0, px=px: e.scalar_tensor_tensor(
                        xn.ap, px.ap, bos[:, oc:oc + 1], xo[oc][:, t0:t0 + 512],
                        OP.add, OP.add), reads=[px, bos, xo[oc]], writes=[xn])
                    p.dma("sp", None, x_new[oc * 128:(oc + 1) * 128, t0:t0 + 512],
                          xn, xn.ap, "xst")
            p.wait_all("sp", "xst")
            p.barrier()
        if standalone:
            blk = stack.enter_context(nc.Block())
            p.play(blk)
    return nc



def host_consts_s2():
    stats_sel = np.zeros((128, 64), np.float32)
    for s in range(8):
        stats_sel[:, 8 * s + s] = 1.0
    bsel = np.zeros((8, 1024), np.float32)
    for s in range(8):
        bsel[s, 128 * s:128 * (s + 1)] = 1.0
    hsel = np.zeros((128, 32), np.float32)
    for oc in range(4):
        for j in range(2):
            hsel[64 * j:64 * (j + 1), 8 * oc + 2 * oc + j] = 1.0
    dmask = np.zeros((128, 4 * 512), np.float32)
    for sub in range(4):
        k = np.arange(128)[:, None] + 128 * sub
        q = np.arange(512)[None, :]
        dmask[:, 512 * sub:512 * (sub + 1)] = (q >= k).astype(np.float32)
    zsel = np.zeros((16, 8 * 128), np.float32)
    for c in range(8):
        for pp in range(128):
            zsel[2 * c + pp // 64, 128 * c + pp] = 1.0
    return {
        "c_stats": stats_sel, "c_bsel": bsel, "c_hsel": hsel,
        "c_dmask": dmask.astype(BF), "c_zsel": zsel,
    }


def build_s4(ctx=None, pre="", io=None):
    standalone = ctx is None
    nc = bass.Bass("TRN2", target_bir_lowering=False) if standalone else ctx.nc
    DI, DO, DN = _mkio(nc, pre, io or {})
    _io = io if io is not None else {}
    prep_done = "tabb" in _io
    x_own = DI("x_own", [D, T], F32)
    wqT = DI("wqT", [D, 512], F32)
    keysx = DI("keysx", [512, 128], F32)
    wgT = None if prep_done else DI("wgT", [4096, D], F32)
    wuT = None if prep_done else DI("wuT", [4096, D], F32)
    wdT = None if prep_done else DI("wdT", [D, 4096], F32)
    tabc = None if prep_done else DI("tabc", [16384, 2048], F32)
    c_stats = DI("c_stats", [128, 64], F32)
    c_bsel = DI("c_bsel", [8, 1024], F32)
    c_ident = DI("c_ident", [128, 128], BF16)
    x_new = DO("x_new", [D, T], F32)
    lnx = DN("lnx", [D, T], BF16)
    tabb = _io.get("tabb") or DN("tabb", [16384, 2048], BF16)
    edram = DN("edram", [D, T], BF16)
    wgbd = _io.get("wgbd") or DN("wgbd", [4096, D], BF16)
    wubd = _io.get("wubd") or DN("wubd", [4096, D], BF16)
    wdbd = _io.get("wdbd") or DN("wdbd", [D, 4096], BF16)
    if io is not None:
        io["tabb"], io["wgbd"], io["wubd"], io["wdbd"] = tabb, wgbd, wubd, wdbd

    NT = T // 128

    with contextlib.ExitStack() as stack:
        p = ctx.p if not standalone else Prog(nc, stack)
        if standalone:
            p.fconst(0.0)
        cs = sbuf(p, stack, "cs", [128, 64], F32)
        cb = sbuf(p, stack, "cb", [8, 1024], F32)
        cid = sbuf(p, stack, "cid", [128, 128], BF16)
        p.dma("sp", cs, cs.ap, None, c_stats[:, :], "cld")
        p.dma("sp", cb, cb.ap, None, c_bsel[:, :], "cld")
        p.dma("sp", cid, cid.ap, None, c_ident[:, :], "cld")
        consts = {"stats_sel": cs, "bsel": cb}
        p.barrier()

        if not prep_done:
            tflat_in = tabc.t.ap().rearrange("(s p a) d -> s p (a d)", s=16, p=128)
            tflat_out = tabb.t.ap().rearrange("(s p a) d -> s p (a d)", s=16, p=128)
        with contextlib.ExitStack() as st:
          if prep_done:
            pass
          else:
              tb = [sbuf(p, st, f"tb{i}", [128, 16384], BF16) for i in range(2)]
              for s in range(16):
                  b = tb[s % 2]
                  p.dma("pool", b, b.ap, None, tflat_in[s], f"tc{s % 2}")
                  p.dma("sp", None, tflat_out[s], b, b.ap, f"ts{s % 2}")
              for (srcb, src_t, dstb, dst_t) in ((wgT, wgT.t, wgbd, wgbd.t),
                                                 (wuT, wuT.t, wubd, wubd.t),
                                                 (wdT, wdT.t, wdbd, wdbd.t)):
                  fin = src_t.ap().rearrange("(s a p) f -> s p a f", a=4, p=128)
                  fout = dst_t.ap().rearrange("(s a p) f -> s p a f", a=4, p=128)
                  nf = fin.shape[3]
                  for sidx in range(fin.shape[0]):
                      b = tb[sidx % 2]
                      bv = b.t[:, 0:4 * nf].rearrange("p (a f) -> p a f", a=4)
                      p.dma("pool", b, bv, srcb, fin[sidx], f"tc{sidx % 2}")
                      p.dma("sp", None, fout[sidx], b, bv, f"ts{sidx % 2}")
              p.wait_all("pool", "ts0")
              p.wait_all("pool", "ts1")
              p.barrier()

        emit_ln_stream(p, stack, "ln", x_own, lnx, T, consts)

        hfm = [sbuf(p, stack, f"hfm{c}", [128, T], BF16) for c in range(NCHUNK)]
        for c in range(NCHUNK):
            p.dma("sp", hfm[c], hfm[c].ap, None, lnx[c * 128:(c + 1) * 128, :], "hld")
        ht = [sbuf(p, stack, f"ht{t}", [128, D], BF16) for t in range(NT)]
        WR = sbuf(p, stack, "WR", [128, NT * 32], F32)
        GT = sbuf(p, stack, "GT", [128, NT * 32], F32)
        EX = sbuf(p, stack, "EX", [128, NT * 32], U32)
        p.barrier()

        with contextlib.ExitStack() as st:
            pT = [psum(p, st, f"pT{i}", [128, 128], BF16) for i in range(2)]
            for t in range(NT):
                for c in range(NCHUNK):
                    pt = pT[c % 2]
                    p.op("pe", lambda e, t=t, c=c, pt=pt: e.transpose(
                        pt.ap, hfm[c][:, t * 128:(t + 1) * 128], cid.ap),
                        reads=[hfm[c], cid], writes=[pt])
                    p.op("dve", lambda e, t=t, c=c, pt=pt: e.tensor_copy(
                        ht[t][:, c * 128:(c + 1) * 128], pt.ap),
                        reads=[pt], writes=[ht[t]])
            p.barrier()

        with contextlib.ExitStack() as st:
            q = [sbuf(p, st, f"q{c}", [128, T], BF16) for c in range(4)]
            wq_s = [sbuf(p, st, f"wqs{i}", [128, 512], BF16) for i in range(NCHUNK)]
            kb = sbuf(p, st, "kb", [128, 512], BF16)
            pQ = [psum(p, st, f"pQ{i}", [128, 512]) for i in range(2)]
            for i in range(NCHUNK):
                p.dma("pool", wq_s[i], wq_s[i].ap, None,
                      wqT[i * 128:(i + 1) * 128, :], "wld")
            for bb in range(8):
                p.dma("pool", kb, kb[64 * (bb % 2):64 * (bb % 2) + 64,
                                     128 * (bb // 2):128 * (bb // 2) + 128],
                      None, keysx[64 * bb:64 * (bb + 1), :], "wld")
            p.barrier()
            for oc in range(4):
                for tt4 in range(4):
                    pq = pQ[tt4 % 2]
                    for ic in range(NCHUNK):
                        p.op("pe", lambda e, oc=oc, tt4=tt4, ic=ic, pq=pq: e.matmul(
                            pq.ap, wq_s[ic][:, oc * 128:(oc + 1) * 128],
                            hfm[ic][:, tt4 * 512:(tt4 + 1) * 512],
                            start=(ic == 0), stop=(ic == NCHUNK - 1)),
                            reads=[wq_s[ic], hfm[ic]], writes=[pq], acc=(ic > 0))
                    p.op("dve", lambda e, oc=oc, tt4=tt4, pq=pq: e.tensor_copy(
                        q[oc][:, tt4 * 512:(tt4 + 1) * 512], pq.ap),
                        reads=[pq], writes=[q[oc]])
            pS = [psum(p, st, f"pS{i}", [128, 128]) for i in range(2)]
            VX = sbuf(p, st, "VX", [128, NT * 32], F32)
            VY = sbuf(p, st, "VY", [128, NT * 32], F32)
            IX = sbuf(p, st, "IX", [128, NT * 32], U32)
            IY = sbuf(p, st, "IY", [128, NT * 32], U32)
            for t in range(NT):
                for pl in range(2):
                    for h in range(4):
                        b = 4 * pl + h
                        chunk, off = b // 2, 64 * (b % 2)
                        ps = pS[(2 * t + 2 * pl + h) % 2]
                        p.op("pe", lambda e, t=t, b=b, chunk=chunk, off=off, ps=ps:
                             e.matmul(ps.ap,
                                      q[chunk][off:off + 64, t * 128:(t + 1) * 128],
                                      kb[off:off + 64, chunk * 128:(chunk + 1) * 128],
                                      start=True, stop=True),
                             reads=[q[chunk], kb], writes=[ps])
                        VV, II = (VX, IX) if pl == 0 else (VY, IY)
                        col = t * 32 + h * 8
                        p.op("dve", lambda e, ps=ps, VV=VV, col=col: e.max(
                            out=VV[:, col:col + 8], in_=ps.ap),
                            reads=[ps], writes=[VV])
                        p.op("dve", lambda e, ps=ps, VV=VV, II=II, col=col:
                             e.max_index(II[:, col:col + 8], VV[:, col:col + 8],
                                         ps.ap), reads=[ps, VV], writes=[II])
        with contextlib.ExitStack() as st:
            s64 = sbuf(p, st, "s64", [128, 1024], F32)
            i64f = sbuf(p, st, "i64f", [128, 1024], F32)
            v8 = sbuf(p, st, "v8", [128, 128], F32)
            c8 = sbuf(p, st, "c8", [128, 128], U32)
            isel = sbuf(p, st, "isel", [128, 128], U32)
            jsel = sbuf(p, st, "jsel", [128, 128], U32)
            mk = sbuf(p, st, "mk", [128, 128], F32)
            tsel = sbuf(p, st, "tsel", [128, 128], F32)
            xac = sbuf(p, st, "xac", [128, 128], F32)
            yac = sbuf(p, st, "yac", [128, 128], F32)
            smax = sbuf(p, st, "smx", [128, 128], F32)
            ssum = sbuf(p, st, "ssm", [128, 128], F32)
            sexp = sbuf(p, st, "sxp", [128, 1024], F32)
            for g4 in range(4):
                cols = slice(g4 * 4 * 32, (g4 + 1) * 4 * 32)
                vx = VX[:, cols].rearrange("p (t h i) -> p (t h) i", t=4, h=4)
                vy = VY[:, cols].rearrange("p (t h j) -> p (t h) j", t=4, h=4)
                ix = IX[:, cols].rearrange("p (t h i) -> p (t h) i", t=4, h=4)
                iy = IY[:, cols].rearrange("p (t h j) -> p (t h) j", t=4, h=4)
                s64v = s64.t[:].rearrange("p (s i j) -> p (s i) j", s=16, i=8)
                i64v = i64f.t[:].rearrange("p (s i j) -> p (s i) j", s=16, i=8)
                for i in range(8):
                    ov = s64v.rearrange("p (s i) j -> p s i j", i=8)[:, :, i, :]
                    p.op("dve", lambda e, vx=vx, vy=vy, ov=ov, i=i: e.tensor_tensor(
                        ov, vx[:, :, i:i + 1].to_broadcast([128, 16, 8]),
                        vy, OP.add), reads=[VX, VY], writes=[s64])
                    oiv = i64v.rearrange("p (s i) j -> p s i j", i=8)[:, :, i, :]
                    p.op("dve", lambda e, ix=ix, iy=iy, oiv=oiv, i=i:
                         e.scalar_tensor_tensor(
                             oiv, ix[:, :, i:i + 1].to_broadcast([128, 16, 8]),
                             128.0, iy, OP.mult, OP.add),
                         reads=[IX, IY], writes=[i64f])
                for sgm in range(16):
                    p.op("dve", lambda e, sgm=sgm: e.max(
                        out=v8[:, 8 * sgm:8 * sgm + 8],
                        in_=s64[:, 64 * sgm:64 * (sgm + 1)]),
                        reads=[s64], writes=[v8])
                    p.op("dve", lambda e, sgm=sgm: e.max_index(
                        c8[:, 8 * sgm:8 * sgm + 8], v8[:, 8 * sgm:8 * sgm + 8],
                        s64[:, 64 * sgm:64 * (sgm + 1)]),
                        reads=[s64, v8], writes=[c8])
                p.op("dve", lambda e: e.tensor_scalar(
                    isel.ap, c8.ap, 3, None, OP.logical_shift_right),
                    reads=[c8], writes=[isel])
                p.op("dve", lambda e: e.tensor_scalar(
                    jsel.ap, c8.ap, 7, None, OP.bitwise_and),
                    reads=[c8], writes=[jsel])
                p.op("dve", lambda e: e.memset(xac.ap, 0.0), writes=[xac])
                p.op("dve", lambda e: e.memset(yac.ap, 0.0), writes=[yac])
                for i in range(8):
                    for (SEL, SRC, ACC) in ((isel, ix, xac), (jsel, iy, yac)):
                        p.op("dve", lambda e, SEL=SEL, i=i: e.tensor_scalar(
                            mk.ap, SEL.ap, i, None, OP.is_equal),
                            reads=[SEL], writes=[mk])
                        srcb = SRC[:, :, i:i + 1].to_broadcast([128, 16, 8])
                        mkv = mk.t[:].rearrange("p (s k) -> p s k", s=16)
                        p.op("dve", lambda e, srcb=srcb, mkv=mkv: e.tensor_tensor(
                            tsel.t[:].rearrange("p (s k) -> p s k", s=16),
                            mkv, srcb, OP.mult),
                            reads=[mk, IX if SRC is ix else IY], writes=[tsel])
                        p.op("dve", lambda e, ACC=ACC: e.tensor_add(
                            ACC.ap, ACC.ap, tsel.ap), reads=[tsel, ACC], writes=[ACC])
                p.op("dve", lambda e: e.scalar_tensor_tensor(
                    tsel.ap, xac.ap, 128.0, yac.ap, OP.mult, OP.add),
                    reads=[xac, yac], writes=[tsel])
                p.op("dve", lambda e, g4=g4: e.tensor_copy(
                    EX[:, g4 * 128:(g4 + 1) * 128], tsel.ap),
                    reads=[tsel], writes=[EX])
                v8v = v8.t[:].rearrange("p (s k) -> p s k", s=16)
                p.op("dve", lambda e, v8v=v8v: e.tensor_reduce(
                    smax.t[:].rearrange("p (s o) -> p s o", s=16, o=1)
                    if False else smax[:, 0:16], v8v, AX.X, OP.max),
                    reads=[v8], writes=[smax])
                p.op("dve", lambda e, v8v=v8v: e.tensor_sub(
                    sexp[:, 0:128].rearrange("p (s k) -> p s k", s=16), v8v,
                    smax[:, 0:16].unsqueeze(2).to_broadcast([128, 16, 8])),
                    reads=[v8, smax], writes=[sexp])
                p.op("act", lambda e: e.activation(
                    sexp[:, 0:128], sexp[:, 0:128], AF.Exp),
                    reads=[sexp], writes=[sexp])
                p.op("dve", lambda e: e.tensor_reduce(
                    ssum[:, 0:16], sexp[:, 0:128].rearrange(
                        "p (s k) -> p s k", s=16), AX.X, OP.add),
                    reads=[sexp], writes=[ssum])
                p.op("dve", lambda e: e.reciprocal(ssum[:, 0:16], ssum[:, 0:16]),
                     reads=[ssum], writes=[ssum])
                p.op("dve", lambda e, g4=g4: e.tensor_mul(
                    GT[:, g4 * 128:(g4 + 1) * 128].rearrange(
                        "p (s k) -> p s k", s=16),
                    sexp[:, 0:128].rearrange("p (s k) -> p s k", s=16),
                    ssum[:, 0:16].unsqueeze(2).to_broadcast([128, 16, 8])),
                    reads=[sexp, ssum], writes=[GT])
            p.barrier()

        with contextlib.ExitStack() as st:
            gbs = [[sbuf(p, st, f"gb{g}_{i}", [128, 2048], BF16) for i in range(8)]
                   for g in range(2)]
            scr = sbuf(p, st, "scr", [128, 1024], BF16)
            scr2 = sbuf(p, st, "scr2", [128, 1024], BF16)
            eacc = [sbuf(p, st, f"eacc{i}", [128, 1024], F32) for i in range(2)]
            ebf = sbuf(p, st, "ebf", [128, 1024], BF16)
            slq = sbuf(p, st, "slq", [128, 32], F32)
            pT2 = [psum(p, st, f"pT2{i}", [128, 128], BF16) for i in range(2)]
            etile = sbuf(p, st, "etile", [128, 128], BF16)
            for t in range(NT):
                ea = eacc[t % 2]
                p.op("dve", lambda e, ea=ea: e.memset(ea.ap, 0.0), writes=[ea])
                for sgrp in range(4):
                    grp = gbs[sgrp % 2]
                    gtok = None
                    for s_i in range(8):
                        s = sgrp * 8 + s_i
                        gtok = p.op("pool", lambda e, t=t, s=s, s_i=s_i, grp=grp:
                                    e.indirect_dma_start(
                                        out=grp[s_i].ap, out_offset=None,
                                        in_=tabb[:, :],
                                        in_offset=bass.IndirectOffsetOnAxis(
                                            ap=EX[:, t * 32 + s:t * 32 + s + 1],
                                            axis=0)),
                                    reads=[EX, tabb], writes=[grp[s_i]],
                                    inc_sem=f"g{sgrp % 2}", inc=16)
                    for s_i in range(8):
                        s = sgrp * 8 + s_i
                        col = t * 32 + s
                        p.op("dve", lambda e, s_i=s_i, t=t, grp=grp:
                             e.tensor_mul(scr.ap, grp[s_i][:, 0:1024], ht[t].ap),
                             reads=[grp[s_i], ht[t]], writes=[scr], deps=[gtok])
                        p.op("act", lambda e, col=col: e.activation(
                            scr2.ap, scr.ap, AF.Identity,
                            accum_out=WR[:, col:col + 1]),
                            reads=[scr], writes=[scr2, WR])
                    c0 = t * 32 + sgrp * 8
                    p.op("act", lambda e, c0=c0: e.activation(
                        slq[:, 0:8], WR[:, c0:c0 + 8], AF.Silu),
                        reads=[WR], writes=[slq])
                    p.op("dve", lambda e, c0=c0: e.scalar_tensor_tensor(
                        GT[:, c0:c0 + 8], slq[:, 0:8], 0.5,
                        GT[:, c0:c0 + 8], OP.mult, OP.mult),
                        reads=[slq, GT], writes=[GT])
                    for s_i in range(8):
                        s = sgrp * 8 + s_i
                        col = t * 32 + s
                        p.op("dve", lambda e, s_i=s_i, col=col, grp=grp, ea=ea:
                             e.scalar_tensor_tensor(
                                 ea.ap, grp[s_i][:, 1024:2048],
                                 GT[:, col:col + 1], ea.ap, OP.mult, OP.add),
                             reads=[grp[s_i], GT, ea], writes=[ea])
                p.op("dve", lambda e, ea=ea: e.tensor_copy(ebf.ap, ea.ap),
                     reads=[ea], writes=[ebf])
                for c in range(NCHUNK):
                    pt = pT2[c % 2]
                    p.op("pe", lambda e, c=c, pt=pt: e.transpose(
                        pt.ap, ebf[:, c * 128:(c + 1) * 128], cid.ap),
                        reads=[ebf, cid], writes=[pt])
                    p.op("dve", lambda e, pt=pt: e.tensor_copy(etile.ap, pt.ap),
                         reads=[pt], writes=[etile])
                    p.dma("sp", None, edram[c * 128:(c + 1) * 128,
                                           t * 128:(t + 1) * 128],
                          etile, etile.ap, "est")
            p.wait_all("sp", "est")
            p.barrier()

        with contextlib.ExitStack() as st:
            ag = [sbuf(p, st, f"ag{o}", [128, 1024], BF16) for o in range(32)]
            wgb = [sbuf(p, st, f"wgb{i}", [128, 1024], BF16) for i in range(2)]
            wub = [sbuf(p, st, f"wub{i}", [128, 1024], BF16) for i in range(2)]
            wdb = [sbuf(p, st, f"wdb{i}", [128, 4096], BF16) for i in range(2)]
            pG = psum(p, st, "pG", [128, 512])
            pU = psum(p, st, "pU", [128, 512])
            pM = [psum(p, st, f"pM{i}", [128, 512]) for i in range(2)]
            sg = sbuf(p, st, "sg", [128, 512], BF16)
            eb = sbuf(p, st, "eb", [128, 512], BF16)
            xoc = sbuf(p, st, "xoc", [128, 512], F32)
            xn = sbuf(p, st, "xn", [128, 512], F32)
            for big in range(2):
                b0 = big * 1024
                for oc in range(32):
                    wg_ = wgb[oc % 2]
                    wu_ = wub[oc % 2]
                    p.dma("sp", wg_, wg_.ap, wgbd,
                          wgbd[oc * 128:(oc + 1) * 128, :], "wmg")
                    p.dma("sp", wu_, wu_.ap, wubd,
                          wubd[oc * 128:(oc + 1) * 128, :], "wmu")
                    for h2 in range(2):
                        for ic in range(NCHUNK):
                            p.op("pe", lambda e, ic=ic, h2=h2, b0=b0, wg_=wg_:
                                 e.matmul(pG.ap,
                                          wg_[:, ic * 128:(ic + 1) * 128],
                                          hfm[ic][:, b0 + h2 * 512:b0 + (h2 + 1) * 512],
                                          start=(ic == 0), stop=(ic == NCHUNK - 1)),
                                 reads=[wg_, hfm[ic]], writes=[pG], acc=(ic > 0))
                        for ic in range(NCHUNK):
                            p.op("pe", lambda e, ic=ic, h2=h2, b0=b0, wu_=wu_:
                                 e.matmul(pU.ap,
                                          wu_[:, ic * 128:(ic + 1) * 128],
                                          hfm[ic][:, b0 + h2 * 512:b0 + (h2 + 1) * 512],
                                          start=(ic == 0), stop=(ic == NCHUNK - 1)),
                                 reads=[wu_, hfm[ic]], writes=[pU], acc=(ic > 0))
                        p.op("act", lambda e: e.activation(sg.ap, pG.ap, AF.Silu),
                             reads=[pG], writes=[sg])
                        p.op("dve", lambda e, oc=oc, h2=h2: e.tensor_mul(
                            ag[oc][:, h2 * 512:(h2 + 1) * 512], sg.ap, pU.ap),
                            reads=[sg, pU], writes=[ag[oc]])
                for oc8 in range(NCHUNK):
                    wd_ = wdb[oc8 % 2]
                    p.dma("sp", wd_, wd_.ap, wdbd,
                          wdbd[oc8 * 128:(oc8 + 1) * 128, :], "wmd")
                    for h2 in range(2):
                        for ic in range(32):
                            p.op("pe", lambda e, ic=ic, h2=h2, oc8=oc8, wd_=wd_:
                                 e.matmul(pM[h2].ap,
                                          wd_[:, ic * 128:(ic + 1) * 128],
                                          ag[ic][:, h2 * 512:(h2 + 1) * 512],
                                          start=(ic == 0), stop=(ic == 31)),
                                 reads=[wd_, ag[ic]], writes=[pM[h2]], acc=(ic > 0))
                        t0 = b0 + h2 * 512
                        p.dma("sp", xoc, xoc.ap, None,
                              x_own[oc8 * 128:(oc8 + 1) * 128, t0:t0 + 512], "xld")
                        p.dma("sp", eb, eb.ap, edram,
                              edram[oc8 * 128:(oc8 + 1) * 128, t0:t0 + 512], "eld")
                        p.op("dve", lambda e, h2=h2: e.scalar_tensor_tensor(
                            xn.ap, pM[h2].ap, 0.5, xoc.ap, OP.mult, OP.add),
                            reads=[pM[h2], xoc], writes=[xn])
                        p.op("dve", lambda e: e.tensor_add(xn.ap, xn.ap, eb.ap),
                             reads=[xn, eb], writes=[xn])
                        p.dma("sp", None, x_new[oc8 * 128:(oc8 + 1) * 128,
                                                t0:t0 + 512], xn, xn.ap, "xst")
            p.wait_all("sp", "xst")
            p.wait_all("pool", "xst")
            p.barrier()
        if standalone:
            blk = stack.enter_context(nc.Block())
            p.play(blk)
    return nc


def build_s5a(ctx=None, pre="", io=None):
    standalone = ctx is None
    nc = bass.Bass("TRN2", target_bir_lowering=False) if standalone else ctx.nc
    DI, DO, DN = _mkio(nc, pre, io or {})
    x5 = DI("x5", [D, T + 1], F32)
    cwv = DI("cwv", [D, 3], F32)
    cbv = DI("cbv", [D, 1], F32)
    kwT = DI("kwT", [D, D], F32)
    kbv = DI("kbv", [D, 1], F32)
    vwT = DI("vwT", [D, D], F32)
    vbv = DI("vbv", [D, 1], F32)
    c_stats = DI("c_stats", [128, 64], F32)
    c_bsel = DI("c_bsel", [8, 1024], F32)
    o_k = DO("o_k", [D, 1024], BF16)
    o_v = DO("o_v", [1024, D], BF16)
    o_lnx = DO("o_lnx", [D, T], BF16)
    lnx5 = DN("lnx5", [D, T + 1], BF16)
    TD = 1024

    with contextlib.ExitStack() as stack:
        p = ctx.p if not standalone else Prog(nc, stack)
        if standalone:
            p.fconst(0.0)
        cs = sbuf(p, stack, "cs", [128, 64], F32)
        cb = sbuf(p, stack, "cb", [8, 1024], F32)
        p.dma("sp", cs, cs.ap, None, c_stats[:, :], "cld")
        p.dma("sp", cb, cb.ap, None, c_bsel[:, :], "cld")
        consts = {"stats_sel": cs, "bsel": cb}
        p.barrier()
        emit_ln_stream(p, stack, "ln", x5, lnx5, T + 1, consts)
        with contextlib.ExitStack() as st:
            cpb = sbuf(p, st, "cpb", [128, T], BF16)
            for c in range(NCHUNK):
                p.dma("sp", cpb, cpb.ap, None, lnx5[c * 128:(c + 1) * 128, 1:T + 1],
                      "cpl")
                p.dma("sp", None, o_lnx[c * 128:(c + 1) * 128, :], cpb, cpb.ap, "cps")
            p.wait_all("sp", "cps")
            p.barrier()
        with contextlib.ExitStack() as st:
            lx = [sbuf(p, st, f"lx{c}", [128, T + 2], BF16) for c in range(NCHUNK)]
            xd = [sbuf(p, st, f"xd{c}", [128, TD], BF16) for c in range(NCHUNK)]
            acc = sbuf(p, st, "acc", [128, TD], F32)
            cw = sbuf(p, st, "cw", [128, 3 * NCHUNK], F32)
            cbb = sbuf(p, st, "cbb", [128, NCHUNK], F32)
            for c in range(NCHUNK):
                p.dma("sp", lx[c], lx[c][:, 0:T + 1], None,
                      lnx5[c * 128:(c + 1) * 128, :], "lxl")
                p.dma("sp", cw, cw[:, 3 * c:3 * c + 3], None,
                      cwv[c * 128:(c + 1) * 128, :], "cld")
                p.dma("sp", cbb, cbb[:, c:c + 1], None,
                      cbv[c * 128:(c + 1) * 128, :], "cld")
            p.barrier()

            def tap(c, off):
                return lx[c][:, off:off + 2 * TD].rearrange(
                    "p (n two) -> p n two", two=2)[:, :, 0:1].squeeze(2)

            for c in range(NCHUNK):
                p.op("dve", lambda e, c=c: e.tensor_scalar(
                    acc.ap, tap(c, 0), cw[:, 3 * c:3 * c + 1], None, OP.mult),
                    reads=[lx[c], cw], writes=[acc])
                p.op("dve", lambda e, c=c: e.scalar_tensor_tensor(
                    acc.ap, tap(c, 1), cw[:, 3 * c + 1:3 * c + 2], acc.ap,
                    OP.mult, OP.add), reads=[lx[c], cw, acc], writes=[acc])
                p.op("dve", lambda e, c=c: e.scalar_tensor_tensor(
                    acc.ap, tap(c, 2), cw[:, 3 * c + 2:3 * c + 3], acc.ap,
                    OP.mult, OP.add), reads=[lx[c], cw, acc], writes=[acc])
                p.op("dve", lambda e, c=c: e.tensor_scalar(
                    xd[c].ap, acc.ap, cbb[:, c:c + 1], None, OP.add),
                    reads=[acc, cbb], writes=[xd[c]])
            p.barrier()
            kw = [sbuf(p, st, f"kw{i}", [128, D], BF16) for i in range(NCHUNK)]
            vw = [sbuf(p, st, f"vw{i}", [128, D], BF16) for i in range(NCHUNK)]
            kbs = sbuf(p, st, "kbs", [128, NCHUNK], F32)
            vbr = sbuf(p, st, "vbr", [1, D], F32)
            onesr = sbuf(p, st, "onesr", [1, 128], F32)
            pP = [psum(p, st, f"pP{i}", [128, 512]) for i in range(2)]
            pBv = [psum(p, st, f"pBv{i}", [128, 512]) for i in range(2)]
            sBv = sbuf(p, st, "sBv", [128, D], F32)
            kout = sbuf(p, st, "kout", [128, TD], BF16)
            vout = sbuf(p, st, "vout", [128, 512], BF16)
            for i in range(NCHUNK):
                p.dma("pool", kw[i], kw[i].ap, None, kwT[i * 128:(i + 1) * 128, :], "wld")
                p.dma("pool", vw[i], vw[i].ap, None, vwT[i * 128:(i + 1) * 128, :], "wld")
                p.dma("sp", kbs, kbs[:, i:i + 1], None, kbv[i * 128:(i + 1) * 128, :], "cld")
            p.dma("sp", vbr, vbr.ap, None, vbv[:, 0:1].rearrange("a b -> b a"), "cld")
            p.op("dve", lambda e: e.memset(onesr.ap, 1.0), writes=[onesr])
            for half in range(2):
                p.op("pe", lambda e, half=half: e.matmul(
                    pBv[half].ap, onesr.ap, vbr[:, half * 512:(half + 1) * 512],
                    start=True, stop=True), reads=[onesr, vbr], writes=[pBv[half]])
                p.op("dve", lambda e, half=half: e.tensor_copy(
                    sBv[:, half * 512:(half + 1) * 512], pBv[half].ap),
                    reads=[pBv[half]], writes=[sBv])
            for oc in range(NCHUNK):
                for tt in range(2):
                    pp = pP[tt % 2]
                    for ic in range(NCHUNK):
                        p.op("pe", lambda e, oc=oc, tt=tt, ic=ic, pp=pp: e.matmul(
                            pp.ap, kw[ic][:, oc * 128:(oc + 1) * 128],
                            xd[ic][:, tt * 512:(tt + 1) * 512],
                            start=(ic == 0), stop=(ic == NCHUNK - 1)),
                            reads=[kw[ic], xd[ic]], writes=[pp], acc=(ic > 0))
                    p.op("act", lambda e, oc=oc, tt=tt, pp=pp: e.activation(
                        kout[:, tt * 512:(tt + 1) * 512], pp.ap, AF.Identity,
                        bias=kbs[:, oc:oc + 1]), reads=[pp, kbs], writes=[kout])
                p.dma("sp", None, o_k[oc * 128:(oc + 1) * 128, :], kout, kout.ap,
                      "kst")
            for vt in range(8):
                for half in range(2):
                    pp = pP[half % 2]
                    for ic in range(NCHUNK):
                        p.op("pe", lambda e, vt=vt, half=half, ic=ic, pp=pp:
                             e.matmul(pp.ap,
                                      xd[ic][:, vt * 128:(vt + 1) * 128],
                                      vw[ic][:, half * 512:(half + 1) * 512],
                                      start=(ic == 0), stop=(ic == NCHUNK - 1)),
                             reads=[vw[ic], xd[ic]], writes=[pp], acc=(ic > 0))
                    p.op("dve", lambda e, half=half, pp=pp: e.tensor_add(
                        vout.ap, pp.ap, sBv[:, half * 512:(half + 1) * 512]),
                        reads=[pp, sBv], writes=[vout])
                    p.dma("sp", None, o_v[vt * 128:(vt + 1) * 128,
                                          half * 512:(half + 1) * 512],
                          vout, vout.ap, "vst")
            p.wait_all("sp", "kst")
            p.wait_all("sp", "vst")
            p.wait_all("pool", "vst")
            p.barrier()
        if standalone:
            blk = stack.enter_context(nc.Block())
            p.play(blk)
    return nc


def build_s5b(ctx=None, pre="", io=None):
    standalone = ctx is None
    nc = bass.Bass("TRN2", target_bir_lowering=False) if standalone else ctx.nc
    DI, DO, DN = _mkio(nc, pre, io or {})
    x_own = DI("x_own", [D, T], F32)
    lnx = DI("lnx", [D, T], BF16)
    kf = DI("kf", [D, T], BF16)
    vf = DI("vf", [T, D], BF16)
    qwT = DI("qwT", [D, D], F32)
    qbv = DI("qbv", [D, 1], F32)
    owT = DI("owT", [D, D], F32)
    obv = DI("obv", [D, 1], F32)
    c_hs16 = DI("c_hs16", [128, 16 * NCHUNK], F32)
    c_bs16 = DI("c_bs16", [16, 16 * 128], F32)
    x_new = DO("x_new", [D, T], F32)
    NH, NG, NKT = 16, 4, 16

    with contextlib.ExitStack() as stack:
        p = ctx.p if not standalone else Prog(nc, stack)
        if standalone:
            p.fconst(0.0)
        ch16 = sbuf(p, stack, "ch16", [128, 16 * NCHUNK], F32)
        cb16 = sbuf(p, stack, "cb16", [16, 16 * 128], F32)
        p.dma("sp", ch16, ch16.ap, None, c_hs16[:, :], "cld")
        p.dma("sp", cb16, cb16.ap, None, c_bs16[:, :], "cld")
        Q = [sbuf(p, stack, f"Q{c}", [128, T], BF16) for c in range(NCHUNK)]
        K = [sbuf(p, stack, f"K{c}", [128, T], BF16) for c in range(NCHUNK)]
        V = [sbuf(p, stack, f"V{t}", [128, D], BF16) for t in range(NKT)]
        hx = [sbuf(p, stack, f"hx{c}", [128, T], BF16) for c in range(NCHUNK)]
        qw = [sbuf(p, stack, f"qw{i}", [128, D], BF16) for i in range(NCHUNK)]
        qbs = sbuf(p, stack, "qbs", [128, NCHUNK], F32)
        qn = sbuf(p, stack, "qn", [16, NG], F32)
        kn = sbuf(p, stack, "kn", [16, NKT // 4], F32)
        knm = sbuf(p, stack, "knm", [16, 1], F32)
        negM = sbuf(p, stack, "negM", [16, NG], F32)
        Mb = sbuf(p, stack, "Mb", [128, NH * NG], F32)
        onesc = sbuf(p, stack, "onesc", [128, 1], BF16)
        onesr = sbuf(p, stack, "onesr", [1, 128], F32)
        ofm = [sbuf(p, stack, f"ofm{c}", [128, T], BF16) for c in range(NCHUNK)]
        p.op("dve", lambda e: e.memset(onesc.ap, 1.0), writes=[onesc])
        p.op("dve", lambda e: e.memset(onesr.ap, 1.0), writes=[onesr])
        for c in range(NCHUNK):
            p.dma("sp", hx[c], hx[c].ap, None, lnx[c * 128:(c + 1) * 128, :], "hld")
            p.dma("sp", K[c], K[c].ap, None, kf[c * 128:(c + 1) * 128, :], "kld")
            p.dma("pool", qw[c], qw[c].ap, None, qwT[c * 128:(c + 1) * 128, :], "wld")
            p.dma("sp", qbs, qbs[:, c:c + 1], None, qbv[c * 128:(c + 1) * 128, :],
                  "cld")
        for t in range(NKT):
            p.dma("sp", V[t], V[t].ap, None, vf[t * 128:(t + 1) * 128, :], "vld")
        p.barrier()
        with contextlib.ExitStack() as st:
            pP = [psum(p, st, f"pP{i}", [128, 512]) for i in range(2)]
            pN = psum(p, st, "pN", [16, 512])
            sqb = [sbuf(p, st, f"sqb{i}", [128, 512], F32) for i in range(2)]
            nmax = sbuf(p, st, "nmax", [16, 1], F32)
            for g in range(NG):
                for oc in range(NCHUNK):
                    pp = pP[oc % 2]
                    for ic in range(NCHUNK):
                        p.op("pe", lambda e, g=g, oc=oc, ic=ic, pp=pp: e.matmul(
                            pp.ap, qw[ic][:, oc * 128:(oc + 1) * 128],
                            hx[ic][:, g * 512:(g + 1) * 512],
                            start=(ic == 0), stop=(ic == NCHUNK - 1)),
                            reads=[qw[ic], hx[ic]], writes=[pp], acc=(ic > 0))
                    p.op("act", lambda e, g=g, oc=oc, pp=pp: e.activation(
                        Q[oc][:, g * 512:(g + 1) * 512], pp.ap, AF.Identity,
                        bias=qbs[:, oc:oc + 1]), reads=[pp, qbs], writes=[Q[oc]])
                    sq = sqb[oc % 2]
                    p.op("act", lambda e, g=g, oc=oc, sq=sq: e.activation(
                        sq.ap, Q[oc][:, g * 512:(g + 1) * 512], AF.Square),
                        reads=[Q[oc]], writes=[sq])
                    p.op("pe", lambda e, oc=oc, sq=sq: e.matmul(
                        pN.ap, ch16[:, 16 * oc:16 * oc + 16], sq.ap,
                        start=(oc == 0), stop=(oc == NCHUNK - 1)),
                        reads=[ch16, sq], writes=[pN], acc=(oc > 0))
                p.op("dve", lambda e: e.reduce_max(nmax.ap, pN.ap, axis=AX.X),
                     reads=[pN], writes=[nmax])
                p.op("dve", lambda e, g=g: e.tensor_copy(qn[:, g:g + 1], nmax.ap),
                     reads=[nmax], writes=[qn])
            for g in range(NKT // 4):
                for oc in range(NCHUNK):
                    sq = sqb[oc % 2]
                    p.op("act", lambda e, g=g, oc=oc, sq=sq: e.activation(
                        sq.ap, K[oc][:, g * 512:(g + 1) * 512], AF.Square),
                        reads=[K[oc]], writes=[sq])
                    p.op("pe", lambda e, oc=oc, sq=sq: e.matmul(
                        pN.ap, ch16[:, 16 * oc:16 * oc + 16], sq.ap,
                        start=(oc == 0), stop=(oc == NCHUNK - 1)),
                        reads=[ch16, sq], writes=[pN], acc=(oc > 0))
                p.op("dve", lambda e: e.reduce_max(nmax.ap, pN.ap, axis=AX.X),
                     reads=[pN], writes=[nmax])
                p.op("dve", lambda e, g=g: e.tensor_copy(kn[:, g:g + 1], nmax.ap),
                     reads=[nmax], writes=[kn])
            p.op("dve", lambda e: e.reduce_max(knm.ap, kn.ap, axis=AX.X),
                 reads=[kn], writes=[knm])
            p.op("dve", lambda e: e.tensor_mul(
                negM.ap, qn.ap, knm[:, 0:1].to_broadcast([16, NG])),
                reads=[qn, knm], writes=[negM])
            tinyc, tinyb = p.fconstn(1e-30, 16)
            p.op("act", lambda e: e.activation(negM.ap, negM.ap, AF.Ln, bias=tinyc),
                 reads=[negM, tinyb], writes=[negM])
            p.op("act", lambda e: e.activation(negM.ap, negM.ap, AF.Exp, scale=0.5),
                 reads=[negM], writes=[negM])
            p.op("dve", lambda e: e.tensor_scalar(
                negM.ap, negM.ap, -0.125, 55.0, OP.mult, OP.add),
                reads=[negM], writes=[negM])
            pMb = psum(p, st, "pMb", [128, NH * NG])
            for h in range(NH):
                p.op("pe", lambda e, h=h: e.matmul(
                    pMb[:, NG * h:NG * h + NG], cb16[:, 128 * h:128 * h + 128],
                    negM.ap, start=True, stop=True),
                    reads=[cb16, negM], writes=[pMb], acc=(h > 0))
            p.op("dve", lambda e: e.tensor_copy(Mb.ap, pMb.ap),
                 reads=[pMb], writes=[Mb])
            p.barrier()
        with contextlib.ExitStack() as st:
            pS = [psum(p, st, f"pS{i}", [128, 512]) for i in range(2)]
            pZ = psum(p, st, "pZ", [1, 512])
            pO = [psum(p, st, f"pO{i}", [64, 512]) for i in range(2)]
            pB = psum(p, st, "pB", [64, 512])
            PH = [sbuf(p, st, f"PH{i}", [128, 512], BF16) for i in range(3)]
            rzb = sbuf(p, st, "rzb", [1, 512], F32)
            rb = sbuf(p, st, "rb", [64, 512], F32)
            phi = 0
            for h in range(NH):
                kc_chunk, koff = h // 2, 64 * (h % 2)
                for g in range(NG):
                    po = pO[g % 2]
                    for s_i in range(NKT):
                        ps = pS[s_i % 2]
                        p.op("pe", lambda e, h=h, g=g, s_i=s_i, ps=ps,
                             kc_chunk=kc_chunk, koff=koff: e.matmul(
                                 ps.ap,
                                 K[kc_chunk][koff:koff + 64,
                                             s_i * 128:(s_i + 1) * 128],
                                 Q[kc_chunk][koff:koff + 64,
                                             g * 512:(g + 1) * 512],
                                 start=True, stop=True),
                             reads=[K[kc_chunk], Q[kc_chunk]], writes=[ps])
                        ph = PH[phi % 3]
                        phi += 1
                        col = h * NG + g
                        p.op("act", lambda e, ps=ps, ph=ph, col=col: e.activation(
                            ph.ap, ps.ap, AF.Exp, scale=0.125,
                            bias=Mb[:, col:col + 1]), reads=[ps, Mb], writes=[ph])
                        p.op("pe", lambda e, ph=ph, s_i=s_i: e.matmul(
                            pZ.ap, onesc.ap, ph.ap,
                            start=(s_i == 0), stop=(s_i == NKT - 1)),
                            reads=[onesc, ph], writes=[pZ], acc=(s_i > 0))
                        p.op("pe", lambda e, ph=ph, s_i=s_i, po=po, h=h: e.matmul(
                            po.ap, V[s_i][:, 64 * h:64 * h + 64], ph.ap,
                            start=(s_i == 0), stop=(s_i == NKT - 1)),
                            reads=[V[s_i], ph], writes=[po], acc=(s_i > 0))
                    p.op("dve", lambda e: e.reciprocal(rzb.ap, pZ.ap),
                         reads=[pZ], writes=[rzb])
                    p.op("pe", lambda e: e.matmul(pB.ap, onesr[:, 0:64], rzb.ap,
                                                  start=True, stop=True),
                         reads=[onesr, rzb], writes=[pB])
                    p.op("dve", lambda e: e.tensor_copy(rb.ap, pB.ap),
                         reads=[pB], writes=[rb])
                    oc, ooff = h // 2, 64 * (h % 2)
                    p.op("dve", lambda e, po=po, oc=oc, ooff=ooff, g=g:
                         e.tensor_mul(ofm[oc][ooff:ooff + 64,
                                              g * 512:(g + 1) * 512],
                                      po.ap, rb.ap),
                         reads=[po, rb], writes=[ofm[oc]])
            p.barrier()
        with contextlib.ExitStack() as st:
            ow = [sbuf(p, st, f"ow{i}", [128, D], BF16) for i in range(NCHUNK)]
            obs = sbuf(p, st, "obs", [128, NCHUNK], F32)
            xoc = sbuf(p, st, "xoc", [128, 512], F32)
            xn = sbuf(p, st, "xn", [128, 512], F32)
            pX = [psum(p, st, f"pX{i}", [128, 512]) for i in range(2)]
            for i in range(NCHUNK):
                p.dma("pool", ow[i], ow[i].ap, None, owT[i * 128:(i + 1) * 128, :],
                      "wld")
                p.dma("sp", obs, obs[:, i:i + 1], None,
                      obv[i * 128:(i + 1) * 128, :], "cld")
            p.barrier()
            for tt in range(4):
                t0 = tt * 512
                for oc in range(NCHUNK):
                    px = pX[oc % 2]
                    for ic in range(NCHUNK):
                        p.op("pe", lambda e, ic=ic, oc=oc, t0=t0, px=px: e.matmul(
                            px.ap, ow[ic][:, oc * 128:(oc + 1) * 128],
                            ofm[ic][:, t0:t0 + 512],
                            start=(ic == 0), stop=(ic == NCHUNK - 1)),
                            reads=[ow[ic], ofm[ic]], writes=[px], acc=(ic > 0))
                    p.dma("sp", xoc, xoc.ap, None,
                          x_own[oc * 128:(oc + 1) * 128, t0:t0 + 512], "xld")
                    p.op("dve", lambda e, oc=oc, px=px: e.scalar_tensor_tensor(
                        xn.ap, px.ap, obs[:, oc:oc + 1], xoc.ap, OP.add, OP.add),
                        reads=[px, obs, xoc], writes=[xn])
                    p.dma("sp", None, x_new[oc * 128:(oc + 1) * 128, t0:t0 + 512],
                          xn, xn.ap, "xst")
            p.wait_all("sp", "xst")
            p.wait_all("pool", "xst")
            p.barrier()
        if standalone:
            blk = stack.enter_context(nc.Block())
            p.play(blk)
    return nc


def host_consts_s5():
    hs16 = np.zeros((128, 16 * NCHUNK), np.float32)
    for oc in range(NCHUNK):
        for j in range(2):
            hs16[64 * j:64 * (j + 1), 16 * oc + 2 * oc + j] = 1.0
    bs16 = np.zeros((16, 16 * 128), np.float32)
    for h in range(16):
        bs16[h, 128 * h:128 * (h + 1)] = 1.0
    return {"c_hs16": hs16, "c_bs16": bs16}

EXT = 4096
NQT = EXT // 128


def build_l0ln(ctx=None, pre="", io=None):
    standalone = ctx is None
    nc = bass.Bass("TRN2", target_bir_lowering=False) if standalone else ctx.nc
    DI, DO, DN = _mkio(nc, pre, io or {})
    x_ext = DI("x_ext", [D, EXT], F32)
    c_stats = DI("c_stats", [128, 64], F32)
    c_bsel = DI("c_bsel", [8, 1024], F32)
    y0 = DO("y0", [D, EXT + 1], F32)
    with contextlib.ExitStack() as stack:
        p = ctx.p if not standalone else Prog(nc, stack)
        if standalone:
            p.fconst(0.0)
        cs = sbuf(p, stack, "cs", [128, 64], F32)
        cb = sbuf(p, stack, "cb", [8, 1024], F32)
        zc = sbuf(p, stack, "zc", [128, 1], F32)
        p.dma("sp", cs, cs.ap, None, c_stats[:, :], "cld")
        p.dma("sp", cb, cb.ap, None, c_bsel[:, :], "cld")
        p.op("dve", lambda e: e.memset(zc.ap, 0.0), writes=[zc])
        consts = {"stats_sel": cs, "bsel": cb}
        p.barrier()
        yv = Buf(y0.t[:, 1:EXT + 1], "yv")
        emit_ln_stream(p, stack, "ln", x_ext, yv, EXT, consts, out_dt=F32)
        for c in range(NCHUNK):
            p.dma("sp", None, y0[c * 128:(c + 1) * 128, 0:1], zc, zc.ap, "z0",
                  allow_slow_non_contiguous=True)
        p.wait_all("sp", "z0")
        p.barrier()
        if standalone:
            blk = stack.enter_context(nc.Block())
            p.play(blk)
    return nc


def build_l0comb(ctx=None, pre="", io=None):
    standalone = ctx is None
    nc = bass.Bass("TRN2", target_bir_lowering=False) if standalone else ctx.nc
    DI, DO, DN = _mkio(nc, pre, io or {})
    x_own = DI("x_own", [D, T], F32)
    y2 = DI("y2", [D, T], F32)
    x_new = DO("x_new", [D, T], F32)
    with contextlib.ExitStack() as stack:
        p = ctx.p if not standalone else Prog(nc, stack)
        xa = sbuf(p, stack, "xa", [128, T], F32)
        ya = sbuf(p, stack, "ya", [128, T], F32)
        xn = sbuf(p, stack, "xn", [128, T], F32)
        for c in range(NCHUNK):
            p.dma("sp", xa, xa.ap, None, x_own[c * 128:(c + 1) * 128, :], "ld")
            p.dma("sp", ya, ya.ap, None, y2[c * 128:(c + 1) * 128, :], "ld")
            p.op("dve", lambda e: e.scalar_tensor_tensor(
                xn.ap, ya.ap, 0.5, xa.ap, OP.mult, OP.add),
                reads=[xa, ya], writes=[xn])
            p.dma("sp", None, x_new[c * 128:(c + 1) * 128, :], xn, xn.ap, "st")
        p.wait_all("sp", "st")
        p.barrier()
        if standalone:
            blk = stack.enter_context(nc.Block())
            p.play(blk)
    return nc


def build_l0inner(ctx=None, pre="", io=None, phases=4, p3m=32, p3wo=True, p3bits=31, dbg=False):
    standalone = ctx is None
    nc = bass.Bass("TRN2", target_bir_lowering=False) if standalone else ctx.nc
    DI, DO, DN = _mkio(nc, pre, io or {})
    y_in = DI("y_in", [D, EXT + 1], F32)
    wqkvp = DI("wqkvp", [D, 1536], F32)
    qsc = DI("qsc", [128, 4], F32)
    c1q = DI("c1q", [128, EXT], F32)
    c2q = DI("c2q", [128, EXT], F32)
    c1k = DI("c1k", [128, EXT], F32)
    c2k = DI("c2k", [128, EXT], F32)
    wo = DI("wo", [512, D], F32)
    lng = DI("lng", [D, 1], F32)
    lnb = DI("lnb", [D, 1], F32)
    w1 = DI("w1", [D, 5460], F32)
    b1 = DI("b1", [5460, 1], F32)
    w2 = DI("w2", [2730, D], F32)
    b2 = DI("b2", [D, 1], F32)
    masks = DI("masks", [128, NQT * 5 * 128], BF16)
    c_stats = DI("c_stats", [128, 64], F32)
    c_bsel = DI("c_bsel", [8, 1024], F32)
    c_hsp = DI("c_hsp", [128, 16], F32)
    c_bs4 = DI("c_bs4", [8, 256], F32)
    c_perm = DI("c_perm", [128, 512], BF16)
    c_hs64 = DI("c_hs64", [128, 32], BF16)
    x_out = DO("x_out", [D, EXT], F32)
    kind2 = "Internal" if not dbg else "ExternalOutput"
    if dbg:
        xmid = DO("xmid", [D, EXT + 1], F32)
        hffn = DO("hffn", [D, EXT], BF16)
        qdbg = DO("qdbg", [D // 2, EXT], BF16)
        vdbg = DO("vdbg", [EXT, 512], BF16)
        mdbg = DO("mdbg", [128, 8 * NQT], F32)
        rdbg = DO("rdbg", [8, 16 * 512], F32)
        ardbg = DO("ardbg", [128, 16 * 512], BF16)
        apdbg = DO("apdbg", [128, 16 * 512], BF16)
        rbdbg = DO("rbdbg", [128, 16 * 512], F32)
        cbdbg = DO("cbdbg", [128, 16 * 512], F32)
        tadbg = DO("tadbg", [128, 16 * 512], F32)
        kdbg = DO("kdbg", [D // 2, EXT], BF16)
        odbg = DO("odbg", [D // 2, EXT], BF16)
    else:
        xmid = DN("xmid", [D, EXT + 1], F32)
        hffn = DN("hffn", [D, EXT], BF16)

    NS = EXT // 512

    with contextlib.ExitStack() as stack:
        p = ctx.p if not standalone else Prog(nc, stack)
        if standalone:
            p.fconst(0.0)
        cs = sbuf(p, stack, "cs", [128, 64], F32)
        cb = sbuf(p, stack, "cb", [8, 1024], F32)
        chp = sbuf(p, stack, "chp", [128, 16], F32)
        cb4 = sbuf(p, stack, "cb4", [8, 256], F32)
        qscb = sbuf(p, stack, "qscb", [128, 4], F32)
        cpm = sbuf(p, stack, "cpm", [128, 512], BF16)
        ch64 = sbuf(p, stack, "ch64", [128, 32], BF16)
        onesc = sbuf(p, stack, "onesc", [128, 1], BF16)
        onesr = sbuf(p, stack, "onesr", [1, 128], F32)
        p.dma("sp", cs, cs.ap, None, c_stats[:, :], "cld")
        p.dma("sp", cb, cb.ap, None, c_bsel[:, :], "cld")
        p.dma("sp", chp, chp.ap, None, c_hsp[:, :], "cld")
        p.dma("sp", cb4, cb4.ap, None, c_bs4[:, :], "cld")
        p.dma("sp", qscb, qscb.ap, None, qsc[:, :], "cld")
        p.dma("sp", cpm, cpm.ap, None, c_perm[:, :], "cld")
        p.dma("sp", ch64, ch64.ap, None, c_hs64[:, :], "cld")
        p.op("dve", lambda e: e.memset(onesc.ap, 1.0), writes=[onesc])
        p.op("dve", lambda e: e.memset(onesr.ap, 1.0), writes=[onesr])
        consts = {"stats_sel": cs, "bsel": cb}
        p.barrier()

        attn_stack = contextlib.ExitStack()
        QH = [sbuf(p, attn_stack, f"QH{g}", [128, EXT], BF16) for g in range(4)]
        KH = [sbuf(p, attn_stack, f"KH{g}", [128, EXT], BF16) for g in range(4)]
        V = [sbuf(p, attn_stack, f"V{t}", [128, 512], BF16) for t in range(NQT)]
        Mb = sbuf(p, attn_stack, "Mb", [128, 8 * NQT], F32)

        def shifted_rhs(c, t0, tl):
            off = 1 if c < 4 else 0
            return y_in[c * 128:(c + 1) * 128, t0 + off:t0 + off + tl]

        with contextlib.ExitStack() as st:
            wb = [sbuf(p, st, f"wb{i}", [128, 1536], BF16) for i in range(NCHUNK)]
            yb = [sbuf(p, st, f"yb{i}", [128, 8 * 512], BF16) for i in range(2)]
            AR = [sbuf(p, st, f"AR{g}", [128, 512], BF16) for g in range(2)]
            BR = [sbuf(p, st, f"BR{g}", [128, 512], BF16) for g in range(2)]
            pP = [psum(p, st, f"pP{i}", [128, 512]) for i in range(2)]
            pN = psum(p, st, "pN", [8, 512])
            pR = psum(p, st, "pR", [128, 512])
            sq = [sbuf(p, st, f"sq{i}", [128, 512], F32) for i in range(2)]
            rstd = sbuf(p, st, "rstd", [8, 512], F32)
            rb = sbuf(p, st, "rb", [128, 512], F32)
            rbA = sbuf(p, st, "rbA", [128, 512], F32)
            rbB = sbuf(p, st, "rbB", [128, 512], F32)
            tA = sbuf(p, st, "tA", [128, 512], F32)
            tB = sbuf(p, st, "tB", [128, 512], F32)
            u1 = sbuf(p, st, "u1", [128, 512], F32)
            u2 = sbuf(p, st, "u2", [128, 512], F32)
            cbuf = [sbuf(p, st, f"cbuf{i}", [128, 512], F32) for i in range(2)]
            for i in range(NCHUNK):
                p.dma("pool", wb[i], wb[i].ap, None,
                      wqkvp[i * 128:(i + 1) * 128, :], "wld")
            p.barrier()
            for s_ in range(NS):
                t0 = s_ * 512
                y = yb[s_ % 2]
                for c in range(NCHUNK):
                    p.dma("pool", y, y[:, c * 512:(c + 1) * 512], None,
                          shifted_rhs(c, t0, 512), "yld", nowaw=(c > 0))
                for sub in range(4):
                    pp = pP[sub % 2]
                    for ic in range(NCHUNK):
                        p.op("pe", lambda e, sub=sub, ic=ic, pp=pp, y=y: e.matmul(
                            pp.ap,
                            y[:, ic * 512 + sub * 128:ic * 512 + sub * 128 + 128],
                            wb[ic][:, 1024:1536],
                            start=(ic == 0), stop=(ic == NCHUNK - 1)),
                            reads=[wb[ic], y], writes=[pp], acc=(ic > 0))
                    vt = V[s_ * 4 + sub]
                    p.op("dve", lambda e, vt=vt, pp=pp: e.tensor_copy(
                        vt.ap, pp.ap), reads=[pp], writes=[vt])
                for (kind, c1t, c2t, acol, bcol, HT) in (
                        ("q", c1q, c2q, 0, 1, QH), ("k", c1k, c2k, 2, 3, KH)):
                    base_oc = 0 if kind == "q" else 4
                    for g in range(2):
                        for (ocd, dst) in ((base_oc * 0 + g, AR[g]),
                                           (2 + g, BR[g])):
                            oc_abs = (0 if kind == "q" else 4) + ocd
                            pp = pP[ocd % 2]
                            for ic in range(NCHUNK):
                                p.op("pe", lambda e, oc_abs=oc_abs, ic=ic, pp=pp,
                                     y=y: e.matmul(
                                         pp.ap,
                                         wb[ic][:, oc_abs * 128:(oc_abs + 1) * 128],
                                         y[:, ic * 512:(ic + 1) * 512],
                                         start=(ic == 0), stop=(ic == NCHUNK - 1)),
                                     reads=[wb[ic], y], writes=[pp], acc=(ic > 0))
                            p.op("dve", lambda e, dst=dst, pp=pp: e.tensor_copy(
                                dst.ap, pp.ap), reads=[pp], writes=[dst])
                    for g in range(2):
                        sqq = sq[g % 2]
                        p.op("act", lambda e, g=g, sqq=sqq: e.activation(
                            sqq.ap, AR[g].ap, AF.Square),
                            reads=[AR[g]], writes=[sqq])
                        p.op("pe", lambda e, g=g, sqq=sqq: e.matmul(
                            pN.ap, chp[:, 8 * g:8 * g + 8], sqq.ap,
                            start=(g == 0), stop=False),
                            reads=[chp, sqq], writes=[pN], acc=(g > 0))
                        p.op("act", lambda e, g=g, sqq=sqq: e.activation(
                            sqq.ap, BR[g].ap, AF.Square),
                            reads=[BR[g]], writes=[sqq])
                        p.op("pe", lambda e, g=g, sqq=sqq: e.matmul(
                            pN.ap, chp[:, 8 * g:8 * g + 8], sqq.ap,
                            start=False, stop=(g == 1)),
                            reads=[chp, sqq], writes=[pN], acc=True)
                    tinyc, tinyb = p.fconstn(1e-24, 8)
                    p.op("act", lambda e: e.activation(rstd.ap, pN.ap, AF.Ln,
                                                       bias=tinyc),
                         reads=[pN, tinyb], writes=[rstd])
                    p.op("act", lambda e: e.activation(rstd.ap, rstd.ap, AF.Exp,
                                                       scale=-0.5),
                         reads=[rstd], writes=[rstd])
                    if dbg:
                        kk_ = 0 if kind == "q" else 1
                        p.dma("sp", None,
                              rdbg[:, (s_ * 2 + kk_) * 512:(s_ * 2 + kk_ + 1) * 512],
                              rstd, rstd.ap, "dbg")
                        p.dma("sp", None,
                              ardbg[:, (s_ * 2 + kk_) * 512:(s_ * 2 + kk_ + 1) * 512],
                              AR[0], AR[0].ap, "dbg")
                    for i, cc in enumerate((c1t, c2t)):
                        p.dma("sp", cbuf[i], cbuf[i].ap, None,
                              cc[:, t0:t0 + 512], "cbl")
                    for g in range(2):
                        p.op("pe", lambda e, g=g: e.matmul(
                            pR.ap, cb4[:, 128 * g:128 * (g + 1)], rstd.ap,
                            start=True, stop=True), reads=[cb4, rstd], writes=[pR])
                        p.op("dve", lambda e: e.tensor_copy(rb.ap, pR.ap),
                             reads=[pR], writes=[rb])
                        p.op("dve", lambda e, acol=acol: e.tensor_scalar(
                            rbA.ap, rb.ap, qscb[:, acol:acol + 1], None, OP.mult),
                            reads=[rb, qscb], writes=[rbA])
                        p.op("dve", lambda e, bcol=bcol: e.tensor_scalar(
                            rbB.ap, rb.ap, qscb[:, bcol:bcol + 1], None, OP.mult),
                            reads=[rb, qscb], writes=[rbB])
                        p.op("dve", lambda e, g=g: e.tensor_mul(
                            tA.ap, AR[g].ap, rbA.ap), reads=[AR[g], rbA],
                            writes=[tA])
                        p.op("dve", lambda e, g=g: e.tensor_mul(
                            tB.ap, BR[g].ap, rbB.ap), reads=[BR[g], rbB],
                            writes=[tB])
                        p.op("dve", lambda e: e.tensor_mul(u1.ap, tA.ap,
                                                           cbuf[0].ap),
                             reads=[tA, cbuf[0]], writes=[u1])
                        p.op("dve", lambda e: e.tensor_mul(u2.ap, tB.ap,
                                                           cbuf[1].ap),
                             reads=[tB, cbuf[1]], writes=[u2])
                        p.op("dve", lambda e, g=g: e.tensor_sub(
                            AR[g].ap, u1.ap, u2.ap), reads=[u1, u2],
                            writes=[AR[g]])
                        if dbg and g == 0:
                            kk_ = 0 if kind == "q" else 1
                            p.dma("sp", None,
                                  apdbg[:, (s_ * 2 + kk_) * 512:
                                        (s_ * 2 + kk_ + 1) * 512],
                                  AR[0], AR[0].ap, "dbg")
                            p.dma("sp", None,
                                  cbdbg[:, (s_ * 2 + kk_) * 512:
                                        (s_ * 2 + kk_ + 1) * 512],
                                  cbuf[0], cbuf[0].ap, "dbg")
                            p.dma("sp", None,
                                  tadbg[:, (s_ * 2 + kk_) * 512:
                                        (s_ * 2 + kk_ + 1) * 512],
                                  tA, tA.ap, "dbg")
                            p.dma("sp", None,
                                  rbdbg[:, (s_ * 2 + kk_) * 512:
                                        (s_ * 2 + kk_ + 1) * 512],
                                  rbA, rbA.ap, "dbg")
                        p.op("dve", lambda e: e.tensor_mul(u1.ap, tB.ap,
                                                           cbuf[0].ap),
                             reads=[tB, cbuf[0]], writes=[u1])
                        p.op("dve", lambda e: e.tensor_mul(u2.ap, tA.ap,
                                                           cbuf[1].ap),
                             reads=[tA, cbuf[1]], writes=[u2])
                        p.op("dve", lambda e, g=g: e.tensor_add(
                            BR[g].ap, u1.ap, u2.ap), reads=[u1, u2],
                            writes=[BR[g]])
                        for u in range(2):
                            p.op("pe", lambda e, g=g, u=u: e.matmul(
                                pR.ap, cpm[:, u * 128:(u + 1) * 128], AR[g].ap,
                                start=True, stop=False),
                                reads=[cpm, AR[g]], writes=[pR])
                            p.op("pe", lambda e, g=g, u=u: e.matmul(
                                pR.ap, cpm[:, 256 + u * 128:256 + (u + 1) * 128],
                                BR[g].ap, start=False, stop=True),
                                reads=[cpm, BR[g]], writes=[pR], acc=True)
                            p.op("dve", lambda e, HT=HT, g=g, u=u, t0=t0:
                                 e.tensor_copy(HT[2 * g + u][:, t0:t0 + 512],
                                               pR.ap),
                                 reads=[pR], writes=[HT[2 * g + u]])
            p.barrier()
        if dbg:
            for g in range(4):
                p.dma("sp", None, qdbg[g * 128:(g + 1) * 128, :], QH[g],
                      QH[g].ap, "dbg")
                p.dma("sp", None, kdbg[g * 128:(g + 1) * 128, :], KH[g],
                      KH[g].ap, "dbg")
            for t in range(NQT):
                p.dma("sp", None, vdbg[t * 128:(t + 1) * 128, :], V[t],
                      V[t].ap, "dbg")
            p.wait_all("sp", "dbg")
            p.barrier()

        if phases < 2:
            attn_stack.close()
            blk = stack.enter_context(nc.Block())
            p.play(blk)
            return nc

        with contextlib.ExitStack() as st:
            KW = sbuf(p, st, "KW", [128, 4 * NQT], F32)
            QM = sbuf(p, st, "QM", [128, 4 * NQT], F32)
            prod = sbuf(p, st, "prod", [128, NQT], BF16)
            negM = sbuf(p, st, "negM", [8, NQT], F32)
            pM8 = psum(p, st, "pM8", [8, NQT])
            pMb = psum(p, st, "pMb", [128, 8 * NQT])
            for g in range(4):
                p.op("dve", lambda e, g=g: e.tensor_reduce(
                    KW[:, g * NQT:(g + 1) * NQT],
                    KH[g].t[:].rearrange("p (m q) -> p m q", m=NQT),
                    AX.X, OP.max, apply_absolute_value=True),
                    reads=[KH[g]], writes=[KW])
                p.op("dve", lambda e, g=g: e.tensor_reduce(
                    QM[:, g * NQT:(g + 1) * NQT],
                    QH[g].t[:].rearrange("p (m q) -> p m q", m=NQT),
                    AX.X, OP.max, apply_absolute_value=True),
                    reads=[QH[g]], writes=[QM])
                for dd in range(1, 5):
                    p.op("dve", lambda e, g=g, dd=dd: e.tensor_max(
                        KW[:, g * NQT + dd:(g + 1) * NQT],
                        KW[:, g * NQT + dd:(g + 1) * NQT],
                        KW[:, g * NQT:(g + 1) * NQT - dd]),
                        reads=[KW], writes=[KW])
            for g in range(4):
                p.op("dve", lambda e, g=g: e.tensor_mul(
                    prod.ap, QM[:, g * NQT:(g + 1) * NQT],
                    KW[:, g * NQT:(g + 1) * NQT]),
                    reads=[QM, KW], writes=[prod])
                p.op("pe", lambda e, g=g: e.matmul(
                    pM8.ap, ch64[:, 8 * g:8 * g + 8], prod.ap,
                    start=(g == 0), stop=(g == 3)),
                    reads=[ch64, prod], writes=[pM8], acc=(g > 0))
            p.op("dve", lambda e: e.tensor_scalar(
                negM.ap, pM8.ap, -1.0, 68.0, OP.mult, OP.add),
                reads=[pM8], writes=[negM])
            for h in range(8):
                p.op("pe", lambda e, h=h: e.matmul(
                    pMb[:, NQT * h:NQT * (h + 1)], cb[:, 128 * h:128 * (h + 1)],
                    negM.ap, start=True, stop=True),
                    reads=[cb, negM], writes=[pMb], acc=(h > 0))
            p.op("dve", lambda e: e.tensor_copy(Mb.ap, pMb.ap),
                 reads=[pMb], writes=[Mb])
            if dbg:
                p.dma("sp", None, mdbg[:, :], Mb, Mb.ap, "dbg")
                p.wait_all("sp", "dbg")
            p.barrier()

        if phases < 3:
            attn_stack.close()
            blk = stack.enter_context(nc.Block())
            p.play(blk)
            return nc

        OC = [sbuf(p, attn_stack, f"OC{c}", [128, EXT], BF16) for c in range(4)]
        with contextlib.ExitStack() as st:
            pS = psum(p, st, "pS", [128, 4 * 512])
            pZ = psum(p, st, "pZ", [1, 512])
            pO = [psum(p, st, f"pO{i}", [128, 256]) for i in range(2)]
            pB = psum(p, st, "pB", [128, 128])
            PH = sbuf(p, st, "PH", [128, 5 * 1024], BF16)
            mbuf = sbuf(p, st, "mbuf", [128, 5 * 128], BF16)
            rz0 = sbuf(p, st, "rz0", [1, 512], F32)
            rz1 = sbuf(p, st, "rz1", [1, 512], F32)
            rbb = sbuf(p, st, "rbb", [128, 128], F32)
            for m in range(min(NQT, p3m)):
                kbs = list(range(max(0, m - 4), m + 1))
                nk = len(kbs)
                q0 = m * 128
                p.dma("sp", mbuf, mbuf[:, 0:nk * 128], None,
                      masks[:, (m * 5) * 128:(m * 5 + nk) * 128], "mld")
                for g2 in range(2):
                    for ci, lo in enumerate(range(0, nk, 4)):
                        hi = min(nk, lo + 4)
                        for kb_i in range(lo, hi):
                            kb = kbs[kb_i]
                            k0 = kb * 128
                            for hp in range(4):
                                h = 4 * g2 + hp
                                ch, off = h // 2, 64 * (h % 2)
                                p.op("pe", lambda e, ch=ch, off=off, k0=k0,
                                     q0=q0, kb_i=kb_i, lo=lo, hp=hp: e.matmul(
                                         pS[:, hp * 512 + (kb_i - lo) * 128:
                                            hp * 512 + (kb_i - lo) * 128 + 128],
                                         KH[ch][off:off + 64, k0:k0 + 128],
                                         QH[ch][off:off + 64, q0:q0 + 128],
                                         start=True, stop=True),
                                     reads=[KH[ch], QH[ch]], writes=[pS])
                        if not (p3bits & 1):
                            continue
                        for hp in range(4):
                            h = 4 * g2 + hp
                            col = h * NQT + m
                            nkb = hi - lo
                            src = bass.AP(pS.t, hp * 512,
                                          [[4 * 512, 128], [128, nkb], [1, 128]])
                            dst = bass.AP(PH.t, lo * 1024 + g2 * 512 + hp * 128,
                                          [[5 * 1024, 128], [1024, nkb], [1, 128]])
                            p.op("act", lambda e, src=src, dst=dst, col=col:
                                 e.activation(dst, src, AF.Exp,
                                              bias=Mb[:, col:col + 1]),
                                 reads=[pS, Mb], writes=[PH])
                if p3bits & 2:
                    for kb_i in range(nk):
                        phv = PH.t[:, kb_i * 1024:(kb_i + 1) * 1024].rearrange(
                            "p (h q) -> p h q", h=8)
                        p.op("dve", lambda e, phv=phv, kb_i=kb_i: e.tensor_mul(
                            phv, phv,
                            mbuf[:, kb_i * 128:(kb_i + 1) * 128].unsqueeze(1)
                            .to_broadcast([128, 8, 128])),
                            reads=[mbuf, PH], writes=[PH])
                for g2 in range(2):
                    if not (p3bits & 4):
                        break
                    rzb = rz0 if g2 == 0 else rz1
                    for hp in range(4):
                        base = g2 * 512 + hp * 128
                        for kb_i in range(nk):
                            off = kb_i * 1024 + base
                            p.op("pe", lambda e, off=off, hp=hp, kb_i=kb_i,
                                 nk=nk: e.matmul(
                                     pZ[:, 128 * hp:128 * (hp + 1)],
                                     onesc.ap, PH[:, off:off + 128],
                                     start=(kb_i == 0), stop=(kb_i == nk - 1)),
                                 reads=[PH, onesc], writes=[pZ], acc=(kb_i > 0))
                    p.op("dve", lambda e, rzb=rzb: e.tensor_scalar(
                        rzb.ap, pZ.ap, 1e-36, None, OP.add),
                        reads=[pZ], writes=[rzb])
                    p.op("dve", lambda e, rzb=rzb: e.reciprocal(rzb.ap, rzb.ap),
                         reads=[rzb], writes=[rzb])
                for g2 in range(2):
                    if not (p3bits & 8):
                        break
                    po = pO[g2]
                    for hp in range(4):
                        h = 4 * g2 + hp
                        for kb_i, kb in enumerate(kbs):
                            p.op("pe", lambda e, h=h, hp=hp, kb=kb, kb_i=kb_i,
                                 g2=g2, po=po, nk=nk: e.matmul(
                                     po[64 * (hp % 2):64 * (hp % 2) + 64,
                                        128 * (hp // 2):128 * (hp // 2) + 128],
                                     V[kb][:, 64 * h:64 * h + 64],
                                     PH[:, kb_i * 1024 + g2 * 512 + hp * 128:
                                        kb_i * 1024 + g2 * 512 + hp * 128 + 128],
                                     start=(kb_i == 0), stop=(kb_i == nk - 1)),
                                 reads=[V[kb], PH], writes=[po], acc=(kb_i > 0))
                    rzb = rz0 if g2 == 0 else rz1
                    for pr in range(2):
                        if not (p3bits & 16):
                            break
                        p.op("pe", lambda e, pr=pr, rzb=rzb: e.matmul(
                            pB[0:64, :], onesr[:, 0:64],
                            rzb[:, 128 * (2 * pr):128 * (2 * pr) + 128],
                            start=True, stop=True), reads=[onesr, rzb],
                            writes=[pB])
                        p.op("pe", lambda e, pr=pr, rzb=rzb: e.matmul(
                            pB[64:128, :], onesr[:, 0:64],
                            rzb[:, 128 * (2 * pr + 1):128 * (2 * pr + 1) + 128],
                            start=True, stop=True, tile_position=(0, 64)),
                            reads=[onesr, rzb], writes=[pB], acc=True)
                        p.op("dve", lambda e: e.tensor_copy(rbb.ap, pB.ap),
                             reads=[pB], writes=[rbb])
                        c = 2 * g2 + pr
                        p.op("dve", lambda e, c=c, q0=q0, po=po, pr=pr:
                             e.tensor_mul(OC[c][:, q0:q0 + 128],
                                          po[:, 128 * pr:128 * (pr + 1)], rbb.ap),
                             reads=[po, rbb], writes=[OC[c]])
            p.barrier()
            if dbg:
                for c in range(4):
                    p.dma("sp", None, odbg[c * 128:(c + 1) * 128, :], OC[c],
                          OC[c].ap, "dbg")
                p.wait_all("sp", "dbg")
                p.barrier()
        if p3wo:
            with contextlib.ExitStack() as st:
                wob = [sbuf(p, st, f"wob{i}", [128, D], BF16) for i in range(4)]
                ysl = sbuf(p, st, "ysl", [128, 512], F32)
                xms = sbuf(p, st, "xms", [128, 512], F32)
                pX = [psum(p, st, f"pX{i}", [128, 512]) for i in range(2)]
                zc = sbuf(p, st, "zc", [128, 1], F32)
                p.op("dve", lambda e: e.memset(zc.ap, 0.0), writes=[zc])
                for i in range(4):
                    p.dma("pool", wob[i], wob[i].ap, None,
                          wo[i * 128:(i + 1) * 128, :], "wld")
                for c in range(NCHUNK):
                    p.dma("sp", None, xmid[c * 128:(c + 1) * 128, 0:1], zc, zc.ap,
                          "xms0", allow_slow_non_contiguous=True)
                p.barrier()
                for s_ in range(NS):
                    t0 = s_ * 512
                    for oc in range(NCHUNK):
                        px = pX[oc % 2]
                        for ic in range(4):
                            p.op("pe", lambda e, ic=ic, oc=oc, t0=t0, px=px:
                                 e.matmul(px.ap,
                                          wob[ic][:, oc * 128:(oc + 1) * 128],
                                          OC[ic][:, t0:t0 + 512],
                                          start=(ic == 0), stop=(ic == 3)),
                                 reads=[wob[ic], OC[ic]], writes=[px],
                                 acc=(ic > 0))
                        p.dma("sp", ysl, ysl.ap, None,
                              y_in[oc * 128:(oc + 1) * 128, t0 + 1:t0 + 513],
                              "yld2")
                        p.op("dve", lambda e, px=px: e.tensor_add(
                            xms.ap, px.ap, ysl.ap), reads=[px, ysl], writes=[xms])
                        p.dma("sp", None, xmid[oc * 128:(oc + 1) * 128,
                                               t0 + 1:t0 + 513], xms, xms.ap,
                              "xmst")
                p.wait_all("sp", "xmst")
                p.barrier()
        attn_stack.close()

        if phases < 4:
            blk = stack.enter_context(nc.Block())
            p.play(blk)
            return nc
        lngc = sbuf(p, stack, "lngc", [128, NCHUNK], F32)
        lnbc = sbuf(p, stack, "lnbc", [128, NCHUNK], F32)
        for c in range(NCHUNK):
            p.dma("sp", lngc, lngc[:, c:c + 1], None,
                  lng[c * 128:(c + 1) * 128, :], "cld")
            p.dma("sp", lnbc, lnbc[:, c:c + 1], None,
                  lnb[c * 128:(c + 1) * 128, :], "cld")
        p.barrier()
        emit_ln_stream(p, stack, "fln", xmid, hffn, EXT, consts,
                       col_off=[1] * 4 + [0] * 4, gamma=lngc, beta=lnbc)

        ACH = chunks_of(2730, 128)
        with contextlib.ExitStack() as st:
            hf = [sbuf(p, st, f"hf{j}", [128, 2048], BF16) for j in range(22)]
            hx = [sbuf(p, st, f"hx{i}", [128, 8 * 512], BF16) for i in range(2)]
            w1a = [sbuf(p, st, f"w1a{i}", [128, 8 * 128], BF16) for i in range(2)]
            w1g = [sbuf(p, st, f"w1g{i}", [128, 8 * 128], BF16) for i in range(2)]
            w2b = [sbuf(p, st, f"w2b{i}", [128, 22 * 128], BF16) for i in range(2)]
            b1a = sbuf(p, st, "b1a", [128, 22], F32)
            b1g = sbuf(p, st, "b1g", [128, 22], F32)
            b2c = sbuf(p, st, "b2c", [128, NCHUNK], F32)
            pA = psum(p, st, "pA", [128, 512])
            pG = psum(p, st, "pG", [128, 512])
            pX2 = [psum(p, st, f"pX2{i}", [128, 512]) for i in range(2)]
            gg = sbuf(p, st, "gg", [128, 512], BF16)
            aa = sbuf(p, st, "aa", [128, 512], BF16)
            xms2 = sbuf(p, st, "xms2", [128, 512], F32)
            xout = sbuf(p, st, "xout", [128, 512], F32)
            for bb, boff in ((b1a, 0), (b1g, 2730)):
                p.dma("sp", bb, bb.t[:, 0:21].rearrange("p (j o) -> p j o", j=21),
                      None, b1[boff:boff + 2688, :].rearrange(
                          "(j p) o -> p j o", p=128), "cld",
                      allow_slow_non_contiguous=True)
                p.dma("sp", bb, bb[0:42, 21:22], None,
                      b1[boff + 2688:boff + 2730, :], "cld",
                      allow_slow_non_contiguous=True, nowaw=True)
            for c in range(NCHUNK):
                p.dma("sp", b2c, b2c[:, c:c + 1], None,
                      b2[c * 128:(c + 1) * 128, :], "cld")
            p.barrier()
            hxm = sbuf(p, st, "hxm", [128, 8 * 2048], BF16)
            xmb = sbuf(p, st, "xmb", [128, 512], F32)
            for macro in range(2):
                mb0 = macro * 2048
                for c in range(NCHUNK):
                    p.dma("sp", hxm, hxm[:, c * 2048:(c + 1) * 2048], None,
                          hffn[c * 128:(c + 1) * 128, mb0:mb0 + 2048], "hxl",
                          nowaw=(c > 0))
                for j, (a0, sz) in enumerate(ACH):
                    w1a_ = w1a[j % 2]
                    w1g_ = w1g[j % 2]
                    va = w1a_.t[:, 0:8 * sz].rearrange("p (i o) -> p i o", i=8)
                    p.dma("pool", w1a_, va, None,
                          w1[:, a0:a0 + sz].rearrange("(i p) o -> p i o", p=128),
                          "w1l")
                    p.dma("pool", w1g_, w1g_.t[:, 0:8 * sz].rearrange(
                        "p (i o) -> p i o", i=8), None,
                        w1[:, 2730 + a0:2730 + a0 + sz].rearrange(
                            "(i p) o -> p i o", p=128), "w1l")
                    for tt in range(4):
                        for ic in range(NCHUNK):
                            p.op("pe", lambda e, ic=ic, tt=tt, sz=sz, w1a_=w1a_:
                                 e.matmul(pA[0:sz, :],
                                          w1a_[:, ic * sz:(ic + 1) * sz],
                                          hxm[:, ic * 2048 + tt * 512:
                                              ic * 2048 + (tt + 1) * 512],
                                          start=(ic == 0), stop=(ic == 7)),
                                 reads=[w1a_, hxm], writes=[pA], acc=(ic > 0))
                        for ic in range(NCHUNK):
                            p.op("pe", lambda e, ic=ic, tt=tt, sz=sz, w1g_=w1g_:
                                 e.matmul(pG[0:sz, :],
                                          w1g_[:, ic * sz:(ic + 1) * sz],
                                          hxm[:, ic * 2048 + tt * 512:
                                              ic * 2048 + (tt + 1) * 512],
                                          start=(ic == 0), stop=(ic == 7)),
                                 reads=[w1g_, hxm], writes=[pG], acc=(ic > 0))
                        p.op("act", lambda e, sz=sz, j=j: e.activation(
                            aa[0:sz, :], pA[0:sz, :], AF.Identity,
                            bias=b1a[0:sz, j:j + 1]),
                            reads=[pA, b1a], writes=[aa])
                        p.op("act", lambda e, sz=sz, j=j: e.activation(
                            gg[0:sz, :], pG[0:sz, :], AF.Gelu,
                            bias=b1g[0:sz, j:j + 1]),
                            reads=[pG, b1g], writes=[gg])
                        p.op("dve", lambda e, sz=sz, j=j, tt=tt: e.tensor_mul(
                            hf[j][0:sz, tt * 512:(tt + 1) * 512],
                            aa[0:sz, :], gg[0:sz, :]),
                            reads=[aa, gg], writes=[hf[j]])
                for oc in range(NCHUNK):
                    w2_ = w2b[oc % 2]
                    p.dma("pool", w2_,
                          w2_.t[:, 0:21 * 128].rearrange("p (j o) -> p j o", j=21),
                          None,
                          w2[0:2688, oc * 128:(oc + 1) * 128].rearrange(
                              "(j p) o -> p j o", p=128), "w2l")
                    p.dma("pool", w2_, w2_[0:42, 21 * 128:22 * 128], None,
                          w2[2688:2730, oc * 128:(oc + 1) * 128], "w2l",
                          nowaw=True)
                    for tt in range(4):
                        px = pX2[tt % 2]
                        for j, (a0, sz) in enumerate(ACH):
                            p.op("pe", lambda e, j=j, sz=sz, tt=tt, w2_=w2_, px=px:
                                 e.matmul(px.ap,
                                          w2_[0:sz, j * 128:(j + 1) * 128],
                                          hf[j][0:sz, tt * 512:(tt + 1) * 512],
                                          start=(j == 0), stop=(j == 21)),
                                 reads=[w2_, hf[j]], writes=[px], acc=(j > 0))
                        p.op("act", lambda e, oc=oc, px=px: e.activation(
                            xms2.ap, px.ap, AF.Identity, bias=b2c[:, oc:oc + 1]),
                            reads=[px, b2c], writes=[xms2])
                        p.dma("sp", xmb, xmb.ap, None,
                              xmid[oc * 128:(oc + 1) * 128,
                                   mb0 + tt * 512 + 1:mb0 + (tt + 1) * 512 + 1],
                              "xmbl")
                        p.op("dve", lambda e: e.tensor_add(
                            xout.ap, xms2.ap, xmb.ap),
                            reads=[xms2, xmb], writes=[xout])
                        p.dma("sp", None,
                              x_out[oc * 128:(oc + 1) * 128,
                                    mb0 + tt * 512:mb0 + (tt + 1) * 512],
                              xout, xout.ap, "xot")
            p.wait_all("sp", "xot")
            p.wait_all("pool", "xot")
            p.barrier()
        if standalone:
            blk = stack.enter_context(nc.Block())
            p.play(blk)
    return nc


def host_consts_l0():
    hsp = np.zeros((128, 16), np.float32)
    for g in range(2):
        for j in range(4):
            hsp[32 * j:32 * (j + 1), 8 * g + 4 * g + j] = 1.0
    bs4 = np.zeros((8, 256), np.float32)
    for g in range(2):
        for pp in range(128):
            bs4[4 * g + pp // 32, 128 * g + pp] = 1.0
    perm = np.zeros((128, 512), np.float32)
    for u in range(2):
        for j in range(2):
            for n in range(32):
                k = 64 * u + 32 * j + n
                perm[k, 128 * u + 64 * j + n] = 1.0
                perm[k, 256 + 128 * u + 64 * j + 32 + n] = 1.0
    hs64 = np.zeros((128, 32), np.float32)
    for g in range(4):
        for j in range(2):
            hs64[64 * j:64 * (j + 1), 8 * g + 2 * g + j] = 1.0
    return {"c_hsp": hsp, "c_bs4": bs4, "c_perm": perm.astype(BF),
            "c_hs64": hs64.astype(BF)}


def permute_wqkv(w):
    cols = []
    for base in (0, 512):
        for half in (0, 32):
            for h in range(8):
                cols.extend(range(base + h * 64 + half, base + h * 64 + half + 32))
    cols.extend(range(1024, 1536))
    return np.ascontiguousarray(w[:, cols])


def rotary_tables(goff):
    j = np.arange(32, dtype=np.float64)
    inv_freq = 1.0 / 10000.0 ** (2 * j / 64)
    base = (2 * j + 0.4 * 64) / (1.4 * 64)
    t = np.arange(goff, goff + EXT, dtype=np.float64)
    freqs = t[:, None] * inv_freq[None, :]
    power = (t - 2048.0) / 256.0
    scale = base[None, :] ** power[:, None]
    cos, sin = np.cos(freqs), np.sin(freqs)
    c1q = (8.0 * cos * scale).T.astype(np.float32)
    c2q = (8.0 * sin * scale).T.astype(np.float32)
    c1k = (cos / scale).T.astype(np.float32)
    c2k = (sin / scale).T.astype(np.float32)
    tile = lambda a: np.ascontiguousarray(np.tile(a, (4, 1)))
    return tile(c1q), tile(c2q), tile(c1k), tile(c2k)


def l0_masks(goff):
    out = np.zeros((128, NQT * 5 * 128), np.float32)
    for m in range(NQT):
        kbs = list(range(max(0, m - 4), m + 1))
        gq = goff + m * 128 + np.arange(128)[None, :]
        all_garbage = (goff + m * 128 + 127) < 0
        for kb_i, kb in enumerate(kbs):
            gk = goff + kb * 128 + np.arange(128)[:, None]
            if all_garbage:
                mk = np.ones((128, 128), np.float32)
            else:
                mk = ((gq >= gk) & (gq - gk <= 512) & (gk >= 0)).astype(np.float32)
            out[:, (m * 5 + kb_i) * 128:(m * 5 + kb_i + 1) * 128] = mk
    return out.astype(BF)


def qsc_cols(q_scale, k_scale):
    p = np.arange(128) % 32
    return np.ascontiguousarray(np.stack(
        [q_scale[p], q_scale[32 + p], k_scale[p], k_scale[32 + p]],
        axis=1).astype(np.float32))



def prep_mlp_w(WT, n_in_chunks, n_out_blocks):
    tmp = WT.reshape(n_in_chunks, 128, n_out_blocks, 128)
    return np.ascontiguousarray(
        tmp.transpose(2, 1, 0, 3).reshape(n_out_blocks * 128, n_in_chunks * 128))





_BUILD_CACHE = {}


def _get(name, fn):
    if name not in _BUILD_CACHE:
        _BUILD_CACHE[name] = fn()
    return _BUILD_CACHE[name]


def _run(nc, maps):
    return run_bass_kernel_spmd(nc, maps, core_ids=list(range(8)))


def kernel_multilaunch(**inp):
    consts = host_consts_s2()
    cl0 = host_consts_l0()
    c5 = host_consts_s5()
    f32 = np.float32
    C = lambda a: np.ascontiguousarray(np.asarray(a, f32))
    col = lambda a: np.ascontiguousarray(np.asarray(a, f32).reshape(-1, 1))

    x = np.asarray(inp["L0_attn_x"], f32)
    xfm = [np.ascontiguousarray(fm(x[b])) for b in range(B)]

    rot = {}
    for hh in range(2):
        goff = -2048 if hh == 0 else 0
        rot[hh] = (KK_rotary := rotary_tables(goff), l0_masks(goff))
    ident = np.eye(128).astype(BF)

    def l0ffn_sublayer():
        nc_ln = _get("l0ln", build_l0ln)
        maps = []
        for c in range(8):
            b, hh = c // 2, c % 2
            if hh == 0:
                x_ext = np.concatenate(
                    [np.zeros((D, 2048), f32), xfm[b][:, 0:2048]], axis=1)
            else:
                x_ext = xfm[b]
            maps.append({"x_ext": C(x_ext), "c_stats": consts["c_stats"],
                         "c_bsel": consts["c_bsel"]})
        res = _run(nc_ln, maps)
        ycur = [np.array(res.results[c]["y0"]) for c in range(8)]
        for c in range(8):
            if c % 2 == 0:
                ycur[c][:, 0:2049] = 0.0
        nc_in = _get("l0inner", build_l0inner)
        for l in range(2):
            maps = []
            for c in range(8):
                b, hh = c // 2, c % 2
                (c1q, c2q, c1k, c2k), msk = rot[hh]
                maps.append({
                    "y_in": ycur[c],
                    "wqkvp": permute_wqkv(C(inp["L0_ffn_wqkv"][l])),
                    "qsc": qsc_cols(C(inp["L0_ffn_q_scale"][l]),
                                    C(inp["L0_ffn_k_scale"][l])),
                    "c1q": c1q, "c2q": c2q, "c1k": c1k, "c2k": c2k,
                    "wo": C(inp["L0_ffn_wo"][l]),
                    "lng": col(inp["L0_ffn_ln_g"][l]),
                    "lnb": col(inp["L0_ffn_ln_b"][l]),
                    "w1": C(inp["L0_ffn_w1"][l]),
                    "b1": col(inp["L0_ffn_b1"][l]),
                    "w2": C(inp["L0_ffn_w2"][l]),
                    "b2": col(inp["L0_ffn_b2"][l]),
                    "masks": msk,
                    "c_stats": consts["c_stats"], "c_bsel": consts["c_bsel"],
                    "c_hsp": cl0["c_hsp"], "c_bs4": cl0["c_bs4"],
                    "c_perm": cl0["c_perm"], "c_hs64": cl0["c_hs64"],
                })
            res = _run(nc_in, maps)
            ynext = []
            for c in range(8):
                yn = np.zeros((D, EXT + 1), f32)
                yn[:, 1:] = res.results[c]["x_out"]
                if c % 2 == 0:
                    yn[:, 0:2049] = 0.0
                ynext.append(yn)
            ycur = ynext
        nc_cb = _get("l0comb", build_l0comb)
        maps = []
        for c in range(8):
            b, hh = c // 2, c % 2
            tc = slice(T * hh, T * (hh + 1))
            maps.append({"x_own": C(xfm[b][:, tc]),
                         "y2": C(ycur[c][:, 2049:4097])})
        res = _run(nc_cb, maps)
        for c in range(8):
            b, hh = c // 2, c % 2
            xfm[b][:, T * hh:T * (hh + 1)] = res.results[c]["x_new"]

    def l0attn_sublayer():
        nc_a = _get("s2a", build_s2a)
        maps = []
        for c in range(8):
            b, hh = c // 2, c % 2
            cols = slice(512 * hh, 512 * (hh + 1))
            maps.append({
                "x_full": C(xfm[b]),
                "wq": C(inp["L0_attn_W_q"][:, cols]),
                "wk": C(inp["L0_attn_W_k"][:, cols]),
                "wv": C(inp["L0_attn_W_v"][:, cols]),
                "bq": col(inp["L0_attn_b_q"][cols]),
                "bk": col(inp["L0_attn_b_k"][cols]),
                "bv": col(inp["L0_attn_b_v"][cols]),
                "c_stats": consts["c_stats"], "c_bsel": consts["c_bsel"],
                "c_hsel": consts["c_hsel"], "c_dmask": consts["c_dmask"],
            })
        res_a = _run(nc_a, maps)
        nc_b = _get("s2b", build_s2b)
        maps = []
        for c in range(8):
            b, hh = c // 2, c % 2
            o_pair = np.concatenate([res_a.results[2 * b]["o_un"],
                                     res_a.results[2 * b + 1]["o_un"]], axis=0)
            zr_pair = np.concatenate([res_a.results[2 * b]["o_zr"],
                                      res_a.results[2 * b + 1]["o_zr"]], axis=0)
            tc = slice(T * hh, T * (hh + 1))
            maps.append({
                "x_own": C(xfm[b][:, tc]),
                "o_sl": np.ascontiguousarray(o_pair[:, tc]),
                "zr": np.ascontiguousarray(zr_pair[:, tc]),
                "wo": C(inp["L0_attn_W_o"]), "bo": col(inp["L0_attn_b_o"]),
                "c_zsel": consts["c_zsel"],
            })
        res = _run(nc_b, maps)
        for c in range(8):
            b, hh = c // 2, c % 2
            xfm[b][:, T * hh:T * (hh + 1)] = res.results[c]["x_new"]

    def l1ffn_sublayer():
        nc = _get("s4", build_s4)
        keysx = np.zeros((512, 128), f32)
        keys = np.asarray(inp["L1_ffn_keys"], f32)
        for pl in range(2):
            for hh in range(4):
                bidx = 4 * pl + hh
                keysx[64 * bidx:64 * (bidx + 1)] = keys[hh, :, pl, :].T
        tabc = np.ascontiguousarray(np.concatenate(
            [np.asarray(inp["L1_ffn_down_embed"], f32),
             np.asarray(inp["L1_ffn_up_embed"], f32)], axis=1))
        wgT = prep_mlp_w(C(np.asarray(inp["L1_ffn_Wg"], f32).T), 8, 32)
        wuT = prep_mlp_w(C(np.asarray(inp["L1_ffn_Wu"], f32).T), 8, 32)
        wdT = prep_mlp_w(C(np.asarray(inp["L1_ffn_Wd"], f32).T), 32, 8)
        maps = []
        for c in range(8):
            b, hh = c // 2, c % 2
            tc = slice(T * hh, T * (hh + 1))
            maps.append({
                "x_own": C(xfm[b][:, tc]),
                "wqT": C(np.asarray(inp["L1_ffn_Wq"], f32).T), "keysx": keysx,
                "wgT": wgT, "wuT": wuT, "wdT": wdT, "tabc": tabc,
                "c_stats": consts["c_stats"], "c_bsel": consts["c_bsel"],
                "c_ident": ident,
            })
        res = _run(nc, maps)
        for c in range(8):
            b, hh = c // 2, c % 2
            xfm[b][:, T * hh:T * (hh + 1)] = res.results[c]["x_new"]

    def l1attn_sublayer():
        nc_a = _get("s5a", build_s5a)
        maps = []
        for c in range(8):
            b, hh = c // 2, c % 2
            t0 = T * hh
            x5 = np.zeros((D, T + 1), f32)
            x5[:, 1:] = xfm[b][:, t0:t0 + T]
            if hh == 1:
                x5[:, 0] = xfm[b][:, t0 - 1]
            maps.append({
                "x5": x5,
                "cwv": C(np.asarray(inp["L1_attn_conv_w"], f32)[:, 0, :]),
                "cbv": col(inp["L1_attn_conv_b"]),
                "kwT": C(np.asarray(inp["L1_attn_k_w"], f32).T),
                "kbv": col(inp["L1_attn_k_b"]),
                "vwT": C(np.asarray(inp["L1_attn_v_w"], f32).T),
                "vbv": col(inp["L1_attn_v_b"]),
                "c_stats": consts["c_stats"], "c_bsel": consts["c_bsel"],
            })
        res_a = _run(nc_a, maps)
        nc_b = _get("s5b", build_s5b)
        maps = []
        for c in range(8):
            b, hh = c // 2, c % 2
            kf = np.concatenate([res_a.results[2 * b]["o_k"],
                                 res_a.results[2 * b + 1]["o_k"]], axis=1)
            vf = np.concatenate([res_a.results[2 * b]["o_v"],
                                 res_a.results[2 * b + 1]["o_v"]], axis=0)
            tc = slice(T * hh, T * (hh + 1))
            maps.append({
                "x_own": C(xfm[b][:, tc]),
                "lnx": res_a.results[c]["o_lnx"],
                "kf": np.ascontiguousarray(kf), "vf": np.ascontiguousarray(vf),
                "qwT": C(np.asarray(inp["L1_attn_q_w"], f32).T),
                "qbv": col(inp["L1_attn_q_b"]),
                "owT": C(np.asarray(inp["L1_attn_out_w"], f32).T),
                "obv": col(inp["L1_attn_out_b"]),
                "c_hs16": c5["c_hs16"], "c_bs16": c5["c_bs16"],
            })
        res = _run(nc_b, maps)
        for c in range(8):
            b, hh = c // 2, c % 2
            xfm[b][:, T * hh:T * (hh + 1)] = res.results[c]["x_new"]

    l0ffn_sublayer()
    l0attn_sublayer()
    l0ffn_sublayer()
    l1ffn_sublayer()
    l1attn_sublayer()
    l1ffn_sublayer()

    out = np.stack([xfm[b].T for b in range(B)], axis=0)
    return np.ascontiguousarray(out.astype(np.float32))


def _ag(p, src_ap, dst_ap, srcb, dstb):
    p.op("pool", lambda e: e.collective_compute(
        "AllGather", OP.bypass, replica_groups=RG_PAIRS,
        ins=[src_ap.opt()], outs=[dst_ap.opt()]),
        reads=[srcb] if srcb else [], writes=[dstb], inc_sem="coll", inc=1)


def emit_pick_half(p, stack, dst, h0ap, h1ap, mcol, nrows, width, dt,
                   srcbufs=()):
    with contextlib.ExitStack() as st:
        a = sbuf(p, st, "ph_a", [128, width], dt)
        b2 = sbuf(p, st, "ph_b", [128, width], dt)
        u = sbuf(p, st, "ph_u", [128, width], F32)
        o = sbuf(p, st, "ph_o", [128, width], dt)
        nck = ceil_div(nrows, 128)
        for c in range(nck):
            rr = min(128, nrows - c * 128)
            p.dma("sp", a, a[0:rr, :], None, h0ap(c, rr), "phl")
            p.dma("sp", b2, b2[0:rr, :], None, h1ap(c, rr), "phl")
            p.op("dve", lambda e, rr=rr: e.tensor_scalar(
                u[0:rr, :], a[0:rr, :], mcol[0:rr, 0:1], None, OP.mult),
                reads=[a, mcol], writes=[u])
            p.op("dve", lambda e, rr=rr: e.scalar_tensor_tensor(
                o[0:rr, :], b2[0:rr, :], mcol[0:rr, 1:2], u[0:rr, :],
                OP.mult, OP.add), reads=[b2, mcol, u], writes=[o])
            p.dma("sp", None, dst[c * 128:c * 128 + rr, :], o, o[0:rr, :], "phs")
        p.wait_all("sp", "phs")
        p.barrier()


def build_fused():
    nc = bass.Bass("TRN2", target_bir_lowering=False)
    with contextlib.ExitStack() as ostack:
        p = Prog(nc, ostack)
        p.fconst(0.0)
        ctx = _Ctx(nc, p)
        x_in = Buf(nc.dram_tensor("x_own", [D, T], F32, kind="ExternalInput"))
        c_half = Buf(nc.dram_tensor("c_half", [128, 2], F32, kind="ExternalInput"))
        x_final = Buf(nc.dram_tensor("x_final", [D, T], F32, kind="ExternalOutput"))
        xst = Buf(nc.dram_tensor("xst", [D, T], F32))
        agx = Buf(nc.dram_tensor("agx", [2 * D, T], F32))
        xf = Buf(nc.dram_tensor("xf", [D, S], F32))
        mcol = sbuf(p, ostack, "mcol", [128, 2], F32)
        zc8 = sbuf(p, ostack, "zc8", [128, 1], F32)
        p.dma("sp", mcol, mcol.ap, None, c_half[:, :], "cini")
        p.op("dve", lambda e: e.memset(zc8.ap, 0.0), writes=[zc8])
        p.dma("sp", xst, xst[:, :], x_in, x_in[:, :], "cini")
        p.barrier()

        def refresh_xf():
            _ag(p, xst.t[:, :], agx.t[:, :], xst, agx)
            p.wait_all("pool", "coll")
            p.barrier()
            p.dma("sp", xf, xf[0:D, 0:T], agx, agx[0:D, :], "xfc")
            p.dma("sp", xf, xf[0:D, T:S], agx, agx[D:2 * D, :], "xfc", nowaw=True)
            p.wait_all("sp", "xfc")
            p.barrier()

        def zero_col0(tens):
            for c in range(NCHUNK):
                p.dma("sp", None, tens[c * 128:(c + 1) * 128, 0:1], zc8,
                      zc8.ap, "zc0", allow_slow_non_contiguous=True)
            p.wait_all("sp", "zc0")
            p.barrier()

        def l0ffn(tag):
            refresh_xf()
            y0 = Buf(nc.dram_tensor(_uq(tag + "y0"), [D, EXT + 1], F32))
            y1 = Buf(nc.dram_tensor(_uq(tag + "y1"), [D, EXT + 1], F32))
            y2 = Buf(nc.dram_tensor(_uq(tag + "y2"), [D, EXT + 1], F32))
            y2own = Buf(nc.dram_tensor(_uq(tag + "y2o"), [D, T], F32))
            build_l0ln(ctx, pre=tag + "ln_", io={"x_ext": xf, "y0": y0})
            p.barrier()
            for l, yi, yo in ((0, y0, y1), (1, y1, y2)):
                zero_col0(yo)
                build_l0inner(ctx, pre=f"{tag}i{l}_",
                              io={"y_in": yi,
                                  "x_out": Buf(yo.t[:, 1:EXT + 1], "xoutv")})
                p.barrier()
            emit_pick_half(p, ostack, y2own.t,
                           lambda c, rr: y2[c * 128:c * 128 + rr, 1:T + 1],
                           lambda c, rr: y2[c * 128:c * 128 + rr, T + 1:S + 1],
                           mcol, D, T, F32, srcbufs=(y2,))
            build_l0comb(ctx, pre=tag + "cb_",
                         io={"x_own": xst, "y2": y2own, "x_new": xst})
            p.barrier()

        def l0attn(tag="s2_"):
            refresh_xf()
            o_un = Buf(nc.dram_tensor(_uq(tag + "oun"), [512, S], BF16))
            o_zr = Buf(nc.dram_tensor(_uq(tag + "ozr"), [8, S], F32))
            ago = Buf(nc.dram_tensor(_uq(tag + "ago"), [1024, S], BF16))
            agz = Buf(nc.dram_tensor(_uq(tag + "agz"), [16, S], F32))
            osl = Buf(nc.dram_tensor(_uq(tag + "osl"), [D, T], BF16))
            zsl = Buf(nc.dram_tensor(_uq(tag + "zsl"), [16, T], F32))
            build_s2a(ctx, pre=tag, io={"x_full": xf, "o_un": o_un,
                                        "o_zr": o_zr})
            p.barrier()
            _ag(p, o_un.t[:, :], ago.t[:, :], o_un, ago)
            _ag(p, o_zr.t[:, :], agz.t[:, :], o_zr, agz)
            p.wait_all("pool", "coll")
            p.barrier()
            emit_pick_half(p, ostack, osl.t,
                           lambda c, rr: ago[c * 128:c * 128 + rr, 0:T],
                           lambda c, rr: ago[c * 128:c * 128 + rr, T:S],
                           mcol, 1024, T, BF16, srcbufs=(ago,))
            emit_pick_half(p, ostack, zsl.t,
                           lambda c, rr: agz[c * 16 * 0:rr, 0:T] if False else
                           agz[0:rr, 0:T],
                           lambda c, rr: agz[0:rr, T:S],
                           mcol, 16, T, F32, srcbufs=(agz,))
            build_s2b(ctx, pre=tag, io={"x_own": xst, "o_sl": osl, "zr": zsl,
                                        "x_new": xst})
            p.barrier()

        s4share = {}

        def l1ffn(tag, final=False):
            io = {"x_own": xst, "x_new": x_final if final else xst}
            io.update(s4share)
            build_s4(ctx, pre=tag, io=io)
            for k in ("tabb", "wgbd", "wubd", "wdbd"):
                s4share[k] = io[k]
            p.barrier()

        def l1attn(tag="s5_"):
            x5b = Buf(nc.dram_tensor(_uq(tag + "x5"), [D, T + 1], F32))
            agp = Buf(nc.dram_tensor(_uq(tag + "agp"), [2 * D, 1], F32))
            o_k = Buf(nc.dram_tensor(_uq(tag + "ok"), [D, 1024], BF16))
            o_v = Buf(nc.dram_tensor(_uq(tag + "ov"), [1024, D], BF16))
            o_ln = Buf(nc.dram_tensor(_uq(tag + "oln"), [D, T], BF16))
            agk = Buf(nc.dram_tensor(_uq(tag + "agk"), [2 * D, 1024], BF16))
            agv = Buf(nc.dram_tensor(_uq(tag + "agv"), [T, D], BF16))
            kfb = Buf(nc.dram_tensor(_uq(tag + "kf"), [D, T], BF16))
            pcsrc = Buf(nc.dram_tensor(_uq(tag + "pcs"), [D, 1], F32))
            p.dma("sp", pcsrc, pcsrc[:, :], xst, xst[:, T - 1:T], "pcx",
                  allow_slow_non_contiguous=True)
            p.wait_all("sp", "pcx")
            p.barrier()
            _ag(p, pcsrc.t[:, :], agp.t[:, :], pcsrc, agp)
            p.wait_all("pool", "coll")
            p.barrier()
            with contextlib.ExitStack() as st:
                pcol = sbuf(p, st, "pcol", [128, 1], F32)
                pcol2 = sbuf(p, st, "pcol2", [128, 1], F32)
                for c in range(NCHUNK):
                    p.dma("sp", pcol, pcol.ap, agp,
                          agp[c * 128:(c + 1) * 128, 0:1], "pcl",
                          allow_slow_non_contiguous=True)
                    p.op("dve", lambda e: e.tensor_scalar(
                        pcol2.ap, pcol.ap, mcol[:, 1:2], None, OP.mult),
                        reads=[pcol, mcol], writes=[pcol2])
                    p.dma("sp", None, x5b[c * 128:(c + 1) * 128, 0:1], pcol2,
                          pcol2.ap, "pcs", allow_slow_non_contiguous=True)
                p.wait_all("sp", "pcs")
            p.dma("sp", x5b, x5b[:, 1:T + 1], xst, xst[:, :], "x5c")
            p.wait_all("sp", "x5c")
            p.barrier()
            build_s5a(ctx, pre=tag, io={"x5": x5b, "o_k": o_k, "o_v": o_v,
                                        "o_lnx": o_ln})
            p.barrier()
            _ag(p, o_k.t[:, :], agk.t[:, :], o_k, agk)
            _ag(p, o_v.t[:, :], agv.t[:, :], o_v, agv)
            p.wait_all("pool", "coll")
            p.barrier()
            p.dma("sp", kfb, kfb[0:D, 0:1024], agk, agk[0:D, :], "kfc")
            p.dma("sp", kfb, kfb[0:D, 1024:T], agk, agk[D:2 * D, :], "kfc",
                  nowaw=True)
            p.wait_all("sp", "kfc")
            p.barrier()
            build_s5b(ctx, pre=tag, io={"x_own": xst, "lnx": o_ln, "kf": kfb,
                                        "vf": agv, "x_new": xst})
            p.barrier()

        l0ffn("s1_")
        l0attn("s2_")
        l0ffn("s3_")
        l1ffn("s4_")
        l1attn("s5_")
        l1ffn("s6_", final=True)
        p.barrier()
        blk = ostack.enter_context(nc.Block())
        p.play(blk)
    return nc


def kernel(**inp):
    consts = host_consts_s2()
    cl0 = host_consts_l0()
    c5 = host_consts_s5()
    f32 = np.float32
    C = lambda a: np.ascontiguousarray(np.asarray(a, f32))
    col = lambda a: np.ascontiguousarray(np.asarray(a, f32).reshape(-1, 1))
    x = np.asarray(inp["L0_attn_x"], f32)
    c1q, c2q, c1k, c2k = rotary_tables(0)
    msk0 = l0_masks(0)
    ident = np.eye(128).astype(BF)

    keysx = np.zeros((512, 128), f32)
    keys = np.asarray(inp["L1_ffn_keys"], f32)
    for pl in range(2):
        for hh in range(4):
            bidx = 4 * pl + hh
            keysx[64 * bidx:64 * (bidx + 1)] = keys[hh, :, pl, :].T
    tabc = np.ascontiguousarray(np.concatenate(
        [np.asarray(inp["L1_ffn_down_embed"], f32),
         np.asarray(inp["L1_ffn_up_embed"], f32)], axis=1))
    wgT = prep_mlp_w(C(np.asarray(inp["L1_ffn_Wg"], f32).T), 8, 32)
    wuT = prep_mlp_w(C(np.asarray(inp["L1_ffn_Wu"], f32).T), 8, 32)
    wdT = prep_mlp_w(C(np.asarray(inp["L1_ffn_Wd"], f32).T), 32, 8)

    common = {}
    for tag in ("s1_", "s3_"):
        common[tag + "ln_c_stats"] = consts["c_stats"]
        common[tag + "ln_c_bsel"] = consts["c_bsel"]
        for l in range(2):
            pre = f"{tag}i{l}_"
            common[pre + "wqkvp"] = permute_wqkv(C(inp["L0_ffn_wqkv"][l]))
            common[pre + "qsc"] = qsc_cols(C(inp["L0_ffn_q_scale"][l]),
                                           C(inp["L0_ffn_k_scale"][l]))
            common[pre + "c1q"] = c1q
            common[pre + "c2q"] = c2q
            common[pre + "c1k"] = c1k
            common[pre + "c2k"] = c2k
            common[pre + "wo"] = C(inp["L0_ffn_wo"][l])
            common[pre + "lng"] = col(inp["L0_ffn_ln_g"][l])
            common[pre + "lnb"] = col(inp["L0_ffn_ln_b"][l])
            common[pre + "w1"] = C(inp["L0_ffn_w1"][l])
            common[pre + "b1"] = col(inp["L0_ffn_b1"][l])
            common[pre + "w2"] = C(inp["L0_ffn_w2"][l])
            common[pre + "b2"] = col(inp["L0_ffn_b2"][l])
            common[pre + "masks"] = msk0
            common[pre + "c_stats"] = consts["c_stats"]
            common[pre + "c_bsel"] = consts["c_bsel"]
            common[pre + "c_hsp"] = cl0["c_hsp"]
            common[pre + "c_bs4"] = cl0["c_bs4"]
            common[pre + "c_perm"] = cl0["c_perm"]
            common[pre + "c_hs64"] = cl0["c_hs64"]
    common["s2_c_stats"] = consts["c_stats"]
    common["s2_c_bsel"] = consts["c_bsel"]
    common["s2_c_hsel"] = consts["c_hsel"]
    common["s2_c_dmask"] = consts["c_dmask"]
    common["s2_wo"] = C(inp["L0_attn_W_o"])
    common["s2_bo"] = col(inp["L0_attn_b_o"])
    common["s2_c_zsel"] = consts["c_zsel"]
    for tag in ("s4_", "s6_"):
        common[tag + "wqT"] = C(np.asarray(inp["L1_ffn_Wq"], f32).T)
        common[tag + "keysx"] = keysx
        common[tag + "c_stats"] = consts["c_stats"]
        common[tag + "c_bsel"] = consts["c_bsel"]
        common[tag + "c_ident"] = ident
    common["s4_wgT"] = wgT
    common["s4_wuT"] = wuT
    common["s4_wdT"] = wdT
    common["s4_tabc"] = tabc
    common["s5_cwv"] = C(np.asarray(inp["L1_attn_conv_w"], f32)[:, 0, :])
    common["s5_cbv"] = col(inp["L1_attn_conv_b"])
    common["s5_kwT"] = C(np.asarray(inp["L1_attn_k_w"], f32).T)
    common["s5_kbv"] = col(inp["L1_attn_k_b"])
    common["s5_vwT"] = C(np.asarray(inp["L1_attn_v_w"], f32).T)
    common["s5_vbv"] = col(inp["L1_attn_v_b"])
    common["s5_c_stats"] = consts["c_stats"]
    common["s5_c_bsel"] = consts["c_bsel"]
    common["s5_qwT"] = C(np.asarray(inp["L1_attn_q_w"], f32).T)
    common["s5_qbv"] = col(inp["L1_attn_q_b"])
    common["s5_owT"] = C(np.asarray(inp["L1_attn_out_w"], f32).T)
    common["s5_obv"] = col(inp["L1_attn_out_b"])
    common["s5_c_hs16"] = c5["c_hs16"]
    common["s5_c_bs16"] = c5["c_bs16"]

    nc = _get("fused", build_fused)
    maps = []
    for c in range(8):
        b, hh = c // 2, c % 2
        m = dict(common)
        xfmb = fm(x[b])
        m["x_own"] = np.ascontiguousarray(xfmb[:, T * hh:T * (hh + 1)])
        ch = np.zeros((128, 2), f32)
        ch[:, 0] = 1.0 - hh
        ch[:, 1] = float(hh)
        m["c_half"] = ch
        cols = slice(512 * hh, 512 * (hh + 1))
        m["s2_wq"] = C(np.asarray(inp["L0_attn_W_q"], f32)[:, cols])
        m["s2_wk"] = C(np.asarray(inp["L0_attn_W_k"], f32)[:, cols])
        m["s2_wv"] = C(np.asarray(inp["L0_attn_W_v"], f32)[:, cols])
        m["s2_bq"] = col(np.asarray(inp["L0_attn_b_q"], f32)[cols])
        m["s2_bk"] = col(np.asarray(inp["L0_attn_b_k"], f32)[cols])
        m["s2_bv"] = col(np.asarray(inp["L0_attn_b_v"], f32)[cols])
        maps.append(m)
    res = run_bass_kernel_spmd(nc, maps, core_ids=list(range(8)))
    out = np.zeros((B, S, D), np.float32)
    for c in range(8):
        b, hh = c // 2, c % 2
        out[b, T * hh:T * (hh + 1)] = res.results[c]["x_final"].T
    return np.ascontiguousarray(out)
